# version 08
# baseline (speedup 1.0000x reference)
import math
import numpy as np
import concourse.bass as bass
import concourse.mybir as mybir
from concourse.bass_utils import run_bass_kernel_spmd
from contextlib import ExitStack

F32 = mybir.dt.float32
BF16 = mybir.dt.bfloat16
ALU = mybir.AluOpType
ACT = mybir.ActivationFunctionType
AXL = mybir.AxisListType

B = 8
T = 2048
D = 2048
NT = T // 128
ND = D // 128
HQ, HKV, DH = 16, 2, 128
FI = 8192
NFI = FI // 128
HALF = 512
NHALF = T // HALF
ICS = 512
NIC = FI // ICS
ALPHA = 1.0 / math.sqrt(DH)
LN_EPS = 1e-5

QA = (3.0, 2.0, 7.5)
QW = (1.0, 2.0, 6.0)
LADDER = [32, 16, 8, 4, 2, 1]
CLIP_LO, CLIP_HI = 2.0 ** -26, 2.0 ** 37
PI2_HI = float(np.float32(2.0 * math.pi))
PI2_LO = 2.0 * math.pi - PI2_HI


def _host_consts():
    ident = np.eye(128, dtype=np.float32)
    swap = np.zeros((128, 128), dtype=np.float32)
    for k in range(128):
        swap[k, (k + 64) % 128] = 1.0
    freqmod = (np.arange(128, dtype=np.float32) % 64)[None, :]
    pos = np.arange(T, dtype=np.float32)[None, :]
    sgn = np.concatenate([-np.ones(64, np.float32), np.ones(64, np.float32)])
    mask01 = np.tril(np.ones((128, 128), np.float32))
    return ident, swap, freqmod, pos, sgn, mask01


class Em:

    def __init__(self, nc, ctx):
        self.nc = nc
        self.sems = {}
        self.tot = {}
        self.nsem = {}
        self.step = {}
        self.cnt = {}
        self.eng = {}
        self._ctx = ctx
        self.wr = {}
        self.rd = {}
        self.mark = {}

    def stream(self, name, eng, step=1, nsem=1):
        for i in range(nsem):
            self.sems[(name, i)] = self._ctx.enter_context(
                self.nc.semaphore(f"S_{name}_{i}"))
            self.tot[(name, i)] = 0
        self.nsem[name] = nsem
        self.eng[name] = eng
        self.step[name] = step
        self.cnt[name] = 0

    def op(self, st, fn, reads=(), writes=(), acc=False):
        waits = {}

        def need(s, i, c):
            if waits.get((s, i), 0) < c:
                waits[(s, i)] = c

        def norm(k):
            return (k, 0, 1) if isinstance(k, str) else k

        reads = [norm(k) for k in reads]
        writes = [norm(k) for k in writes]
        for tag, lo, hi in reads:
            for (l2, h2, s, i, c) in self.wr.get(tag, ()):
                if l2 < hi and lo < h2:
                    need(s, i, c)
        for tag, lo, hi in writes:
            for (l2, h2, s, i, c) in self.wr.get(tag, ()):
                if l2 < hi and lo < h2 and not (acc and s == st):
                    need(s, i, c)
            for (l2, h2, s, i, c) in self.rd.get(tag, ()):
                if l2 < hi and lo < h2 and not (acc and s == st):
                    need(s, i, c)
        eng = self.eng[st]
        for (s, i) in sorted(waits):
            c = waits[(s, i)]
            if self.step[s] > 1:
                c = self.tot[(s, i)]
            if self.mark.get((st, s, i), -1) >= c:
                continue
            eng.wait_ge(self.sems[(s, i)], c)
            self.mark[(st, s, i)] = c
        inst = fn()
        idx = (self.cnt[st] // self.step[st]) % self.nsem[st]             if self.step[st] > 1 else 0
        inst.then_inc(self.sems[(st, idx)], self.step[st])
        self.cnt[st] += self.step[st]
        self.tot[(st, idx)] += self.step[st]
        c = self.tot[(st, idx)]
        for tag, lo, hi in writes:
            lst = [e for e in self.wr.get(tag, []) if not (e[0] < hi and lo < e[1])]
            lst.append((lo, hi, st, idx, c))
            self.wr[tag] = lst
            self.rd[tag] = [e for e in self.rd.get(tag, [])
                            if not (e[0] < hi and lo < e[1])]
        for tag, lo, hi in reads:
            lst = self.rd.setdefault(tag, [])
            lst[:] = [e for e in lst
                      if not (e[0] == lo and e[1] == hi and e[2] == st and e[3] == idx)]
            lst.append((lo, hi, st, idx, c))
        return inst

    def barrier(self):
        for w in self.nsem:
            eng = self.eng[w]
            for (s, i), t in self.tot.items():
                if t > 0 and self.mark.get((w, s, i), -1) < t:
                    eng.wait_ge(self.sems[(s, i)], t)
                    self.mark[(w, s, i)] = t
        self.wr.clear()
        self.rd.clear()

    def finish(self, st):
        eng = self.eng[st]
        for (s, i), t in self.tot.items():
            if t > 0 and self.mark.get((st, s, i), -1) < t:
                eng.wait_ge(self.sems[(s, i)], t)


def build_nc():
    nc = bass.Bass("TRN2", target_bir_lowering=False)

    x_h = nc.dram_tensor("x", [T, D], F32, kind="ExternalInput").ap()
    wqkvT_h = nc.dram_tensor("wqkvT", [D, (HQ + 2 * HKV) * DH], F32, kind="ExternalInput").ap()
    woT_h = nc.dram_tensor("woT", [D, D], F32, kind="ExternalInput").ap()
    wg_h = nc.dram_tensor("wg", [FI, D], F32, kind="ExternalInput").ap()
    wu_h = nc.dram_tensor("wu", [FI, D], F32, kind="ExternalInput").ap()
    wd_h = nc.dram_tensor("wd", [D, FI], F32, kind="ExternalInput").ap()
    bqkv_h = nc.dram_tensor("bqkv", [(HQ + 2 * HKV) * DH], F32, kind="ExternalInput").ap()
    ident_h = nc.dram_tensor("c_ident", [128, 128], F32, kind="ExternalInput").ap()
    swap_h = nc.dram_tensor("c_swap", [128, 128], F32, kind="ExternalInput").ap()
    freqmod_h = nc.dram_tensor("c_freqmod", [1, 128], F32, kind="ExternalInput").ap()
    pos_h = nc.dram_tensor("c_pos", [1, T], F32, kind="ExternalInput").ap()
    sgn_h = nc.dram_tensor("c_sgn", [128], F32, kind="ExternalInput").ap()
    mask_h = nc.dram_tensor("c_mask01", [128, 128], F32, kind="ExternalInput").ap()
    out_h = nc.dram_tensor("out", [T, D], F32, kind="ExternalOutput").ap()
    s_attn = nc.dram_tensor("s_attn", [T, D], BF16, kind="Internal").ap()
    s_x1 = nc.dram_tensor("s_x1", [T, D], BF16, kind="Internal").ap()
    s_x1qT = nc.dram_tensor("s_x1qT", [D, T], BF16, kind="Internal").ap()
    s_wgT = nc.dram_tensor("s_wgT", [NIC, ND, 128, ICS], BF16, kind="Internal").ap()
    s_wuT = nc.dram_tensor("s_wuT", [NIC, ND, 128, ICS], BF16, kind="Internal").ap()
    s_wdT = nc.dram_tensor("s_wdT", [ND, NFI, 128, 128], BF16, kind="Internal").ap()
    s_x2 = nc.dram_tensor("s_x2", [NHALF, HALF // 128, ND, 128, 128], BF16, kind="Internal").ap()

    with ExitStack() as ctx:
        def sbt(name, shape, dt=F32):
            return ctx.enter_context(nc.sbuf_tensor(name, shape, dt))

        Aa = sbt("arenaA", [128, 32768], BF16)
        Bb = sbt("arenaB", [128, 32768], BF16)
        Dd = sbt("arenaD", [128, 6784], F32)
        Ee = sbt("arenaE", [128, 3160], F32)
        Ff = sbt("arenaF", [128, 8192], BF16)
        Gg = sbt("arenaG", [128, 8192], BF16)
        Hh = sbt("arenaH", [128, 4352], BF16)
        CS2 = Gg[:, 4096:8192]

        ps_big = ctx.enter_context(nc.psum_tensor("ps_big", [128, 2048], F32))
        ps_a = ctx.enter_context(nc.psum_tensor("ps_a", [128, 512], F32))
        ps_b = ctx.enter_context(nc.psum_tensor("ps_b", [128, 512], F32))
        ps_tr = ctx.enter_context(nc.psum_tensor("ps_tr", [128, 1024], BF16))
        ps_c = ctx.enter_context(nc.psum_tensor("ps_c", [128, 512], F32))

        block = ctx.enter_context(nc.Block())
        em = Em(nc, ctx)
        em.stream("t", nc.tensor)
        em.stream("v", nc.vector)
        em.stream("s", nc.scalar)
        em.stream("g", nc.gpsimd)
        em.stream("dS", nc.sync, 16, nsem=12)
        em.stream("dG", nc.gpsimd, 16, nsem=12)
        op = em.op

        def Dx(lo, w):
            return Dd[:, lo:lo + w], ("D", lo, lo + w)

        def Fx(lo, w):
            return Ff[:, lo:lo + w], ("F", lo, lo + w)

        def Gx(lo, w):
            return Gg[:, lo:lo + w], ("G", lo, lo + w)

        def Hx(lo, w):
            return Hh[:, lo:lo + w], ("H", lo, lo + w)

        EC = {}
        _eo = [0]

        def edef(name, w):
            EC[name] = (_eo[0], w)
            _eo[0] += w
            assert _eo[0] <= 3160, name

        def E(name):
            o, w = EC[name]
            return Ee[:, o:o + w], ("E", o, o + w)

        def eN(name, nb, off=0):
            o, w = EC[name]
            assert off + nb <= w
            return Ee[:, o + off:o + off + nb], ("E", o + off, o + off + nb)

        for nm in ["amax", "rr", "pA", "pB", "pw", "csb", "cge", "ff",
                   "psm", "pinv", "zmul", "qmul"]:
            edef(nm, 256)
        for nm in ["cmag", "det_a", "det_b", "det_c", "kcn", "kcs", "keps",
                   "k2p30", "kc26", "kcm26", "kzA", "kqA", "kzW", "kqW",
                   "kclo", "kchi", "k2pinv", "kpihalf", "kpiclp", "kpicln",
                   "kan", "keps_ln", "bias_sm", "rowmax", "sum_a", "sum_d",
                   "sum_t", "inv_s", "mean", "msc", "m2sq", "var_s", "var1",
                   "var2", "var3", "sqv", "rstd", "sgn"]:
            edef(nm, 1)
        for w in LADDER:
            edef(f"kp2_{w}", 1)
            edef(f"kp2m1_{w}", 1)
        edef("bq", 20)

        idf, idf_k = Dx(0, 128)
        swf, swf_k = Dx(128, 128)
        maskf, maskf_k = Dx(256, 128)
        fqrow = Dd[0:1, 384:512]
        fqrow_k = ("D", 384, 512)
        invf = Dd[0:1, 512:640]
        invf_k = ("D", 512, 640)
        posrow = Dd[0:1, 640:640 + T]
        posrow_k = ("D", 640, 640 + T)
        identb, identb_k = Hx(0, 128)
        maskb, maskb_k = Hx(128, 128)

        op("dS", lambda: nc.sync.dma_start(out=idf, in_=ident_h[:]), writes=[idf_k])
        op("dS", lambda: nc.sync.dma_start(out=swf, in_=swap_h[:]), writes=[swf_k])
        op("dS", lambda: nc.sync.dma_start(out=maskf, in_=mask_h[:]), writes=[maskf_k])
        op("dS", lambda: nc.sync.dma_start(
            out=E("bq")[0], in_=bqkv_h.rearrange("(t p) -> p t", p=128),
            allow_slow_non_contiguous=True), writes=[E("bq")[1]])
        op("dS", lambda: nc.sync.dma_start(
            out=E("sgn")[0], in_=sgn_h.rearrange("(p o) -> p o", o=1),
            allow_slow_non_contiguous=True), writes=[E("sgn")[1]])
        op("dS", lambda: nc.sync.dma_start(out=fqrow, in_=freqmod_h[:]), writes=[fqrow_k])
        op("dS", lambda: nc.sync.dma_start(out=posrow, in_=pos_h[:]), writes=[posrow_k])

        op("v", lambda: nc.vector.tensor_copy(out=identb, in_=idf),
           reads=[idf_k], writes=[identb_k])
        op("v", lambda: nc.vector.tensor_copy(out=maskb, in_=maskf),
           reads=[maskf_k], writes=[maskb_k])

        def kmem(name, val):
            ap, key = E(name)
            op("v", lambda ap=ap, val=val: nc.vector.memset(ap, float(val)),
               writes=[key])

        kmem("kcn", 1.5 * 2.0 ** 23)
        kmem("kcs", 1.5 * 2.0 ** 52)
        kmem("keps", 2.0 ** -30)
        kmem("k2p30", 2.0 ** 30)
        kmem("kc26", 2.0 ** 26)
        kmem("kcm26", 2.0 ** -26)
        kmem("kzA", 2.0 ** (QA[0] + QA[1]))
        kmem("kqA", 2.0 ** -(QA[0] + QA[1]))
        kmem("kzW", 2.0 ** (QW[0] + QW[1]))
        kmem("kqW", 2.0 ** -(QW[0] + QW[1]))
        kmem("kclo", CLIP_LO)
        kmem("kchi", CLIP_HI)
        kmem("k2pinv", 1.0 / (2.0 * math.pi))
        kmem("kpihalf", math.pi / 2.0)
        kmem("kpiclp", 3.1415925)
        kmem("kpicln", -3.1415925)
        kmem("kan", -ALPHA)
        kmem("keps_ln", LN_EPS)
        for w in LADDER:
            kmem(f"kp2_{w}", 2.0 ** w)
            kmem(f"kp2m1_{w}", 2.0 ** w - 1.0)

        det_a, det_a_k = E("det_a")
        det_b, det_b_k = E("det_b")
        det_c, det_c_k = E("det_c")
        cmag, cmag_k = E("cmag")
        op("v", lambda: nc.vector.tensor_scalar(
            out=det_a, in0=Dd[:, 0:1], scalar1=float(-3.0e38), scalar2=None,
            op0=ALU.is_ge), reads=[("D", 0, 1)], writes=[det_a_k])
        op("v", lambda: nc.vector.tensor_tensor(
            out=det_b, in0=det_a, in1=E("keps")[0], op=ALU.mult),
            reads=[det_a_k, E("keps")[1]], writes=[det_b_k])
        op("v", lambda: nc.vector.tensor_tensor(
            out=det_c, in0=det_a, in1=det_b, op=ALU.add),
            reads=[det_a_k, det_b_k], writes=[det_c_k])
        op("v", lambda: nc.vector.tensor_tensor(
            out=det_b, in0=det_c, in1=det_a, op=ALU.subtract),
            reads=[det_c_k, det_a_k], writes=[det_b_k])
        op("v", lambda: nc.vector.tensor_tensor(
            out=det_c, in0=det_b, in1=E("k2p30")[0], op=ALU.mult),
            reads=[det_b_k, E("k2p30")[1]], writes=[det_c_k])
        op("v", lambda: nc.vector.tensor_tensor(
            out=det_a, in0=det_c, in1=E("kcs")[0], op=ALU.mult),
            reads=[det_c_k, E("kcs")[1]], writes=[det_a_k])
        op("v", lambda: nc.vector.tensor_scalar(
            out=det_b, in0=det_c, scalar1=-1.0, scalar2=1.0,
            op0=ALU.mult, op1=ALU.add), reads=[det_c_k], writes=[det_b_k])
        op("v", lambda: nc.vector.tensor_tensor(
            out=det_c, in0=det_b, in1=E("kcn")[0], op=ALU.mult),
            reads=[det_b_k, E("kcn")[1]], writes=[det_c_k])
        op("v", lambda: nc.vector.tensor_tensor(
            out=cmag, in0=det_a, in1=det_c, op=ALU.add),
            reads=[det_a_k, det_c_k], writes=[cmag_k])

        def rne(out_ap, in_ap, rk, wk, st="v"):
            op(st, lambda: nc.vector.tensor_scalar(
                out=out_ap, in0=in_ap, scalar1=cmag[:, 0:1], scalar2=cmag[:, 0:1],
                op0=ALU.add, op1=ALU.subtract),
                reads=list(rk) + [cmag_k], writes=list(wk))

        op("s", lambda: nc.scalar.activation(
            out=invf, in_=fqrow, func=ACT.Exp, bias=0.0,
            scale=-math.log(10000.0) / 64.0),
            reads=[fqrow_k], writes=[invf_k])
        angA, angA_k = Dx(2688, 1024)
        angB, angB_k = Dx(3712, 1024)
        angC, angC_k = Dx(4736, 1024)
        scrX, scrX_k = Dx(5760, 1024)
        for hh in range(2):
            base = hh * 1024
            for c in range(2):
                op("t", lambda c=c, base=base: nc.tensor.matmul(
                    ps_c[:, 0:512], invf,
                    posrow[:, base + c * 512:base + (c + 1) * 512],
                    start=True, stop=True),
                    reads=[invf_k, posrow_k], writes=[("ps_c", 0, 512)])
                op("v", lambda c=c: nc.vector.tensor_copy(
                    out=angA[:, c * 512:(c + 1) * 512], in_=ps_c[:, 0:512]),
                    reads=[("ps_c", 0, 512)], writes=[angA_k])
            for (use_shift, dst_lo) in [(True, 0), (False, T)]:
                if use_shift:
                    op("v", lambda: nc.vector.tensor_scalar(
                        out=angB, in0=angA, scalar1=E("kpihalf")[0][:, 0:1],
                        scalar2=None, op0=ALU.add),
                        reads=[angA_k, E("kpihalf")[1]], writes=[angB_k])
                else:
                    op("v", lambda: nc.vector.tensor_copy(out=angB, in_=angA),
                       reads=[angA_k], writes=[angB_k])
                for _ in range(2):
                    op("v", lambda: nc.vector.tensor_scalar(
                        out=angC, in0=angB, scalar1=E("k2pinv")[0][:, 0:1],
                        scalar2=None, op0=ALU.mult),
                        reads=[angB_k, E("k2pinv")[1]], writes=[angC_k])
                    rne(angC, angC, [angC_k], [angC_k])
                    op("v", lambda: nc.vector.scalar_tensor_tensor(
                        out=scrX, in0=angC, scalar=-PI2_HI, in1=angB,
                        op0=ALU.mult, op1=ALU.add),
                        reads=[angC_k, angB_k], writes=[scrX_k])
                    op("v", lambda: nc.vector.scalar_tensor_tensor(
                        out=angB, in0=angC, scalar=-PI2_LO, in1=scrX,
                        op0=ALU.mult, op1=ALU.add),
                        reads=[angC_k, scrX_k], writes=[angB_k])
                op("v", lambda: nc.vector.tensor_scalar(
                    out=angC, in0=angB, scalar1=E("kpiclp")[0][:, 0:1],
                    scalar2=E("kpicln")[0][:, 0:1], op0=ALU.min, op1=ALU.max),
                    reads=[angB_k, E("kpiclp")[1], E("kpicln")[1]], writes=[angC_k])
                if not use_shift:
                    op("s", lambda base=base, dst_lo=dst_lo: nc.scalar.activation(
                        out=scrX, in_=angC, func=ACT.Sin, bias=0.0, scale=1.0),
                        reads=[angC_k], writes=[scrX_k])
                    op("v", lambda base=base, dst_lo=dst_lo: nc.vector.tensor_scalar(
                        out=CS2[:, dst_lo + base:dst_lo + base + 1024], in0=scrX,
                        scalar1=E("sgn")[0][:, 0:1], scalar2=None, op0=ALU.mult),
                        reads=[scrX_k, E("sgn")[1]],
                        writes=[("G", 4096 + dst_lo + base, 4096 + dst_lo + base + 1024)])
                else:
                    op("s", lambda base=base, dst_lo=dst_lo: nc.scalar.activation(
                        out=CS2[:, dst_lo + base:dst_lo + base + 1024], in_=angC,
                        func=ACT.Sin, bias=0.0, scale=1.0),
                        reads=[angC_k],
                        writes=[("G", 4096 + dst_lo + base, 4096 + dst_lo + base + 1024)])
        C2 = CS2[:, 0:T]
        C2_k = ("G", 4096, 4096 + T)
        S2 = CS2[:, T:2 * T]
        S2_k = ("G", 4096 + T, 4096 + 2 * T)

        def q_amax(src_ap, src_keys, nb, off):
            amax, amax_k = eN("amax", nb, off)
            op("v", lambda: nc.vector.tensor_reduce(
                out=amax, in_=src_ap.rearrange("p (b k) -> p b k", k=32),
                axis=AXL.X, op=ALU.max, apply_absolute_value=True),
                reads=list(src_keys), writes=[amax_k])

        def q_ladder(nb, params, off=0, lst="v"):
            mfrac, emax, mn = params
            le = nc.gpsimd if lst == "g" else nc.vector
            kz = "kzA" if mfrac == QA[0] else "kzW"
            kq = "kqA" if mfrac == QA[0] else "kqW"
            amax, amax_k = eN("amax", nb, off)
            rr, rr_k = eN("rr", nb, off)
            pw, pw_k = eN("pw", nb, off)
            csb, csb_k = eN("csb", nb, off)
            cge, cge_k = eN("cge", nb, off)
            ffb, ffb_k = eN("ff", nb, off)
            psm, psm_k = eN("psm", nb, off)
            pinv, pinv_k = eN("pinv", nb, off)
            zmul, zmul_k = eN("zmul", nb, off)
            qmul, qmul_k = eN("qmul", nb, off)
            pa, pa_k = eN("pA", nb, off)
            pb, pb_k = eN("pB", nb, off)
            op(lst, lambda: le.tensor_scalar(
                out=csb, in0=amax, scalar1=E("kchi")[0][:, 0:1],
                scalar2=E("kclo")[0][:, 0:1], op0=ALU.min, op1=ALU.max),
                reads=[amax_k, E("kchi")[1], E("kclo")[1]], writes=[csb_k])
            op(lst, lambda: le.tensor_scalar(
                out=rr, in0=csb, scalar1=E("kc26")[0][:, 0:1], scalar2=None,
                op0=ALU.mult), reads=[csb_k, E("kc26")[1]], writes=[rr_k])
            cur, cur_k, nxt, nxt_k = pa, pa_k, pb, pb_k
            first = True
            for w in LADDER:
                if first:
                    op(lst, lambda w=w: le.tensor_scalar(
                        out=cge, in0=rr, scalar1=E(f"kp2_{w}")[0][:, 0:1],
                        scalar2=None, op0=ALU.is_ge),
                        reads=[rr_k, E(f"kp2_{w}")[1]], writes=[cge_k])
                else:
                    op(lst, lambda cur=cur, w=w: le.tensor_scalar(
                        out=pw, in0=cur, scalar1=E(f"kp2_{w}")[0][:, 0:1],
                        scalar2=None, op0=ALU.mult),
                        reads=[cur_k, E(f"kp2_{w}")[1]], writes=[pw_k])
                    op(lst, lambda: le.tensor_tensor(
                        out=cge, in0=rr, in1=pw, op=ALU.is_ge),
                        reads=[rr_k, pw_k], writes=[cge_k])
                if w >= 32:
                    op(lst, lambda w=w: le.tensor_scalar(
                        out=csb, in0=cge, scalar1=E(f"kp2_{w}")[0][:, 0:1],
                        scalar2=None, op0=ALU.mult),
                        reads=[cge_k, E(f"kp2_{w}")[1]], writes=[csb_k])
                    op(lst, lambda: le.tensor_scalar(
                        out=pw, in0=cge, scalar1=-1.0, scalar2=1.0,
                        op0=ALU.mult, op1=ALU.add), reads=[cge_k], writes=[pw_k])
                    op(lst, lambda: le.tensor_tensor(
                        out=ffb, in0=csb, in1=pw, op=ALU.add),
                        reads=[csb_k, pw_k], writes=[ffb_k])
                else:
                    op(lst, lambda w=w: le.tensor_scalar(
                        out=ffb, in0=cge, scalar1=E(f"kp2m1_{w}")[0][:, 0:1],
                        scalar2=1.0, op0=ALU.mult, op1=ALU.add),
                        reads=[cge_k, E(f"kp2m1_{w}")[1]], writes=[ffb_k])
                if first:
                    op(lst, lambda cur=cur: le.tensor_copy(
                        out=cur, in_=ffb), reads=[ffb_k], writes=[cur_k])
                    first = False
                else:
                    op(lst, lambda cur=cur, nxt=nxt: le.tensor_tensor(
                        out=nxt, in0=cur, in1=ffb, op=ALU.mult),
                        reads=[cur_k, ffb_k], writes=[nxt_k])
                    cur, cur_k, nxt, nxt_k = nxt, nxt_k, cur, cur_k
            op(lst, lambda cur=cur: le.tensor_scalar(
                out=psm, in0=cur, scalar1=E("kcm26")[0][:, 0:1], scalar2=None,
                op0=ALU.mult), reads=[cur_k, E("kcm26")[1]], writes=[psm_k])
            op("v", lambda: nc.vector.reciprocal(out=pinv, in_=psm),
               reads=[psm_k], writes=[pinv_k])
            op("v", lambda: nc.vector.tensor_scalar(
                out=zmul, in0=pinv, scalar1=E(kz)[0][:, 0:1], scalar2=None,
                op0=ALU.mult), reads=[pinv_k, E(kz)[1]], writes=[zmul_k])
            op(lst, lambda: le.tensor_scalar(
                out=qmul, in0=psm, scalar1=E(kq)[0][:, 0:1], scalar2=None,
                op0=ALU.mult), reads=[psm_k, E(kq)[1]], writes=[qmul_k])

        def q_scales(src_ap, src_keys, nb, params):
            mfrac, emax, mn = params
            q_amax(src_ap, src_keys, nb, 0)
            q_ladder(nb, params)

        def q_elem_steps(src_ap, src_keys, dst_ap, dst_key, ncol, params, scr,
                         nb_off=0):
            steps = []
            add = steps.append
            mfrac, emax, mn = params
            nb = ncol // 32
            zz, zz_k = scr["zz"]
            yy, yy_k = scr["yy"]
            c1, c1_k = scr["c1"]
            c2, c2_k = scr["c2"]
            zmul, zmul_k = eN("zmul", nb, nb_off)
            qmul, qmul_k = eN("qmul", nb, nb_off)
            th1 = float(2.0 ** (mfrac + 1))
            th2 = float(2.0 ** (mfrac + 2))
            mnz = float(mn * 2.0 ** mfrac)
            add(lambda: op("v", lambda: nc.vector.tensor_tensor(
                out=zz.rearrange("p (b k) -> p b k", k=32),
                in0=src_ap.rearrange("p (b k) -> p b k", k=32),
                in1=zmul.unsqueeze(2).broadcast_to([128, nb, 32]), op=ALU.mult),
                reads=list(src_keys) + [zmul_k], writes=[zz_k]))
            add(lambda: op("s", lambda: nc.scalar.activation(
                out=yy, in_=zz, func=ACT.Abs, bias=0.0, scale=1.0),
                reads=[zz_k], writes=[yy_k]))
            add(lambda: op("v", lambda: nc.vector.tensor_scalar(
                out=c1, in0=yy, scalar1=th1, scalar2=None, op0=ALU.is_ge),
                reads=[yy_k], writes=[c1_k]))
            add(lambda: op("v", lambda: nc.vector.tensor_scalar(
                out=c2, in0=yy, scalar1=th2, scalar2=None, op0=ALU.is_ge),
                reads=[yy_k], writes=[c2_k]))
            add(lambda: op("v", lambda: nc.vector.tensor_scalar(
                out=c2, in0=c2, scalar1=cmag[:, 0:1], scalar2=cmag[:, 0:1],
                op0=ALU.mult, op1=ALU.add),
                reads=[c2_k, cmag_k], writes=[c2_k]))
            add(lambda: op("v", lambda: nc.vector.scalar_tensor_tensor(
                out=c1, in0=c1, scalar=1.0, in1=c2, op0=ALU.add, op1=ALU.mult),
                reads=[c1_k, c2_k], writes=[c1_k]))
            add(lambda: op("v", lambda: nc.vector.tensor_tensor(
                out=yy, in0=zz, in1=c1, op=ALU.add),
                reads=[zz_k, c1_k], writes=[yy_k]))
            add(lambda: op("v", lambda: nc.vector.tensor_tensor(
                out=zz, in0=yy, in1=c1, op=ALU.subtract),
                reads=[yy_k, c1_k], writes=[zz_k]))
            add(lambda: op("v", lambda: nc.vector.tensor_scalar(
                out=zz, in0=zz, scalar1=mnz, scalar2=-mnz, op0=ALU.min,
                op1=ALU.max), reads=[zz_k], writes=[zz_k]))
            add(lambda: op("v", lambda: nc.vector.tensor_tensor(
                out=dst_ap.rearrange("p (b k) -> p b k", k=32),
                in0=zz.rearrange("p (b k) -> p b k", k=32),
                in1=qmul.unsqueeze(2).broadcast_to([128, nb, 32]), op=ALU.mult),
                reads=[zz_k, qmul_k], writes=[dst_key]))
            return steps

        def zipper(chains):
            i = 0
            while any(chains):
                for ch in chains:
                    if i < len(ch):
                        ch[i]()
                i += 1
                if all(i >= len(ch) for ch in chains):
                    break

        def q_elem(src_ap, src_keys, dst_ap, dst_key, ncol, params, scr, nb_off=0):
            for st_ in q_elem_steps(src_ap, src_keys, dst_ap, dst_key, ncol,
                                    params, scr, nb_off):
                st_()

        def transpose8(src_ap_fn, src_keys, dst_ap, dst_key, n, ev="v"):
            for j in range(n):
                op("t", lambda j=j: nc.tensor.transpose(
                    ps_tr[:, j * 128:(j + 1) * 128], src_ap_fn(j), identb),
                    reads=list(src_keys) + [identb_k],
                    writes=[("ps_tr", j * 128, (j + 1) * 128)])
            if ev == "v":
                op("v", lambda: nc.vector.tensor_copy(
                    out=dst_ap, in_=ps_tr[:, 0:n * 128]),
                    reads=[("ps_tr", 0, n * 128)], writes=[dst_key])
            else:
                op("s", lambda: nc.scalar.copy(
                    out=dst_ap, in_=ps_tr[:, 0:n * 128]),
                    reads=[("ps_tr", 0, n * 128)], writes=[dst_key])

        em.barrier()
        xrow, xrow_k = Dx(2688, D)
        xrow_bf, xrow_bf_k = Hx(256, D)
        for tt in range(NT):
            op("dG", lambda tt=tt: nc.gpsimd.dma_start(
                out=xrow, in_=x_h[tt * 128:(tt + 1) * 128, :]), writes=[xrow_k])
            op("v", lambda: nc.vector.tensor_copy(out=xrow_bf, in_=xrow),
               reads=[xrow_k], writes=[xrow_bf_k])
            for g in range(2):
                dst = Aa[:].rearrange("p (dt c) -> p dt c", c=T)[
                    :, g * 8:(g + 1) * 8, tt * 128:(tt + 1) * 128]
                transpose8(
                    lambda j, g=g: xrow_bf[:, (g * 8 + j) * 128:(g * 8 + j + 1) * 128],
                    [xrow_bf_k], dst, ("A", g * 8 * T, (g * 8 + 8) * T), 8)

        NOUT = HQ + 2 * HKV
        wrawA, wrawA_k = Dx(640, D)
        wrawB, wrawB_k = Dx(2688, D)
        qraw, qraw_k = Dx(4736, D)
        wbfA, wbfA_k = Fx(0, D)
        wbfB, wbfB_k = Fx(2048, D)
        t1b, t1b_k = Fx(4096, D)
        vtmp, vtmp_k = Hx(256, 512)
        for ot in range(NOUT):
            wr, wr_k, wb, wb_k = ((wrawA, wrawA_k, wbfA, wbfA_k) if ot % 2 == 0
                                  else (wrawB, wrawB_k, wbfB, wbfB_k))
            src = wqkvT_h[:, ot * 128:(ot + 1) * 128].rearrange(
                "(a p) o -> p a o", p=128)
            op("dS", lambda src=src, wr=wr: nc.sync.dma_start(
                out=wr.rearrange("p (a o) -> p a o", o=128), in_=src),
                writes=[wr_k])
            op("v", lambda wb=wb, wr=wr: nc.vector.tensor_copy(out=wb, in_=wr),
               reads=[wr_k], writes=[wb_k])
            for ch in range(4):
                for dt in range(ND):
                    op("t", lambda ch=ch, dt=dt, wb=wb: nc.tensor.matmul(
                        ps_big[:, ch * 512:(ch + 1) * 512],
                        wb[:, dt * 128:(dt + 1) * 128],
                        Aa[:, dt * T + ch * 512:dt * T + (ch + 1) * 512],
                        start=(dt == 0), stop=(dt == ND - 1)),
                        reads=[wb_k, ("A", dt * T + ch * 512, dt * T + (ch + 1) * 512)],
                        writes=[("ps_big", ch * 512, (ch + 1) * 512)], acc=True)
            if ot < HQ + HKV:
                for ch in range(4):
                    qck = ("D", 4736 + ch * 512, 4736 + (ch + 1) * 512)
                    op("s", lambda ch=ch, ot=ot: nc.scalar.activation(
                        out=qraw[:, ch * 512:(ch + 1) * 512],
                        in_=ps_big[:, ch * 512:(ch + 1) * 512],
                        func=ACT.Identity, bias=E("bq")[0][:, ot:ot + 1], scale=1.0),
                        reads=[("ps_big", ch * 512, (ch + 1) * 512), E("bq")[1]],
                        writes=[qck])
                for ch in range(4):
                    qck = ("D", 4736 + ch * 512, 4736 + (ch + 1) * 512)
                    op("t", lambda ch=ch: nc.tensor.matmul(
                        ps_c[:, 0:512], swf, qraw[:, ch * 512:(ch + 1) * 512],
                        start=True, stop=True),
                        reads=[swf_k, qck], writes=[("ps_c", 0, 512)])
                    op("v", lambda ch=ch: nc.vector.tensor_tensor(
                        out=t1b[:, ch * 512:(ch + 1) * 512],
                        in0=qraw[:, ch * 512:(ch + 1) * 512],
                        in1=C2[:, ch * 512:(ch + 1) * 512], op=ALU.mult),
                        reads=[qck, C2_k],
                        writes=[("F", 4096 + ch * 512, 4096 + (ch + 1) * 512)])
                    op("v", lambda ch=ch: nc.vector.tensor_tensor(
                        out=qraw[:, ch * 512:(ch + 1) * 512], in0=ps_c[:, 0:512],
                        in1=S2[:, ch * 512:(ch + 1) * 512], op=ALU.mult),
                        reads=[("ps_c", 0, 512), S2_k], writes=[qck])
                    if ot < HQ:
                        dst = Bb[:, ot * T + ch * 512:ot * T + (ch + 1) * 512]
                        dk = ("B", ot * T + ch * 512, ot * T + (ch + 1) * 512)
                    else:
                        kvo = (ot - HQ) * T
                        dst = Gg[:, kvo + ch * 512:kvo + (ch + 1) * 512]
                        dk = ("G", kvo + ch * 512, kvo + (ch + 1) * 512)
                    op("v", lambda dst=dst, ch=ch: nc.vector.tensor_tensor(
                        out=dst, in0=t1b[:, ch * 512:(ch + 1) * 512],
                        in1=qraw[:, ch * 512:(ch + 1) * 512], op=ALU.add),
                        reads=[("F", 4096 + ch * 512, 4096 + (ch + 1) * 512), qck],
                        writes=[dk])
            else:
                kv = ot - HQ - HKV
                for ch in range(4):
                    op("s", lambda ch=ch, ot=ot: nc.scalar.activation(
                        out=vtmp, in_=ps_big[:, ch * 512:(ch + 1) * 512],
                        func=ACT.Identity, bias=E("bq")[0][:, ot:ot + 1], scale=1.0),
                        reads=[("ps_big", ch * 512, (ch + 1) * 512), E("bq")[1]],
                        writes=[vtmp_k])
                    base = 4096 + (kv * NT + ch * 4) * 128
                    transpose8(
                        lambda j: vtmp[:, j * 128:(j + 1) * 128],
                        [vtmp_k], Gg[:, base:base + 512],
                        ("G", base, base + 512), 4)

        rawI = [Dx(2688, 2048), Dx(4736, 2048)]
        scrI2 = [{
            "yy": Dx(640, 1024),
            "c1": Fx(4096, 1024), "c2": Fx(5120, 1024),
        }, {
            "yy": Dx(1664, 1024),
            "c1": Fx(6144, 1024), "c2": Fx(7168, 1024),
        }]
        wqnI, wqnI_k = Hx(2304, 2048)
        stageI, stageI_k = Hx(896, 1024)
        wq_parity = [0]

        def wq_pair_gen(kind, ic_or_et, pair):
            w_h = [wg_h, wu_h, wd_h][kind]
            eoff = 128 * wq_parity[0]
            wq_parity[0] ^= 1
            for k in range(2):
                wr, wr_k = rawI[k]
                if kind < 2:
                    ck = pair * 2 + k
                    r0 = ic_or_et * ICS + ck * 128
                    op("dS", lambda w_h=w_h, r0=r0, wr=wr: nc.sync.dma_start(
                        out=wr, in_=w_h[r0:r0 + 128, :]), writes=[wr_k])
                else:
                    cp = pair * 2 + k
                    op("dS", lambda w_h=w_h, et=ic_or_et, cp=cp, wr=wr:
                       nc.sync.dma_start(
                           out=wr, in_=w_h[et * 128:(et + 1) * 128,
                                           cp * 2048:(cp + 1) * 2048]),
                       writes=[wr_k])
                q_amax(rawI[k][0], [rawI[k][1]], 64, eoff + k * 64)
            yield
            q_ladder(128, QW, off=eoff)
            yield
            for k in range(2):
                wr, wr_k = rawI[k]
                chains = []
                for s in range(2):
                    lo = s * 1024
                    raw_sl = (wr[:, lo:lo + 1024],
                              (wr_k[0], wr_k[1] + lo, wr_k[1] + lo + 1024))
                    scr = dict(scrI2[s])
                    scr["zz"] = raw_sl
                    chains.append(q_elem_steps(
                        raw_sl[0], [raw_sl[1]],
                        wqnI[:, lo:lo + 1024],
                        (wqnI_k[0], wqnI_k[1] + lo, wqnI_k[1] + lo + 1024),
                        1024, QW, scr,
                        nb_off=eoff + k * 64 + s * 32))
                zipper(chains)
                yield
                for g in range(2):
                    transpose8(
                        lambda j, g=g: wqnI[:, (g * 8 + j) * 128:(g * 8 + j + 1) * 128],
                        [(wqnI_k[0], wqnI_k[1] + g * 1024, wqnI_k[1] + (g + 1) * 1024)],
                        stageI, stageI_k, 8, ev="s")
                    if kind < 2:
                        ck = pair * 2 + k
                        s_wT = s_wgT if kind == 0 else s_wuT
                        dst = s_wT[ic_or_et, g * 8:(g + 1) * 8, :,
                                   ck * 128:(ck + 1) * 128].rearrange(
                                       "dt p c -> p dt c")
                        hk = (f"hw{kind}", ic_or_et, ic_or_et + 1)
                    else:
                        cp = pair * 2 + k
                        dst = s_wdT[ic_or_et, cp * 16 + g * 8:cp * 16 + (g + 1) * 8,
                                    :, :].rearrange("it p c -> p it c")
                        hk = ("hwd", ic_or_et, ic_or_et + 1)
                    op("dS", lambda dst=dst: nc.sync.dma_start(
                        out=dst, in_=stageI.rearrange("p (n c) -> p n c", c=128)),
                        reads=[stageI_k], writes=[hk])
                    yield

        wq_tasks = []
        for kind in range(3):
            for i in range(16):
                for pair in range(2):
                    wq_tasks.append((kind, i, pair))
        wq_tasks.sort(key=lambda t: (t[1], t[2], t[0]))

        def _wq_units():
            for kind, i, pair in wq_tasks:
                for _ in wq_pair_gen(kind, i, pair):
                    yield
        wq_gen = _wq_units()
        wq_done = [False]

        def emit_wq_task(n=1):
            for _ in range(n):
                try:
                    next(wq_gen)
                except StopIteration:
                    wq_done[0] = True
                    return

        em.barrier()
        Pbuf, Pbuf_k = Fx(0, T)
        Pd, Pd_k = Hx(768, 128)
        Pn, Pn_k = Fx(2048, T)
        for h in range(HQ):
            kv = h // (HQ // HKV)
            qkb = ("B", h * T + 0, h * T + T)
            for qt in range(NT):
                emit_wq_task(3)
                L = (qt + 1) * 128
                nch = (L + 511) // 512
                for c in range(nch):
                    c0, c1_ = c * 512, min((c + 1) * 512, L)
                    op("t", lambda h=h, qt=qt, kv=kv, c0=c0, c1_=c1_:
                       nc.tensor.matmul(
                           ps_big[:, c0:c1_],
                           Bb[:, h * T + qt * 128:h * T + (qt + 1) * 128],
                           Gg[:, kv * T + c0:kv * T + c1_], start=True, stop=True),
                       reads=[qkb, ("G", kv * T + c0, kv * T + c1_)],
                       writes=[("ps_big", c0, c1_)])
                op("v", lambda L=L: nc.vector.tensor_reduce(
                    out=E("rowmax")[0], in_=ps_big[:, 0:L], axis=AXL.X, op=ALU.max),
                    reads=[("ps_big", 0, L)], writes=[E("rowmax")[1]])
                op("v", lambda: nc.vector.tensor_scalar(
                    out=E("bias_sm")[0], in0=E("rowmax")[0],
                    scalar1=E("kan")[0][:, 0:1], scalar2=None, op0=ALU.mult),
                    reads=[E("rowmax")[1], E("kan")[1]], writes=[E("bias_sm")[1]])
                if qt > 0:
                    op("s", lambda qt=qt: nc.scalar.activation(
                        out=Pbuf[:, 0:qt * 128], in_=ps_big[:, 0:qt * 128],
                        func=ACT.Exp, bias=E("bias_sm")[0][:, 0:1], scale=ALPHA,
                        accum_out=E("sum_a")[0]),
                        reads=[("ps_big", 0, qt * 128), E("bias_sm")[1]],
                        writes=[("F", 0, qt * 128), E("sum_a")[1]])
                op("s", lambda qt=qt, L=L: nc.scalar.activation(
                    out=Pd, in_=ps_big[:, qt * 128:L], func=ACT.Exp,
                    bias=E("bias_sm")[0][:, 0:1], scale=ALPHA),
                    reads=[("ps_big", qt * 128, L), E("bias_sm")[1]], writes=[Pd_k])
                op("v", lambda qt=qt, L=L: nc.vector.tensor_tensor(
                    out=Pbuf[:, qt * 128:L], in0=Pd, in1=maskb, op=ALU.mult),
                    reads=[Pd_k, maskb_k], writes=[("F", qt * 128, L)])
                op("v", lambda qt=qt, L=L: nc.vector.tensor_reduce(
                    out=E("sum_d")[0], in_=Pbuf[:, qt * 128:L], axis=AXL.X,
                    op=ALU.add),
                    reads=[("F", qt * 128, L)], writes=[E("sum_d")[1]])
                if qt > 0:
                    op("v", lambda: nc.vector.tensor_tensor(
                        out=E("sum_t")[0], in0=E("sum_a")[0], in1=E("sum_d")[0],
                        op=ALU.add),
                        reads=[E("sum_a")[1], E("sum_d")[1]], writes=[E("sum_t")[1]])
                else:
                    op("v", lambda: nc.vector.tensor_copy(
                        out=E("sum_t")[0], in_=E("sum_d")[0]),
                        reads=[E("sum_d")[1]], writes=[E("sum_t")[1]])
                op("v", lambda: nc.vector.reciprocal(
                    out=E("inv_s")[0], in_=E("sum_t")[0]),
                    reads=[E("sum_t")[1]], writes=[E("inv_s")[1]])
                op("v", lambda L=L: nc.vector.tensor_scalar(
                    out=Pn[:, 0:L], in0=Pbuf[:, 0:L],
                    scalar1=E("inv_s")[0][:, 0:1], scalar2=None, op0=ALU.mult),
                    reads=[("F", 0, L), E("inv_s")[1]],
                    writes=[("F", 2048, 2048 + L)])
                nkt = qt + 1
                for g in range((nkt + 7) // 8):
                    jn = min(8, nkt - g * 8)
                    transpose8(
                        lambda j, g=g: Pn[:, (g * 8 + j) * 128:(g * 8 + j + 1) * 128],
                        [("F", 2048 + g * 1024, 2048 + g * 1024 + jn * 128)],
                        Pbuf[:, g * 1024:g * 1024 + jn * 128],
                        ("F", g * 1024, g * 1024 + jn * 128), jn, ev="s")
                for kt in range(nkt):
                    vb = 4096 + (kv * NT + kt) * 128
                    op("t", lambda vb=vb, kt=kt, qt=qt: nc.tensor.matmul(
                        ps_a[:, 0:128], Gg[:, vb:vb + 128],
                        Pbuf[:, kt * 128:(kt + 1) * 128],
                        start=(kt == 0), stop=(kt == qt)),
                        reads=[("G", vb, vb + 128),
                               ("F", kt * 128, (kt + 1) * 128)],
                        writes=[("ps_a", 0, 128)], acc=True)
                op("s", lambda h=h, qt=qt: nc.scalar.copy(
                    out=Aa[:, h * T + qt * 128:h * T + (qt + 1) * 128],
                    in_=ps_a[:, 0:128]),
                    reads=[("ps_a", 0, 128)],
                    writes=[("A", h * T + qt * 128, h * T + (qt + 1) * 128)])

        em.barrier()
        wtmp, wtmp_k = Dx(0, 512)
        x1row, x1row_k = Hx(256 + D, D)
        x1q, x1q_k = Gx(0, D)
        xqs, xqs_k = Gx(D, 2048)
        rrow, rrow_k = Dx(640, D)
        srow, srow_k = Dx(2688, D)
        scrP5 = {
            "zz": Dx(4736, 2048), "yy": Dx(2688, 2048),
            "c1": Fx(0, 2048), "c2": Fx(2048, 2048),
        }

        def ln_stats(rr_ap, rr_key):
            op("v", lambda: nc.vector.tensor_reduce(
                out=E("mean")[0], in_=rr_ap, axis=AXL.X, op=ALU.add),
                reads=[rr_key], writes=[E("mean")[1]])
            op("v", lambda: nc.vector.tensor_scalar(
                out=E("msc")[0], in0=E("mean")[0], scalar1=1.0 / D, scalar2=None,
                op0=ALU.mult), reads=[E("mean")[1]], writes=[E("msc")[1]])
            op("s", lambda: nc.scalar.activation(
                out=srow, in_=rr_ap, func=ACT.Square, bias=0.0, scale=1.0,
                accum_out=E("var_s")[0]),
                reads=[rr_key], writes=[srow_k, E("var_s")[1]])
            op("v", lambda: nc.vector.tensor_tensor(
                out=E("m2sq")[0], in0=E("msc")[0], in1=E("msc")[0], op=ALU.mult),
                reads=[E("msc")[1]], writes=[E("m2sq")[1]])
            op("v", lambda: nc.vector.tensor_scalar(
                out=E("var1")[0], in0=E("var_s")[0], scalar1=1.0 / D,
                scalar2=None, op0=ALU.mult),
                reads=[E("var_s")[1]], writes=[E("var1")[1]])
            op("v", lambda: nc.vector.tensor_tensor(
                out=E("var2")[0], in0=E("var1")[0], in1=E("m2sq")[0],
                op=ALU.subtract),
                reads=[E("var1")[1], E("m2sq")[1]], writes=[E("var2")[1]])
            op("v", lambda: nc.vector.tensor_scalar(
                out=E("var3")[0], in0=E("var2")[0],
                scalar1=E("keps_ln")[0][:, 0:1], scalar2=None, op0=ALU.add),
                reads=[E("var2")[1], E("keps_ln")[1]], writes=[E("var3")[1]])
            op("s", lambda: nc.scalar.activation(
                out=E("sqv")[0], in_=E("var3")[0], func=ACT.Sqrt, bias=0.0,
                scale=1.0), reads=[E("var3")[1]], writes=[E("sqv")[1]])
            op("v", lambda: nc.vector.reciprocal(
                out=E("rstd")[0], in_=E("sqv")[0]),
                reads=[E("sqv")[1]], writes=[E("rstd")[1]])

        for eq in range(4):
            for ot in range(16):
                op("dS", lambda ot=ot, eq=eq: nc.sync.dma_start(
                    out=wtmp,
                    in_=woT_h[ot * 128:(ot + 1) * 128, eq * 512:(eq + 1) * 512]),
                    writes=[wtmp_k])
                wqlo = eq * 8192
                op("v", lambda ot=ot, wqlo=wqlo: nc.vector.tensor_copy(
                    out=Bb[:, wqlo + ot * 512:wqlo + (ot + 1) * 512], in_=wtmp),
                    reads=[wtmp_k],
                    writes=[("B", wqlo + ot * 512, wqlo + (ot + 1) * 512)])
        for qt in range(NT):
            emit_wq_task(2)
            op("dG", lambda qt=qt: nc.gpsimd.dma_start(
                out=srow, in_=x_h[qt * 128:(qt + 1) * 128, :]), writes=[srow_k])
            for eq in range(4):
                wqlo = eq * 8192
                pcol = eq * 512
                for ot in range(16):
                    op("t", lambda qt=qt, ot=ot, wqlo=wqlo, pcol=pcol:
                       nc.tensor.matmul(
                           ps_big[:, pcol:pcol + 512],
                           Aa[:, ot * T + qt * 128:ot * T + (qt + 1) * 128],
                           Bb[:, wqlo + ot * 512:wqlo + (ot + 1) * 512],
                           start=(ot == 0), stop=(ot == 15)),
                       reads=[("A", ot * T + qt * 128, ot * T + (qt + 1) * 128),
                              ("B", wqlo + ot * 512, wqlo + (ot + 1) * 512)],
                       writes=[("ps_big", pcol, pcol + 512)], acc=True)
            op("v", lambda: nc.vector.tensor_tensor(
                out=rrow, in0=srow, in1=ps_big[:, 0:2048], op=ALU.add),
                reads=[srow_k, ("ps_big", 0, 2048)], writes=[rrow_k])
            ln_stats(rrow, rrow_k)
            op("v", lambda: nc.vector.tensor_scalar(
                out=x1row, in0=rrow, scalar1=E("msc")[0][:, 0:1],
                scalar2=E("rstd")[0][:, 0:1], op0=ALU.subtract, op1=ALU.mult),
                reads=[rrow_k, E("msc")[1], E("rstd")[1]], writes=[x1row_k])
            op("dG", lambda qt=qt: nc.gpsimd.dma_start(
                out=s_x1[qt * 128:(qt + 1) * 128, :], in_=x1row),
                reads=[x1row_k], writes=[("hx1", qt, qt + 1)])
            q_scales(x1row, [x1row_k], 64, QA)
            q_elem(x1row, [x1row_k], x1q, x1q_k, D, QA, scrP5)
            for g in range(2):
                transpose8(
                    lambda j, g=g: x1q[:, (g * 8 + j) * 128:(g * 8 + j + 1) * 128],
                    [x1q_k], xqs[:, g * 1024:(g + 1) * 1024],
                    ("G", D + g * 1024, D + (g + 1) * 1024), 8)
            dst = s_x1qT.rearrange("(dt p) t -> p dt t", p=128)[
                :, :, qt * 128:(qt + 1) * 128]
            op("dG", lambda dst=dst: nc.gpsimd.dma_start(
                out=dst, in_=xqs.rearrange("p (dt c) -> p dt c", c=128)),
                reads=[xqs_k], writes=[("hx1qT", qt, qt + 1)])

        while not wq_done[0]:
            emit_wq_task()

        em.barrier()
        xqT, xqT_k = Gx(0, ND * HALF)
        otile, otile_k = Hx(1792, 512)
        otile_b, otile_b_k = Hx(3328, 512)
        sgb, sgb_k = Hx(256, 512)
        ubb, ubb_k = Hx(768, 512)
        intq, intq_k = Hx(1280, 512)
        scrP6b2 = [{
            "zz": Dx(640, 512), "yy": Dx(1152, 512),
            "c1": Dx(1664, 512), "c2": Dx(2176, 512),
        }, {
            "zz": Dx(2688, 512), "yy": Dx(3200, 512),
            "c1": Dx(3712, 512), "c2": Dx(4224, 512),
        }]

        def wslot(k):
            return Bb[:, k * 8192:(k + 1) * 8192], ("B", k * 8192, (k + 1) * 8192)

        def wdslot(k):
            lo = 24576 + k * 4096
            return Bb[:, lo:lo + 4096], ("B", lo, lo + 4096)

        qd = [Hx(2304, 512), Hx(2816, 512)]

        def p6b_products(ic):
            ici = ic % 4
            sgs, sgs_k = wslot((2 * ic) % 3)
            sus, sus_k = wslot((2 * ic + 1) % 3)
            op("dS", lambda ic=ic, sgs=sgs: nc.sync.dma_start(
                out=sgs.rearrange("p (dt c) -> p dt c", c=ICS),
                in_=s_wgT[ic].rearrange("dt p c -> p dt c")),
                reads=[("hw0", ic, ic + 1)], writes=[sgs_k])
            op("dS", lambda ic=ic, sus=sus: nc.sync.dma_start(
                out=sus.rearrange("p (dt c) -> p dt c", c=ICS),
                in_=s_wuT[ic].rearrange("dt p c -> p dt c")),
                reads=[("hw1", ic, ic + 1)], writes=[sus_k])
            for tokt in range(4):
                gps, gps_k = (ps_a[:], ("ps_a", 0, 512)) if tokt % 2 == 0 \
                    else (ps_c[:], ("ps_c", 0, 512))
                ups, ups_k = (ps_b[:], ("ps_b", 0, 512)) if tokt % 2 == 0 \
                    else (ps_big[:, 0:512], ("ps_big", 0, 512))
                sgb2, sgb2_k = (sgb, sgb_k) if tokt % 2 == 0 else (intq, intq_k)
                for dt in range(ND):
                    op("t", lambda dt=dt, tokt=tokt, sgs=sgs, gps=gps:
                       nc.tensor.matmul(
                           gps,
                           xqT[:, dt * HALF + tokt * 128:
                               dt * HALF + (tokt + 1) * 128],
                           sgs[:, dt * ICS:(dt + 1) * ICS],
                           start=(dt == 0), stop=(dt == ND - 1)),
                       reads=[xqT_k, sgs_k],
                       writes=[gps_k], acc=True)
                for dt in range(ND):
                    op("t", lambda dt=dt, tokt=tokt, sus=sus, ups=ups:
                       nc.tensor.matmul(
                           ups,
                           xqT[:, dt * HALF + tokt * 128:
                               dt * HALF + (tokt + 1) * 128],
                           sus[:, dt * ICS:(dt + 1) * ICS],
                           start=(dt == 0), stop=(dt == ND - 1)),
                       reads=[xqT_k, sus_k],
                       writes=[ups_k], acc=True)
                op("s", lambda gps=gps, sgb2=sgb2: nc.scalar.activation(
                    out=sgb2, in_=gps, func=ACT.Silu, bias=0.0,
                    scale=1.0), reads=[gps_k], writes=[sgb2_k])
                op("s", lambda ups=ups: nc.scalar.copy(out=ubb, in_=ups),
                   reads=[ups_k], writes=[ubb_k])
                dcol = (ici * 4 + tokt) * 512
                op("v", lambda dcol=dcol, sgb2=sgb2: nc.vector.tensor_tensor(
                    out=Ff[:, dcol:dcol + 512], in0=sgb2, in1=ubb,
                    op=ALU.mult),
                    reads=[sgb2_k, ubb_k], writes=[("F", dcol, dcol + 512)])

        for hf in range(NHALF):
            op("dG", lambda hf=hf: nc.gpsimd.dma_start(
                out=xqT.rearrange("p (dt c) -> p dt c", c=HALF),
                in_=s_x1qT.rearrange("(dt p) t -> p dt t", p=128)[
                    :, :, hf * HALF:(hf + 1) * HALF]),
                reads=[("hx1qT", 0, NT)], writes=[xqT_k])
            for ici in range(4):
                p6b_products(ici)
            for icq in range(NIC // 4):
                q_scales(Ff[:, 0:8192], [("F", 0, 8192)], 256, QA)
                pend = []
                for sub in range(16):
                    ici, tokt = sub // 4, sub % 4
                    ic2 = icq * 4 + ici
                    dcol = sub * 512
                    s = sub % 2
                    ch = q_elem_steps(
                        Ff[:, dcol:dcol + 512], [("F", dcol, dcol + 512)],
                        qd[s][0], qd[s][1], 512, QA, scrP6b2[s],
                        nb_off=dcol // 32)
                    pend.append((ch, ic2, tokt, s))
                    if len(pend) == 2 or sub == 15:
                        zipper([p[0] for p in pend])
                        if icq < 3 and sub % 4 == 3:
                            p6b_products((icq + 1) * 4 + sub // 4)
                        for (_, ic2b, tokt2, s2) in pend:
                            for j in range(4):
                                op("t", lambda j=j, s2=s2: nc.tensor.transpose(
                                    ps_tr[:, j * 128:(j + 1) * 128],
                                    qd[s2][0][:, j * 128:(j + 1) * 128], identb),
                                    reads=[qd[s2][1], identb_k],
                                    writes=[("ps_tr", j * 128, (j + 1) * 128)])
                            dsts = Aa[:].rearrange("p (it c) -> p it c", c=512)[
                                :, ic2b * 4:ic2b * 4 + 4,
                                tokt2 * 128:(tokt2 + 1) * 128]
                            op("s", lambda dsts=dsts: nc.scalar.copy(
                                out=dsts,
                                in_=ps_tr[:, 0:512].rearrange("p (n c) -> p n c", c=128)),
                                reads=[("ps_tr", 0, 512)],
                                writes=[("A", ic2b * 4 * 512, (ic2b * 4 + 4) * 512)])
                        pend = []
            def wd_dma(idx):
                et2, hlf = idx // 2, idx % 2
                wds, wds_k = wdslot(idx % 2)
                st = "dS" if (idx // 2) % 2 == 0 else "dG"
                eng = nc.sync if st == "dS" else nc.gpsimd
                op(st, lambda et2=et2, hlf=hlf, wds=wds, eng=eng: eng.dma_start(
                    out=wds.rearrange("p (it c) -> p it c", c=128),
                    in_=s_wdT[et2, hlf * 32:(hlf + 1) * 32].rearrange(
                        "it p c -> p it c")),
                    reads=[("hwd", et2, et2 + 1)], writes=[wds_k])

            wd_dma(0)
            wd_dma(1)
            for et in range(ND):
                pcol = (et % 4) * 512
                for hlf in range(2):
                    idx = et * 2 + hlf
                    wds, wds_k = wdslot(idx % 2)
                    for it2 in range(32):
                        it = hlf * 32 + it2
                        op("t", lambda it=it, it2=it2, wds=wds, pcol=pcol:
                           nc.tensor.matmul(
                               ps_big[:, pcol:pcol + 512],
                               wds[:, it2 * 128:(it2 + 1) * 128],
                               Aa[:, it * 512:(it + 1) * 512],
                               start=(it == 0), stop=(it == NFI - 1)),
                           reads=[wds_k, ("A", it * 512, (it + 1) * 512)],
                           writes=[("ps_big", pcol, pcol + 512)], acc=True)
                    if idx + 2 < ND * 2:
                        wd_dma(idx + 2)
                ot2, ot2_k = (otile, otile_k) if et % 2 == 0 else (otile_b, otile_b_k)
                op("s", lambda pcol=pcol, ot2=ot2: nc.scalar.copy(
                    out=ot2, in_=ps_big[:, pcol:pcol + 512]),
                    reads=[("ps_big", pcol, pcol + 512)], writes=[ot2_k])
                for tokt in range(4):
                    op("dG", lambda hf=hf, tokt=tokt, et=et, ot2=ot2:
                       nc.gpsimd.dma_start(
                           out=s_x2[hf, tokt, et],
                           in_=ot2[:, tokt * 128:(tokt + 1) * 128]),
                       reads=[ot2_k],
                       writes=[("hx2", (hf * 4 + tokt) * ND + et,
                                (hf * 4 + tokt) * ND + et + 1)])

        em.barrier()
        x2t, x2t_k = Gx(0, 2048)
        x2row, x2row_k = Hx(256, 2048)
        x1row7, x1row7_k = Hx(256 + 2048, 2048)
        outrow, outrow_k = Dx(4736, D)
        for qt in range(NT):
            hf, tokt = qt // 4, qt % 4
            op("dG", lambda hf=hf, tokt=tokt: nc.gpsimd.dma_start(
                out=x2t.rearrange("p (et c) -> p et c", c=128),
                in_=s_x2[hf, tokt].rearrange("et p c -> p et c")),
                reads=[("hx2", (hf * 4 + tokt) * ND, (hf * 4 + tokt + 1) * ND)],
                writes=[x2t_k])
            for g in range(2):
                transpose8(
                    lambda j, g=g: x2t[:, (g * 8 + j) * 128:(g * 8 + j + 1) * 128],
                    [x2t_k], x2row[:, g * 1024:(g + 1) * 1024],
                    ("H", 256 + g * 1024, 256 + (g + 1) * 1024), 8)
            op("dG", lambda qt=qt: nc.gpsimd.dma_start(
                out=x1row7, in_=s_x1[qt * 128:(qt + 1) * 128, :]),
                reads=[("hx1", qt, qt + 1)], writes=[x1row7_k])
            op("v", lambda: nc.vector.tensor_tensor(
                out=rrow, in0=x1row7, in1=x2row, op=ALU.add),
                reads=[x1row7_k, ("H", 256, 256 + 2048)], writes=[rrow_k])
            ln_stats(rrow, rrow_k)
            op("v", lambda: nc.vector.tensor_scalar(
                out=outrow, in0=rrow, scalar1=E("msc")[0][:, 0:1],
                scalar2=E("rstd")[0][:, 0:1], op0=ALU.subtract, op1=ALU.mult),
                reads=[rrow_k, E("msc")[1], E("rstd")[1]], writes=[outrow_k])
            op("dG", lambda qt=qt: nc.gpsimd.dma_start(
                out=out_h[qt * 128:(qt + 1) * 128, :], in_=outrow),
                reads=[outrow_k], writes=[("hout", qt, qt + 1)])

        em.finish("dS")

    return nc


_CACHE = {}


def kernel(L0_attn_x, L0_attn_wq, L0_attn_wk, L0_attn_wv, L0_attn_wo,
           L0_attn_bq, L0_attn_bk, L0_attn_bv,
           L0_ffn_w_gate, L0_ffn_w_up, L0_ffn_w_down):
    x = np.ascontiguousarray(np.asarray(L0_attn_x, dtype=np.float32))
    wq = np.asarray(L0_attn_wq, np.float32)
    wk = np.asarray(L0_attn_wk, np.float32)
    wv = np.asarray(L0_attn_wv, np.float32)
    wo = np.asarray(L0_attn_wo, np.float32)
    wqkvT = np.ascontiguousarray(np.concatenate([wq, wk, wv], axis=0).T)
    woT = np.ascontiguousarray(wo.T)
    bqkv = np.ascontiguousarray(np.concatenate(
        [np.asarray(L0_attn_bq, np.float32), np.asarray(L0_attn_bk, np.float32),
         np.asarray(L0_attn_bv, np.float32)]))
    wg = np.ascontiguousarray(np.asarray(L0_ffn_w_gate, np.float32))
    wu = np.ascontiguousarray(np.asarray(L0_ffn_w_up, np.float32))
    wd = np.ascontiguousarray(np.asarray(L0_ffn_w_down, np.float32))
    ident, swap, freqmod, pos, sgn, mask01 = _host_consts()

    if "nc" not in _CACHE:
        _CACHE["nc"] = build_nc()
    nc = _CACHE["nc"]

    shared = {
        "wqkvT": wqkvT, "woT": woT, "bqkv": bqkv,
        "wg": wg, "wu": wu, "wd": wd,
        "c_ident": ident, "c_swap": swap, "c_freqmod": freqmod,
        "c_pos": pos, "c_sgn": sgn, "c_mask01": mask01,
    }
    in_maps = [dict(shared, x=np.ascontiguousarray(x[c])) for c in range(B)]
    res = run_bass_kernel_spmd(nc, in_maps, core_ids=list(range(B)))
    out = np.stack([np.asarray(res.results[c]["out"], dtype=np.float32)
                    for c in range(B)], axis=0)
    return out


# version 09
# speedup vs baseline: 1.0345x; 1.0345x over previous
import math
import numpy as np
import concourse.bass as bass
import concourse.mybir as mybir
from concourse.bass_utils import run_bass_kernel_spmd
from contextlib import ExitStack

F32 = mybir.dt.float32
BF16 = mybir.dt.bfloat16
ALU = mybir.AluOpType
ACT = mybir.ActivationFunctionType
AXL = mybir.AxisListType

B = 8
T = 2048
D = 2048
NT = T // 128
ND = D // 128
HQ, HKV, DH = 16, 2, 128
FI = 8192
NFI = FI // 128
HALF = 512
NHALF = T // HALF
ICS = 512
NIC = FI // ICS
ALPHA = 1.0 / math.sqrt(DH)
LN_EPS = 1e-5

QA = (3.0, 2.0, 7.5)
QW = (1.0, 2.0, 6.0)
LADDER = [32, 16, 8, 4, 2, 1]
CLIP_LO, CLIP_HI = 2.0 ** -26, 2.0 ** 37
PI2_HI = float(np.float32(2.0 * math.pi))
PI2_LO = 2.0 * math.pi - PI2_HI


def _host_consts():
    ident = np.eye(128, dtype=np.float32)
    swap = np.zeros((128, 128), dtype=np.float32)
    for k in range(128):
        swap[k, (k + 64) % 128] = 1.0
    freqmod = (np.arange(128, dtype=np.float32) % 64)[None, :]
    pos = np.arange(T, dtype=np.float32)[None, :]
    sgn = np.concatenate([-np.ones(64, np.float32), np.ones(64, np.float32)])
    mask01 = np.tril(np.ones((128, 128), np.float32))
    return ident, swap, freqmod, pos, sgn, mask01


class Em:

    def __init__(self, nc, ctx):
        self.nc = nc
        self.sems = {}
        self.tot = {}
        self.nsem = {}
        self.step = {}
        self.cnt = {}
        self.eng = {}
        self._ctx = ctx
        self.wr = {}
        self.rd = {}
        self.mark = {}

    def stream(self, name, eng, step=1, nsem=1):
        for i in range(nsem):
            self.sems[(name, i)] = self._ctx.enter_context(
                self.nc.semaphore(f"S_{name}_{i}"))
            self.tot[(name, i)] = 0
        self.nsem[name] = nsem
        self.eng[name] = eng
        self.step[name] = step
        self.cnt[name] = 0

    def op(self, st, fn, reads=(), writes=(), acc=False):
        waits = {}

        def need(s, i, c):
            if waits.get((s, i), 0) < c:
                waits[(s, i)] = c

        def norm(k):
            return (k, 0, 1) if isinstance(k, str) else k

        reads = [norm(k) for k in reads]
        writes = [norm(k) for k in writes]
        for tag, lo, hi in reads:
            for (l2, h2, s, i, c) in self.wr.get(tag, ()):
                if l2 < hi and lo < h2:
                    need(s, i, c)
        for tag, lo, hi in writes:
            for (l2, h2, s, i, c) in self.wr.get(tag, ()):
                if l2 < hi and lo < h2 and not (acc and s == st):
                    need(s, i, c)
            for (l2, h2, s, i, c) in self.rd.get(tag, ()):
                if l2 < hi and lo < h2 and not (acc and s == st):
                    need(s, i, c)
        eng = self.eng[st]
        for (s, i) in sorted(waits):
            c = waits[(s, i)]
            if self.step[s] > 1:
                c = self.tot[(s, i)]
            if self.mark.get((st, s, i), -1) >= c:
                continue
            eng.wait_ge(self.sems[(s, i)], c)
            self.mark[(st, s, i)] = c
        inst = fn()
        idx = (self.cnt[st] // self.step[st]) % self.nsem[st]             if self.step[st] > 1 else 0
        inst.then_inc(self.sems[(st, idx)], self.step[st])
        self.cnt[st] += self.step[st]
        self.tot[(st, idx)] += self.step[st]
        c = self.tot[(st, idx)]
        for tag, lo, hi in writes:
            lst = [e for e in self.wr.get(tag, []) if not (e[0] < hi and lo < e[1])]
            lst.append((lo, hi, st, idx, c))
            self.wr[tag] = lst
            self.rd[tag] = [e for e in self.rd.get(tag, [])
                            if not (e[0] < hi and lo < e[1])]
        for tag, lo, hi in reads:
            lst = self.rd.setdefault(tag, [])
            lst[:] = [e for e in lst
                      if not (e[0] == lo and e[1] == hi and e[2] == st and e[3] == idx)]
            lst.append((lo, hi, st, idx, c))
        return inst

    def barrier(self):
        for w in self.nsem:
            eng = self.eng[w]
            for (s, i), t in self.tot.items():
                if t > 0 and self.mark.get((w, s, i), -1) < t:
                    eng.wait_ge(self.sems[(s, i)], t)
                    self.mark[(w, s, i)] = t
        self.wr.clear()
        self.rd.clear()

    def finish(self, st):
        eng = self.eng[st]
        for (s, i), t in self.tot.items():
            if t > 0 and self.mark.get((st, s, i), -1) < t:
                eng.wait_ge(self.sems[(s, i)], t)


def build_nc():
    nc = bass.Bass("TRN2", target_bir_lowering=False)

    x_h = nc.dram_tensor("x", [T, D], F32, kind="ExternalInput").ap()
    wqkvT_h = nc.dram_tensor("wqkvT", [D, (HQ + 2 * HKV) * DH], F32, kind="ExternalInput").ap()
    woT_h = nc.dram_tensor("woT", [D, D], F32, kind="ExternalInput").ap()
    wg_h = nc.dram_tensor("wg", [FI, D], F32, kind="ExternalInput").ap()
    wu_h = nc.dram_tensor("wu", [FI, D], F32, kind="ExternalInput").ap()
    wd_h = nc.dram_tensor("wd", [D, FI], F32, kind="ExternalInput").ap()
    bqkv_h = nc.dram_tensor("bqkv", [(HQ + 2 * HKV) * DH], F32, kind="ExternalInput").ap()
    ident_h = nc.dram_tensor("c_ident", [128, 128], F32, kind="ExternalInput").ap()
    swap_h = nc.dram_tensor("c_swap", [128, 128], F32, kind="ExternalInput").ap()
    freqmod_h = nc.dram_tensor("c_freqmod", [1, 128], F32, kind="ExternalInput").ap()
    pos_h = nc.dram_tensor("c_pos", [1, T], F32, kind="ExternalInput").ap()
    sgn_h = nc.dram_tensor("c_sgn", [128], F32, kind="ExternalInput").ap()
    mask_h = nc.dram_tensor("c_mask01", [128, 128], F32, kind="ExternalInput").ap()
    out_h = nc.dram_tensor("out", [T, D], F32, kind="ExternalOutput").ap()
    s_attn = nc.dram_tensor("s_attn", [T, D], BF16, kind="Internal").ap()
    s_x1 = nc.dram_tensor("s_x1", [T, D], BF16, kind="Internal").ap()
    s_x1qT = nc.dram_tensor("s_x1qT", [D, T], BF16, kind="Internal").ap()
    s_wgT = nc.dram_tensor("s_wgT", [NIC, ND, 128, ICS], BF16, kind="Internal").ap()
    s_wuT = nc.dram_tensor("s_wuT", [NIC, ND, 128, ICS], BF16, kind="Internal").ap()
    s_wdT = nc.dram_tensor("s_wdT", [ND, NFI, 128, 128], BF16, kind="Internal").ap()
    s_x2 = nc.dram_tensor("s_x2", [NHALF, HALF // 128, ND, 128, 128], BF16, kind="Internal").ap()

    with ExitStack() as ctx:
        def sbt(name, shape, dt=F32):
            return ctx.enter_context(nc.sbuf_tensor(name, shape, dt))

        Aa = sbt("arenaA", [128, 32768], BF16)
        Bb = sbt("arenaB", [128, 32768], BF16)
        Dd = sbt("arenaD", [128, 6784], F32)
        Ee = sbt("arenaE", [128, 3160], F32)
        Ff = sbt("arenaF", [128, 8192], BF16)
        Gg = sbt("arenaG", [128, 8192], BF16)
        Hh = sbt("arenaH", [128, 4352], BF16)
        CS2 = Gg[:, 4096:8192]

        ps_big = ctx.enter_context(nc.psum_tensor("ps_big", [128, 2048], F32))
        ps_a = ctx.enter_context(nc.psum_tensor("ps_a", [128, 512], F32))
        ps_b = ctx.enter_context(nc.psum_tensor("ps_b", [128, 512], F32))
        ps_tr = ctx.enter_context(nc.psum_tensor("ps_tr", [128, 1024], BF16))
        ps_c = ctx.enter_context(nc.psum_tensor("ps_c", [128, 512], F32))

        block = ctx.enter_context(nc.Block())
        em = Em(nc, ctx)
        em.stream("t", nc.tensor)
        em.stream("v", nc.vector)
        em.stream("s", nc.scalar)
        em.stream("g", nc.gpsimd)
        em.stream("dS", nc.sync, 16, nsem=12)
        em.stream("dG", nc.gpsimd, 16, nsem=12)
        op = em.op

        def Dx(lo, w):
            return Dd[:, lo:lo + w], ("D", lo, lo + w)

        def Fx(lo, w):
            return Ff[:, lo:lo + w], ("F", lo, lo + w)

        def Gx(lo, w):
            return Gg[:, lo:lo + w], ("G", lo, lo + w)

        def Hx(lo, w):
            return Hh[:, lo:lo + w], ("H", lo, lo + w)

        EC = {}
        _eo = [0]

        def edef(name, w):
            EC[name] = (_eo[0], w)
            _eo[0] += w
            assert _eo[0] <= 3160, name

        def E(name):
            o, w = EC[name]
            return Ee[:, o:o + w], ("E", o, o + w)

        def eN(name, nb, off=0):
            o, w = EC[name]
            assert off + nb <= w
            return Ee[:, o + off:o + off + nb], ("E", o + off, o + off + nb)

        for nm in ["amax", "rr", "pA", "pB", "pw", "csb", "cge", "ff",
                   "psm", "pinv", "zmul", "qmul"]:
            edef(nm, 256)
        for nm in ["cmag", "det_a", "det_b", "det_c", "kcn", "kcs", "keps",
                   "k2p30", "kc26", "kcm26", "kzA", "kqA", "kzW", "kqW",
                   "kclo", "kchi", "k2pinv", "kpihalf", "kpiclp", "kpicln",
                   "kan", "keps_ln", "bias_sm", "rowmax", "sum_a", "sum_d",
                   "sum_t", "inv_s", "mean", "msc", "m2sq", "var_s", "var1",
                   "var2", "var3", "sqv", "rstd", "sgn"]:
            edef(nm, 1)
        for w in LADDER:
            edef(f"kp2_{w}", 1)
            edef(f"kp2m1_{w}", 1)
        edef("bq", 20)

        idf, idf_k = Dx(0, 128)
        swf, swf_k = Dx(128, 128)
        maskf, maskf_k = Dx(256, 128)
        fqrow = Dd[0:1, 384:512]
        fqrow_k = ("D", 384, 512)
        invf = Dd[0:1, 512:640]
        invf_k = ("D", 512, 640)
        posrow = Dd[0:1, 640:640 + T]
        posrow_k = ("D", 640, 640 + T)
        identb, identb_k = Hx(0, 128)
        maskb, maskb_k = Hx(128, 128)

        op("dS", lambda: nc.sync.dma_start(out=idf, in_=ident_h[:]), writes=[idf_k])
        op("dS", lambda: nc.sync.dma_start(out=swf, in_=swap_h[:]), writes=[swf_k])
        op("dS", lambda: nc.sync.dma_start(out=maskf, in_=mask_h[:]), writes=[maskf_k])
        op("dS", lambda: nc.sync.dma_start(
            out=E("bq")[0], in_=bqkv_h.rearrange("(t p) -> p t", p=128),
            allow_slow_non_contiguous=True), writes=[E("bq")[1]])
        op("dS", lambda: nc.sync.dma_start(
            out=E("sgn")[0], in_=sgn_h.rearrange("(p o) -> p o", o=1),
            allow_slow_non_contiguous=True), writes=[E("sgn")[1]])
        op("dS", lambda: nc.sync.dma_start(out=fqrow, in_=freqmod_h[:]), writes=[fqrow_k])
        op("dS", lambda: nc.sync.dma_start(out=posrow, in_=pos_h[:]), writes=[posrow_k])

        op("v", lambda: nc.vector.tensor_copy(out=identb, in_=idf),
           reads=[idf_k], writes=[identb_k])
        op("v", lambda: nc.vector.tensor_copy(out=maskb, in_=maskf),
           reads=[maskf_k], writes=[maskb_k])

        def kmem(name, val):
            ap, key = E(name)
            op("v", lambda ap=ap, val=val: nc.vector.memset(ap, float(val)),
               writes=[key])

        kmem("kcn", 1.5 * 2.0 ** 23)
        kmem("kcs", 1.5 * 2.0 ** 52)
        kmem("keps", 2.0 ** -30)
        kmem("k2p30", 2.0 ** 30)
        kmem("kc26", 2.0 ** 26)
        kmem("kcm26", 2.0 ** -26)
        kmem("kzA", 2.0 ** (QA[0] + QA[1]))
        kmem("kqA", 2.0 ** -(QA[0] + QA[1]))
        kmem("kzW", 2.0 ** (QW[0] + QW[1]))
        kmem("kqW", 2.0 ** -(QW[0] + QW[1]))
        kmem("kclo", CLIP_LO)
        kmem("kchi", CLIP_HI)
        kmem("k2pinv", 1.0 / (2.0 * math.pi))
        kmem("kpihalf", math.pi / 2.0)
        kmem("kpiclp", 3.1415925)
        kmem("kpicln", -3.1415925)
        kmem("kan", -ALPHA)
        kmem("keps_ln", LN_EPS)
        for w in LADDER:
            kmem(f"kp2_{w}", 2.0 ** w)
            kmem(f"kp2m1_{w}", 2.0 ** w - 1.0)

        det_a, det_a_k = E("det_a")
        det_b, det_b_k = E("det_b")
        det_c, det_c_k = E("det_c")
        cmag, cmag_k = E("cmag")
        op("v", lambda: nc.vector.tensor_scalar(
            out=det_a, in0=Dd[:, 0:1], scalar1=float(-3.0e38), scalar2=None,
            op0=ALU.is_ge), reads=[("D", 0, 1)], writes=[det_a_k])
        op("v", lambda: nc.vector.tensor_tensor(
            out=det_b, in0=det_a, in1=E("keps")[0], op=ALU.mult),
            reads=[det_a_k, E("keps")[1]], writes=[det_b_k])
        op("v", lambda: nc.vector.tensor_tensor(
            out=det_c, in0=det_a, in1=det_b, op=ALU.add),
            reads=[det_a_k, det_b_k], writes=[det_c_k])
        op("v", lambda: nc.vector.tensor_tensor(
            out=det_b, in0=det_c, in1=det_a, op=ALU.subtract),
            reads=[det_c_k, det_a_k], writes=[det_b_k])
        op("v", lambda: nc.vector.tensor_tensor(
            out=det_c, in0=det_b, in1=E("k2p30")[0], op=ALU.mult),
            reads=[det_b_k, E("k2p30")[1]], writes=[det_c_k])
        op("v", lambda: nc.vector.tensor_tensor(
            out=det_a, in0=det_c, in1=E("kcs")[0], op=ALU.mult),
            reads=[det_c_k, E("kcs")[1]], writes=[det_a_k])
        op("v", lambda: nc.vector.tensor_scalar(
            out=det_b, in0=det_c, scalar1=-1.0, scalar2=1.0,
            op0=ALU.mult, op1=ALU.add), reads=[det_c_k], writes=[det_b_k])
        op("v", lambda: nc.vector.tensor_tensor(
            out=det_c, in0=det_b, in1=E("kcn")[0], op=ALU.mult),
            reads=[det_b_k, E("kcn")[1]], writes=[det_c_k])
        op("v", lambda: nc.vector.tensor_tensor(
            out=cmag, in0=det_a, in1=det_c, op=ALU.add),
            reads=[det_a_k, det_c_k], writes=[cmag_k])

        def rne(out_ap, in_ap, rk, wk, st="v"):
            op(st, lambda: nc.vector.tensor_scalar(
                out=out_ap, in0=in_ap, scalar1=cmag[:, 0:1], scalar2=cmag[:, 0:1],
                op0=ALU.add, op1=ALU.subtract),
                reads=list(rk) + [cmag_k], writes=list(wk))

        op("s", lambda: nc.scalar.activation(
            out=invf, in_=fqrow, func=ACT.Exp, bias=0.0,
            scale=-math.log(10000.0) / 64.0),
            reads=[fqrow_k], writes=[invf_k])
        angA, angA_k = Dx(2688, 1024)
        angB, angB_k = Dx(3712, 1024)
        angC, angC_k = Dx(4736, 1024)
        scrX, scrX_k = Dx(5760, 1024)
        for hh in range(2):
            base = hh * 1024
            for c in range(2):
                op("t", lambda c=c, base=base: nc.tensor.matmul(
                    ps_c[:, 0:512], invf,
                    posrow[:, base + c * 512:base + (c + 1) * 512],
                    start=True, stop=True),
                    reads=[invf_k, posrow_k], writes=[("ps_c", 0, 512)])
                op("v", lambda c=c: nc.vector.tensor_copy(
                    out=angA[:, c * 512:(c + 1) * 512], in_=ps_c[:, 0:512]),
                    reads=[("ps_c", 0, 512)], writes=[angA_k])
            for (use_shift, dst_lo) in [(True, 0), (False, T)]:
                if use_shift:
                    op("v", lambda: nc.vector.tensor_scalar(
                        out=angB, in0=angA, scalar1=E("kpihalf")[0][:, 0:1],
                        scalar2=None, op0=ALU.add),
                        reads=[angA_k, E("kpihalf")[1]], writes=[angB_k])
                else:
                    op("v", lambda: nc.vector.tensor_copy(out=angB, in_=angA),
                       reads=[angA_k], writes=[angB_k])
                for _ in range(2):
                    op("v", lambda: nc.vector.tensor_scalar(
                        out=angC, in0=angB, scalar1=E("k2pinv")[0][:, 0:1],
                        scalar2=None, op0=ALU.mult),
                        reads=[angB_k, E("k2pinv")[1]], writes=[angC_k])
                    rne(angC, angC, [angC_k], [angC_k])
                    op("v", lambda: nc.vector.scalar_tensor_tensor(
                        out=scrX, in0=angC, scalar=-PI2_HI, in1=angB,
                        op0=ALU.mult, op1=ALU.add),
                        reads=[angC_k, angB_k], writes=[scrX_k])
                    op("v", lambda: nc.vector.scalar_tensor_tensor(
                        out=angB, in0=angC, scalar=-PI2_LO, in1=scrX,
                        op0=ALU.mult, op1=ALU.add),
                        reads=[angC_k, scrX_k], writes=[angB_k])
                op("v", lambda: nc.vector.tensor_scalar(
                    out=angC, in0=angB, scalar1=E("kpiclp")[0][:, 0:1],
                    scalar2=E("kpicln")[0][:, 0:1], op0=ALU.min, op1=ALU.max),
                    reads=[angB_k, E("kpiclp")[1], E("kpicln")[1]], writes=[angC_k])
                if not use_shift:
                    op("s", lambda base=base, dst_lo=dst_lo: nc.scalar.activation(
                        out=scrX, in_=angC, func=ACT.Sin, bias=0.0, scale=1.0),
                        reads=[angC_k], writes=[scrX_k])
                    op("v", lambda base=base, dst_lo=dst_lo: nc.vector.tensor_scalar(
                        out=CS2[:, dst_lo + base:dst_lo + base + 1024], in0=scrX,
                        scalar1=E("sgn")[0][:, 0:1], scalar2=None, op0=ALU.mult),
                        reads=[scrX_k, E("sgn")[1]],
                        writes=[("G", 4096 + dst_lo + base, 4096 + dst_lo + base + 1024)])
                else:
                    op("s", lambda base=base, dst_lo=dst_lo: nc.scalar.activation(
                        out=CS2[:, dst_lo + base:dst_lo + base + 1024], in_=angC,
                        func=ACT.Sin, bias=0.0, scale=1.0),
                        reads=[angC_k],
                        writes=[("G", 4096 + dst_lo + base, 4096 + dst_lo + base + 1024)])
        C2 = CS2[:, 0:T]
        C2_k = ("G", 4096, 4096 + T)
        S2 = CS2[:, T:2 * T]
        S2_k = ("G", 4096 + T, 4096 + 2 * T)

        def q_amax(src_ap, src_keys, nb, off):
            amax, amax_k = eN("amax", nb, off)
            op("v", lambda: nc.vector.tensor_reduce(
                out=amax, in_=src_ap.rearrange("p (b k) -> p b k", k=32),
                axis=AXL.X, op=ALU.max, apply_absolute_value=True),
                reads=list(src_keys), writes=[amax_k])

        def q_ladder(nb, params, off=0, lst="v"):
            mfrac, emax, mn = params
            le = nc.gpsimd if lst == "g" else nc.vector
            kz = "kzA" if mfrac == QA[0] else "kzW"
            kq = "kqA" if mfrac == QA[0] else "kqW"
            amax, amax_k = eN("amax", nb, off)
            rr, rr_k = eN("rr", nb, off)
            pw, pw_k = eN("pw", nb, off)
            csb, csb_k = eN("csb", nb, off)
            cge, cge_k = eN("cge", nb, off)
            ffb, ffb_k = eN("ff", nb, off)
            psm, psm_k = eN("psm", nb, off)
            pinv, pinv_k = eN("pinv", nb, off)
            zmul, zmul_k = eN("zmul", nb, off)
            qmul, qmul_k = eN("qmul", nb, off)
            pa, pa_k = eN("pA", nb, off)
            pb, pb_k = eN("pB", nb, off)
            op(lst, lambda: le.tensor_scalar(
                out=csb, in0=amax, scalar1=E("kchi")[0][:, 0:1],
                scalar2=E("kclo")[0][:, 0:1], op0=ALU.min, op1=ALU.max),
                reads=[amax_k, E("kchi")[1], E("kclo")[1]], writes=[csb_k])
            op(lst, lambda: le.tensor_scalar(
                out=rr, in0=csb, scalar1=E("kc26")[0][:, 0:1], scalar2=None,
                op0=ALU.mult), reads=[csb_k, E("kc26")[1]], writes=[rr_k])
            cur, cur_k, nxt, nxt_k = pa, pa_k, pb, pb_k
            first = True
            for w in LADDER:
                if first:
                    op(lst, lambda w=w: le.tensor_scalar(
                        out=cge, in0=rr, scalar1=E(f"kp2_{w}")[0][:, 0:1],
                        scalar2=None, op0=ALU.is_ge),
                        reads=[rr_k, E(f"kp2_{w}")[1]], writes=[cge_k])
                else:
                    op(lst, lambda cur=cur, w=w: le.tensor_scalar(
                        out=pw, in0=cur, scalar1=E(f"kp2_{w}")[0][:, 0:1],
                        scalar2=None, op0=ALU.mult),
                        reads=[cur_k, E(f"kp2_{w}")[1]], writes=[pw_k])
                    op(lst, lambda: le.tensor_tensor(
                        out=cge, in0=rr, in1=pw, op=ALU.is_ge),
                        reads=[rr_k, pw_k], writes=[cge_k])
                if w >= 32:
                    op(lst, lambda w=w: le.tensor_scalar(
                        out=csb, in0=cge, scalar1=E(f"kp2_{w}")[0][:, 0:1],
                        scalar2=None, op0=ALU.mult),
                        reads=[cge_k, E(f"kp2_{w}")[1]], writes=[csb_k])
                    op(lst, lambda: le.tensor_scalar(
                        out=pw, in0=cge, scalar1=-1.0, scalar2=1.0,
                        op0=ALU.mult, op1=ALU.add), reads=[cge_k], writes=[pw_k])
                    op(lst, lambda: le.tensor_tensor(
                        out=ffb, in0=csb, in1=pw, op=ALU.add),
                        reads=[csb_k, pw_k], writes=[ffb_k])
                else:
                    op(lst, lambda w=w: le.tensor_scalar(
                        out=ffb, in0=cge, scalar1=E(f"kp2m1_{w}")[0][:, 0:1],
                        scalar2=1.0, op0=ALU.mult, op1=ALU.add),
                        reads=[cge_k, E(f"kp2m1_{w}")[1]], writes=[ffb_k])
                if first:
                    op(lst, lambda cur=cur: le.tensor_copy(
                        out=cur, in_=ffb), reads=[ffb_k], writes=[cur_k])
                    first = False
                else:
                    op(lst, lambda cur=cur, nxt=nxt: le.tensor_tensor(
                        out=nxt, in0=cur, in1=ffb, op=ALU.mult),
                        reads=[cur_k, ffb_k], writes=[nxt_k])
                    cur, cur_k, nxt, nxt_k = nxt, nxt_k, cur, cur_k
            op(lst, lambda cur=cur: le.tensor_scalar(
                out=psm, in0=cur, scalar1=E("kcm26")[0][:, 0:1], scalar2=None,
                op0=ALU.mult), reads=[cur_k, E("kcm26")[1]], writes=[psm_k])
            op("v", lambda: nc.vector.reciprocal(out=pinv, in_=psm),
               reads=[psm_k], writes=[pinv_k])
            op("v", lambda: nc.vector.tensor_scalar(
                out=zmul, in0=pinv, scalar1=E(kz)[0][:, 0:1], scalar2=None,
                op0=ALU.mult), reads=[pinv_k, E(kz)[1]], writes=[zmul_k])
            op(lst, lambda: le.tensor_scalar(
                out=qmul, in0=psm, scalar1=E(kq)[0][:, 0:1], scalar2=None,
                op0=ALU.mult), reads=[psm_k, E(kq)[1]], writes=[qmul_k])

        def q_scales(src_ap, src_keys, nb, params):
            mfrac, emax, mn = params
            q_amax(src_ap, src_keys, nb, 0)
            q_ladder(nb, params)

        def q_elem_steps(src_ap, src_keys, dst_ap, dst_key, ncol, params, scr,
                         nb_off=0):
            steps = []
            add = steps.append
            mfrac, emax, mn = params
            nb = ncol // 32
            zz, zz_k = scr["zz"]
            yy, yy_k = scr["yy"]
            c1, c1_k = scr["c1"]
            c2, c2_k = scr["c2"]
            zmul, zmul_k = eN("zmul", nb, nb_off)
            qmul, qmul_k = eN("qmul", nb, nb_off)
            th1 = float(2.0 ** (mfrac + 1))
            th2 = float(2.0 ** (mfrac + 2))
            mnz = float(mn * 2.0 ** mfrac)
            add(lambda: op("v", lambda: nc.vector.tensor_tensor(
                out=zz.rearrange("p (b k) -> p b k", k=32),
                in0=src_ap.rearrange("p (b k) -> p b k", k=32),
                in1=zmul.unsqueeze(2).broadcast_to([128, nb, 32]), op=ALU.mult),
                reads=list(src_keys) + [zmul_k], writes=[zz_k]))
            add(lambda: op("v", lambda: nc.vector.tensor_tensor(
                out=yy, in0=zz, in1=zz, op=ALU.mult),
                reads=[zz_k], writes=[yy_k]))
            add(lambda: op("v", lambda: nc.vector.tensor_scalar(
                out=c1, in0=yy, scalar1=th1 * th1, scalar2=None,
                op0=ALU.is_ge), reads=[yy_k], writes=[c1_k]))
            add(lambda: op("v", lambda: nc.vector.tensor_scalar(
                out=c2, in0=yy, scalar1=th2 * th2, scalar2=None,
                op0=ALU.is_ge), reads=[yy_k], writes=[c2_k]))
            add(lambda: op("v", lambda: nc.vector.tensor_scalar(
                out=c2, in0=c2, scalar1=cmag[:, 0:1], scalar2=cmag[:, 0:1],
                op0=ALU.mult, op1=ALU.add),
                reads=[c2_k, cmag_k], writes=[c2_k]))
            add(lambda: op("v", lambda: nc.vector.scalar_tensor_tensor(
                out=c1, in0=c1, scalar=1.0, in1=c2, op0=ALU.add, op1=ALU.mult),
                reads=[c1_k, c2_k], writes=[c1_k]))
            add(lambda: op("v", lambda: nc.vector.tensor_tensor(
                out=yy, in0=zz, in1=c1, op=ALU.add),
                reads=[zz_k, c1_k], writes=[yy_k]))
            add(lambda: op("v", lambda: nc.vector.tensor_tensor(
                out=zz, in0=yy, in1=c1, op=ALU.subtract),
                reads=[yy_k, c1_k], writes=[zz_k]))
            add(lambda: op("v", lambda: nc.vector.tensor_scalar(
                out=zz, in0=zz, scalar1=mnz, scalar2=-mnz, op0=ALU.min,
                op1=ALU.max), reads=[zz_k], writes=[zz_k]))
            add(lambda: op("v", lambda: nc.vector.tensor_tensor(
                out=dst_ap.rearrange("p (b k) -> p b k", k=32),
                in0=zz.rearrange("p (b k) -> p b k", k=32),
                in1=qmul.unsqueeze(2).broadcast_to([128, nb, 32]), op=ALU.mult),
                reads=[zz_k, qmul_k], writes=[dst_key]))
            return steps

        def zipper(chains):
            i = 0
            while any(chains):
                for ch in chains:
                    if i < len(ch):
                        ch[i]()
                i += 1
                if all(i >= len(ch) for ch in chains):
                    break

        def q_elem(src_ap, src_keys, dst_ap, dst_key, ncol, params, scr, nb_off=0):
            for st_ in q_elem_steps(src_ap, src_keys, dst_ap, dst_key, ncol,
                                    params, scr, nb_off):
                st_()

        def transpose8(src_ap_fn, src_keys, dst_ap, dst_key, n, ev="v"):
            for j in range(n):
                op("t", lambda j=j: nc.tensor.transpose(
                    ps_tr[:, j * 128:(j + 1) * 128], src_ap_fn(j), identb),
                    reads=list(src_keys) + [identb_k],
                    writes=[("ps_tr", j * 128, (j + 1) * 128)])
            if ev == "v":
                op("v", lambda: nc.vector.tensor_copy(
                    out=dst_ap, in_=ps_tr[:, 0:n * 128]),
                    reads=[("ps_tr", 0, n * 128)], writes=[dst_key])
            else:
                op("s", lambda: nc.scalar.copy(
                    out=dst_ap, in_=ps_tr[:, 0:n * 128]),
                    reads=[("ps_tr", 0, n * 128)], writes=[dst_key])

        em.barrier()
        xrow, xrow_k = Dx(2688, D)
        xrow_bf, xrow_bf_k = Hx(256, D)
        for tt in range(NT):
            op("dG", lambda tt=tt: nc.gpsimd.dma_start(
                out=xrow, in_=x_h[tt * 128:(tt + 1) * 128, :]), writes=[xrow_k])
            op("v", lambda: nc.vector.tensor_copy(out=xrow_bf, in_=xrow),
               reads=[xrow_k], writes=[xrow_bf_k])
            for g in range(2):
                dst = Aa[:].rearrange("p (dt c) -> p dt c", c=T)[
                    :, g * 8:(g + 1) * 8, tt * 128:(tt + 1) * 128]
                transpose8(
                    lambda j, g=g: xrow_bf[:, (g * 8 + j) * 128:(g * 8 + j + 1) * 128],
                    [xrow_bf_k], dst, ("A", g * 8 * T, (g * 8 + 8) * T), 8)

        NOUT = HQ + 2 * HKV
        wrawA, wrawA_k = Dx(640, D)
        wrawB, wrawB_k = Dx(2688, D)
        qraw, qraw_k = Dx(4736, D)
        wbfA, wbfA_k = Fx(0, D)
        wbfB, wbfB_k = Fx(2048, D)
        t1b, t1b_k = Fx(4096, D)
        vtmp, vtmp_k = Hx(256, 512)
        for ot in range(NOUT):
            wr, wr_k, wb, wb_k = ((wrawA, wrawA_k, wbfA, wbfA_k) if ot % 2 == 0
                                  else (wrawB, wrawB_k, wbfB, wbfB_k))
            src = wqkvT_h[:, ot * 128:(ot + 1) * 128].rearrange(
                "(a p) o -> p a o", p=128)
            op("dS", lambda src=src, wr=wr: nc.sync.dma_start(
                out=wr.rearrange("p (a o) -> p a o", o=128), in_=src),
                writes=[wr_k])
            op("v", lambda wb=wb, wr=wr: nc.vector.tensor_copy(out=wb, in_=wr),
               reads=[wr_k], writes=[wb_k])
            for ch in range(4):
                for dt in range(ND):
                    op("t", lambda ch=ch, dt=dt, wb=wb: nc.tensor.matmul(
                        ps_big[:, ch * 512:(ch + 1) * 512],
                        wb[:, dt * 128:(dt + 1) * 128],
                        Aa[:, dt * T + ch * 512:dt * T + (ch + 1) * 512],
                        start=(dt == 0), stop=(dt == ND - 1)),
                        reads=[wb_k, ("A", dt * T + ch * 512, dt * T + (ch + 1) * 512)],
                        writes=[("ps_big", ch * 512, (ch + 1) * 512)], acc=True)
            if ot < HQ + HKV:
                for ch in range(4):
                    qck = ("D", 4736 + ch * 512, 4736 + (ch + 1) * 512)
                    op("s", lambda ch=ch, ot=ot: nc.scalar.activation(
                        out=qraw[:, ch * 512:(ch + 1) * 512],
                        in_=ps_big[:, ch * 512:(ch + 1) * 512],
                        func=ACT.Identity, bias=E("bq")[0][:, ot:ot + 1], scale=1.0),
                        reads=[("ps_big", ch * 512, (ch + 1) * 512), E("bq")[1]],
                        writes=[qck])
                for ch in range(4):
                    qck = ("D", 4736 + ch * 512, 4736 + (ch + 1) * 512)
                    op("t", lambda ch=ch: nc.tensor.matmul(
                        ps_c[:, 0:512], swf, qraw[:, ch * 512:(ch + 1) * 512],
                        start=True, stop=True),
                        reads=[swf_k, qck], writes=[("ps_c", 0, 512)])
                    op("v", lambda ch=ch: nc.vector.tensor_tensor(
                        out=t1b[:, ch * 512:(ch + 1) * 512],
                        in0=qraw[:, ch * 512:(ch + 1) * 512],
                        in1=C2[:, ch * 512:(ch + 1) * 512], op=ALU.mult),
                        reads=[qck, C2_k],
                        writes=[("F", 4096 + ch * 512, 4096 + (ch + 1) * 512)])
                    op("v", lambda ch=ch: nc.vector.tensor_tensor(
                        out=qraw[:, ch * 512:(ch + 1) * 512], in0=ps_c[:, 0:512],
                        in1=S2[:, ch * 512:(ch + 1) * 512], op=ALU.mult),
                        reads=[("ps_c", 0, 512), S2_k], writes=[qck])
                    if ot < HQ:
                        dst = Bb[:, ot * T + ch * 512:ot * T + (ch + 1) * 512]
                        dk = ("B", ot * T + ch * 512, ot * T + (ch + 1) * 512)
                    else:
                        kvo = (ot - HQ) * T
                        dst = Gg[:, kvo + ch * 512:kvo + (ch + 1) * 512]
                        dk = ("G", kvo + ch * 512, kvo + (ch + 1) * 512)
                    op("v", lambda dst=dst, ch=ch: nc.vector.tensor_tensor(
                        out=dst, in0=t1b[:, ch * 512:(ch + 1) * 512],
                        in1=qraw[:, ch * 512:(ch + 1) * 512], op=ALU.add),
                        reads=[("F", 4096 + ch * 512, 4096 + (ch + 1) * 512), qck],
                        writes=[dk])
            else:
                kv = ot - HQ - HKV
                for ch in range(4):
                    op("s", lambda ch=ch, ot=ot: nc.scalar.activation(
                        out=vtmp, in_=ps_big[:, ch * 512:(ch + 1) * 512],
                        func=ACT.Identity, bias=E("bq")[0][:, ot:ot + 1], scale=1.0),
                        reads=[("ps_big", ch * 512, (ch + 1) * 512), E("bq")[1]],
                        writes=[vtmp_k])
                    base = 4096 + (kv * NT + ch * 4) * 128
                    transpose8(
                        lambda j: vtmp[:, j * 128:(j + 1) * 128],
                        [vtmp_k], Gg[:, base:base + 512],
                        ("G", base, base + 512), 4)

        rawI = [Dx(2688, 2048), Dx(4736, 2048)]
        scrI2 = [{
            "yy": Dx(640, 1024),
            "c1": Fx(4096, 1024), "c2": Fx(5120, 1024),
        }, {
            "yy": Dx(1664, 1024),
            "c1": Fx(6144, 1024), "c2": Fx(7168, 1024),
        }]
        wqnI, wqnI_k = Hx(2304, 2048)
        stageI, stageI_k = Hx(896, 1024)
        wq_parity = [0]

        def wq_pair_gen(kind, ic_or_et, pair):
            w_h = [wg_h, wu_h, wd_h][kind]
            eoff = 128 * wq_parity[0]
            wq_parity[0] ^= 1
            for k in range(2):
                wr, wr_k = rawI[k]
                if kind < 2:
                    ck = pair * 2 + k
                    r0 = ic_or_et * ICS + ck * 128
                    op("dS", lambda w_h=w_h, r0=r0, wr=wr: nc.sync.dma_start(
                        out=wr, in_=w_h[r0:r0 + 128, :]), writes=[wr_k])
                else:
                    cp = pair * 2 + k
                    op("dS", lambda w_h=w_h, et=ic_or_et, cp=cp, wr=wr:
                       nc.sync.dma_start(
                           out=wr, in_=w_h[et * 128:(et + 1) * 128,
                                           cp * 2048:(cp + 1) * 2048]),
                       writes=[wr_k])
                q_amax(rawI[k][0], [rawI[k][1]], 64, eoff + k * 64)
            yield
            q_ladder(128, QW, off=eoff)
            yield
            for k in range(2):
                wr, wr_k = rawI[k]
                chains = []
                for s in range(2):
                    lo = s * 1024
                    raw_sl = (wr[:, lo:lo + 1024],
                              (wr_k[0], wr_k[1] + lo, wr_k[1] + lo + 1024))
                    scr = dict(scrI2[s])
                    scr["zz"] = raw_sl
                    chains.append(q_elem_steps(
                        raw_sl[0], [raw_sl[1]],
                        wqnI[:, lo:lo + 1024],
                        (wqnI_k[0], wqnI_k[1] + lo, wqnI_k[1] + lo + 1024),
                        1024, QW, scr,
                        nb_off=eoff + k * 64 + s * 32))
                zipper(chains)
                yield
                for g in range(2):
                    transpose8(
                        lambda j, g=g: wqnI[:, (g * 8 + j) * 128:(g * 8 + j + 1) * 128],
                        [(wqnI_k[0], wqnI_k[1] + g * 1024, wqnI_k[1] + (g + 1) * 1024)],
                        stageI, stageI_k, 8, ev="s")
                    if kind < 2:
                        ck = pair * 2 + k
                        s_wT = s_wgT if kind == 0 else s_wuT
                        dst = s_wT[ic_or_et, g * 8:(g + 1) * 8, :,
                                   ck * 128:(ck + 1) * 128].rearrange(
                                       "dt p c -> p dt c")
                        hk = (f"hw{kind}", ic_or_et, ic_or_et + 1)
                    else:
                        cp = pair * 2 + k
                        dst = s_wdT[ic_or_et, cp * 16 + g * 8:cp * 16 + (g + 1) * 8,
                                    :, :].rearrange("it p c -> p it c")
                        hk = ("hwd", ic_or_et, ic_or_et + 1)
                    op("dS", lambda dst=dst: nc.sync.dma_start(
                        out=dst, in_=stageI.rearrange("p (n c) -> p n c", c=128)),
                        reads=[stageI_k], writes=[hk])
                    yield

        wq_tasks = []
        for kind in range(3):
            for i in range(16):
                for pair in range(2):
                    wq_tasks.append((kind, i, pair))
        wq_tasks.sort(key=lambda t: (t[1], t[2], t[0]))

        def _wq_units():
            for kind, i, pair in wq_tasks:
                for _ in wq_pair_gen(kind, i, pair):
                    yield
        wq_gen = _wq_units()
        wq_done = [False]

        def emit_wq_task(n=1):
            for _ in range(n):
                try:
                    next(wq_gen)
                except StopIteration:
                    wq_done[0] = True
                    return

        em.barrier()
        Pbuf, Pbuf_k = Fx(0, T)
        Pd, Pd_k = Hx(768, 128)
        Pn, Pn_k = Fx(2048, T)
        for h in range(HQ):
            kv = h // (HQ // HKV)
            qkb = ("B", h * T + 0, h * T + T)
            for qt in range(NT):
                emit_wq_task(3)
                L = (qt + 1) * 128
                nch = (L + 511) // 512
                for c in range(nch):
                    c0, c1_ = c * 512, min((c + 1) * 512, L)
                    op("t", lambda h=h, qt=qt, kv=kv, c0=c0, c1_=c1_:
                       nc.tensor.matmul(
                           ps_big[:, c0:c1_],
                           Bb[:, h * T + qt * 128:h * T + (qt + 1) * 128],
                           Gg[:, kv * T + c0:kv * T + c1_], start=True, stop=True),
                       reads=[qkb, ("G", kv * T + c0, kv * T + c1_)],
                       writes=[("ps_big", c0, c1_)])
                op("v", lambda L=L: nc.vector.tensor_reduce(
                    out=E("rowmax")[0], in_=ps_big[:, 0:L], axis=AXL.X, op=ALU.max),
                    reads=[("ps_big", 0, L)], writes=[E("rowmax")[1]])
                op("v", lambda: nc.vector.tensor_scalar(
                    out=E("bias_sm")[0], in0=E("rowmax")[0],
                    scalar1=E("kan")[0][:, 0:1], scalar2=None, op0=ALU.mult),
                    reads=[E("rowmax")[1], E("kan")[1]], writes=[E("bias_sm")[1]])
                if qt > 0:
                    op("s", lambda qt=qt: nc.scalar.activation(
                        out=Pbuf[:, 0:qt * 128], in_=ps_big[:, 0:qt * 128],
                        func=ACT.Exp, bias=E("bias_sm")[0][:, 0:1], scale=ALPHA,
                        accum_out=E("sum_a")[0]),
                        reads=[("ps_big", 0, qt * 128), E("bias_sm")[1]],
                        writes=[("F", 0, qt * 128), E("sum_a")[1]])
                op("s", lambda qt=qt, L=L: nc.scalar.activation(
                    out=Pd, in_=ps_big[:, qt * 128:L], func=ACT.Exp,
                    bias=E("bias_sm")[0][:, 0:1], scale=ALPHA),
                    reads=[("ps_big", qt * 128, L), E("bias_sm")[1]], writes=[Pd_k])
                op("v", lambda qt=qt, L=L: nc.vector.tensor_tensor(
                    out=Pbuf[:, qt * 128:L], in0=Pd, in1=maskb, op=ALU.mult),
                    reads=[Pd_k, maskb_k], writes=[("F", qt * 128, L)])
                op("v", lambda qt=qt, L=L: nc.vector.tensor_reduce(
                    out=E("sum_d")[0], in_=Pbuf[:, qt * 128:L], axis=AXL.X,
                    op=ALU.add),
                    reads=[("F", qt * 128, L)], writes=[E("sum_d")[1]])
                if qt > 0:
                    op("v", lambda: nc.vector.tensor_tensor(
                        out=E("sum_t")[0], in0=E("sum_a")[0], in1=E("sum_d")[0],
                        op=ALU.add),
                        reads=[E("sum_a")[1], E("sum_d")[1]], writes=[E("sum_t")[1]])
                else:
                    op("v", lambda: nc.vector.tensor_copy(
                        out=E("sum_t")[0], in_=E("sum_d")[0]),
                        reads=[E("sum_d")[1]], writes=[E("sum_t")[1]])
                op("v", lambda: nc.vector.reciprocal(
                    out=E("inv_s")[0], in_=E("sum_t")[0]),
                    reads=[E("sum_t")[1]], writes=[E("inv_s")[1]])
                op("v", lambda L=L: nc.vector.tensor_scalar(
                    out=Pn[:, 0:L], in0=Pbuf[:, 0:L],
                    scalar1=E("inv_s")[0][:, 0:1], scalar2=None, op0=ALU.mult),
                    reads=[("F", 0, L), E("inv_s")[1]],
                    writes=[("F", 2048, 2048 + L)])
                nkt = qt + 1
                for g in range((nkt + 7) // 8):
                    jn = min(8, nkt - g * 8)
                    transpose8(
                        lambda j, g=g: Pn[:, (g * 8 + j) * 128:(g * 8 + j + 1) * 128],
                        [("F", 2048 + g * 1024, 2048 + g * 1024 + jn * 128)],
                        Pbuf[:, g * 1024:g * 1024 + jn * 128],
                        ("F", g * 1024, g * 1024 + jn * 128), jn, ev="s")
                for kt in range(nkt):
                    vb = 4096 + (kv * NT + kt) * 128
                    op("t", lambda vb=vb, kt=kt, qt=qt: nc.tensor.matmul(
                        ps_a[:, 0:128], Gg[:, vb:vb + 128],
                        Pbuf[:, kt * 128:(kt + 1) * 128],
                        start=(kt == 0), stop=(kt == qt)),
                        reads=[("G", vb, vb + 128),
                               ("F", kt * 128, (kt + 1) * 128)],
                        writes=[("ps_a", 0, 128)], acc=True)
                op("s", lambda h=h, qt=qt: nc.scalar.copy(
                    out=Aa[:, h * T + qt * 128:h * T + (qt + 1) * 128],
                    in_=ps_a[:, 0:128]),
                    reads=[("ps_a", 0, 128)],
                    writes=[("A", h * T + qt * 128, h * T + (qt + 1) * 128)])

        em.barrier()
        wtmp, wtmp_k = Dx(0, 512)
        x1row, x1row_k = Hx(256 + D, D)
        x1q, x1q_k = Gx(0, D)
        xqs, xqs_k = Gx(D, 2048)
        rrow, rrow_k = Dx(640, D)
        srow, srow_k = Dx(2688, D)
        scrP5 = {
            "zz": Dx(4736, 2048), "yy": Dx(2688, 2048),
            "c1": Fx(0, 2048), "c2": Fx(2048, 2048),
        }

        def ln_stats(rr_ap, rr_key):
            op("v", lambda: nc.vector.tensor_reduce(
                out=E("mean")[0], in_=rr_ap, axis=AXL.X, op=ALU.add),
                reads=[rr_key], writes=[E("mean")[1]])
            op("v", lambda: nc.vector.tensor_scalar(
                out=E("msc")[0], in0=E("mean")[0], scalar1=1.0 / D, scalar2=None,
                op0=ALU.mult), reads=[E("mean")[1]], writes=[E("msc")[1]])
            op("s", lambda: nc.scalar.activation(
                out=srow, in_=rr_ap, func=ACT.Square, bias=0.0, scale=1.0,
                accum_out=E("var_s")[0]),
                reads=[rr_key], writes=[srow_k, E("var_s")[1]])
            op("v", lambda: nc.vector.tensor_tensor(
                out=E("m2sq")[0], in0=E("msc")[0], in1=E("msc")[0], op=ALU.mult),
                reads=[E("msc")[1]], writes=[E("m2sq")[1]])
            op("v", lambda: nc.vector.tensor_scalar(
                out=E("var1")[0], in0=E("var_s")[0], scalar1=1.0 / D,
                scalar2=None, op0=ALU.mult),
                reads=[E("var_s")[1]], writes=[E("var1")[1]])
            op("v", lambda: nc.vector.tensor_tensor(
                out=E("var2")[0], in0=E("var1")[0], in1=E("m2sq")[0],
                op=ALU.subtract),
                reads=[E("var1")[1], E("m2sq")[1]], writes=[E("var2")[1]])
            op("v", lambda: nc.vector.tensor_scalar(
                out=E("var3")[0], in0=E("var2")[0],
                scalar1=E("keps_ln")[0][:, 0:1], scalar2=None, op0=ALU.add),
                reads=[E("var2")[1], E("keps_ln")[1]], writes=[E("var3")[1]])
            op("s", lambda: nc.scalar.activation(
                out=E("sqv")[0], in_=E("var3")[0], func=ACT.Sqrt, bias=0.0,
                scale=1.0), reads=[E("var3")[1]], writes=[E("sqv")[1]])
            op("v", lambda: nc.vector.reciprocal(
                out=E("rstd")[0], in_=E("sqv")[0]),
                reads=[E("sqv")[1]], writes=[E("rstd")[1]])

        for eq in range(4):
            for ot in range(16):
                op("dS", lambda ot=ot, eq=eq: nc.sync.dma_start(
                    out=wtmp,
                    in_=woT_h[ot * 128:(ot + 1) * 128, eq * 512:(eq + 1) * 512]),
                    writes=[wtmp_k])
                wqlo = eq * 8192
                op("v", lambda ot=ot, wqlo=wqlo: nc.vector.tensor_copy(
                    out=Bb[:, wqlo + ot * 512:wqlo + (ot + 1) * 512], in_=wtmp),
                    reads=[wtmp_k],
                    writes=[("B", wqlo + ot * 512, wqlo + (ot + 1) * 512)])
        for qt in range(NT):
            emit_wq_task(2)
            op("dG", lambda qt=qt: nc.gpsimd.dma_start(
                out=srow, in_=x_h[qt * 128:(qt + 1) * 128, :]), writes=[srow_k])
            for eq in range(4):
                wqlo = eq * 8192
                pcol = eq * 512
                for ot in range(16):
                    op("t", lambda qt=qt, ot=ot, wqlo=wqlo, pcol=pcol:
                       nc.tensor.matmul(
                           ps_big[:, pcol:pcol + 512],
                           Aa[:, ot * T + qt * 128:ot * T + (qt + 1) * 128],
                           Bb[:, wqlo + ot * 512:wqlo + (ot + 1) * 512],
                           start=(ot == 0), stop=(ot == 15)),
                       reads=[("A", ot * T + qt * 128, ot * T + (qt + 1) * 128),
                              ("B", wqlo + ot * 512, wqlo + (ot + 1) * 512)],
                       writes=[("ps_big", pcol, pcol + 512)], acc=True)
            op("v", lambda: nc.vector.tensor_tensor(
                out=rrow, in0=srow, in1=ps_big[:, 0:2048], op=ALU.add),
                reads=[srow_k, ("ps_big", 0, 2048)], writes=[rrow_k])
            ln_stats(rrow, rrow_k)
            op("v", lambda: nc.vector.tensor_scalar(
                out=x1row, in0=rrow, scalar1=E("msc")[0][:, 0:1],
                scalar2=E("rstd")[0][:, 0:1], op0=ALU.subtract, op1=ALU.mult),
                reads=[rrow_k, E("msc")[1], E("rstd")[1]], writes=[x1row_k])
            op("dG", lambda qt=qt: nc.gpsimd.dma_start(
                out=s_x1[qt * 128:(qt + 1) * 128, :], in_=x1row),
                reads=[x1row_k], writes=[("hx1", qt, qt + 1)])
            q_scales(x1row, [x1row_k], 64, QA)
            q_elem(x1row, [x1row_k], x1q, x1q_k, D, QA, scrP5)
            for g in range(2):
                transpose8(
                    lambda j, g=g: x1q[:, (g * 8 + j) * 128:(g * 8 + j + 1) * 128],
                    [x1q_k], xqs[:, g * 1024:(g + 1) * 1024],
                    ("G", D + g * 1024, D + (g + 1) * 1024), 8)
            dst = s_x1qT.rearrange("(dt p) t -> p dt t", p=128)[
                :, :, qt * 128:(qt + 1) * 128]
            op("dG", lambda dst=dst: nc.gpsimd.dma_start(
                out=dst, in_=xqs.rearrange("p (dt c) -> p dt c", c=128)),
                reads=[xqs_k], writes=[("hx1qT", qt, qt + 1)])

        while not wq_done[0]:
            emit_wq_task()

        em.barrier()
        xqT, xqT_k = Gx(0, ND * HALF)
        otile, otile_k = Hx(1792, 512)
        otile_b, otile_b_k = Hx(3328, 512)
        sgb, sgb_k = Hx(256, 512)
        ubb, ubb_k = Hx(768, 512)
        intq, intq_k = Hx(1280, 512)
        scrP6b2 = [{
            "zz": Dx(640, 512), "yy": Dx(1152, 512),
            "c1": Dx(1664, 512), "c2": Dx(2176, 512),
        }, {
            "zz": Dx(2688, 512), "yy": Dx(3200, 512),
            "c1": Dx(3712, 512), "c2": Dx(4224, 512),
        }]

        def wslot(k):
            return Bb[:, k * 8192:(k + 1) * 8192], ("B", k * 8192, (k + 1) * 8192)

        def wdslot(k):
            lo = 24576 + k * 4096
            return Bb[:, lo:lo + 4096], ("B", lo, lo + 4096)

        qd = [Hx(2304, 512), Hx(2816, 512)]

        def p6b_products(ic):
            ici = ic % 4
            sgs, sgs_k = wslot((2 * ic) % 3)
            sus, sus_k = wslot((2 * ic + 1) % 3)
            op("dS", lambda ic=ic, sgs=sgs: nc.sync.dma_start(
                out=sgs.rearrange("p (dt c) -> p dt c", c=ICS),
                in_=s_wgT[ic].rearrange("dt p c -> p dt c")),
                reads=[("hw0", ic, ic + 1)], writes=[sgs_k])
            op("dS", lambda ic=ic, sus=sus: nc.sync.dma_start(
                out=sus.rearrange("p (dt c) -> p dt c", c=ICS),
                in_=s_wuT[ic].rearrange("dt p c -> p dt c")),
                reads=[("hw1", ic, ic + 1)], writes=[sus_k])
            for tokt in range(4):
                gps, gps_k = (ps_a[:], ("ps_a", 0, 512)) if tokt % 2 == 0 \
                    else (ps_c[:], ("ps_c", 0, 512))
                ups, ups_k = (ps_b[:], ("ps_b", 0, 512)) if tokt % 2 == 0 \
                    else (ps_big[:, 0:512], ("ps_big", 0, 512))
                sgb2, sgb2_k = (sgb, sgb_k) if tokt % 2 == 0 else (intq, intq_k)
                for dt in range(ND):
                    op("t", lambda dt=dt, tokt=tokt, sgs=sgs, gps=gps:
                       nc.tensor.matmul(
                           gps,
                           xqT[:, dt * HALF + tokt * 128:
                               dt * HALF + (tokt + 1) * 128],
                           sgs[:, dt * ICS:(dt + 1) * ICS],
                           start=(dt == 0), stop=(dt == ND - 1)),
                       reads=[xqT_k, sgs_k],
                       writes=[gps_k], acc=True)
                for dt in range(ND):
                    op("t", lambda dt=dt, tokt=tokt, sus=sus, ups=ups:
                       nc.tensor.matmul(
                           ups,
                           xqT[:, dt * HALF + tokt * 128:
                               dt * HALF + (tokt + 1) * 128],
                           sus[:, dt * ICS:(dt + 1) * ICS],
                           start=(dt == 0), stop=(dt == ND - 1)),
                       reads=[xqT_k, sus_k],
                       writes=[ups_k], acc=True)
                op("s", lambda gps=gps, sgb2=sgb2: nc.scalar.activation(
                    out=sgb2, in_=gps, func=ACT.Silu, bias=0.0,
                    scale=1.0), reads=[gps_k], writes=[sgb2_k])
                op("s", lambda ups=ups: nc.scalar.copy(out=ubb, in_=ups),
                   reads=[ups_k], writes=[ubb_k])
                dcol = (ici * 4 + tokt) * 512
                op("v", lambda dcol=dcol, sgb2=sgb2: nc.vector.tensor_tensor(
                    out=Ff[:, dcol:dcol + 512], in0=sgb2, in1=ubb,
                    op=ALU.mult),
                    reads=[sgb2_k, ubb_k], writes=[("F", dcol, dcol + 512)])

        for hf in range(NHALF):
            op("dG", lambda hf=hf: nc.gpsimd.dma_start(
                out=xqT.rearrange("p (dt c) -> p dt c", c=HALF),
                in_=s_x1qT.rearrange("(dt p) t -> p dt t", p=128)[
                    :, :, hf * HALF:(hf + 1) * HALF]),
                reads=[("hx1qT", 0, NT)], writes=[xqT_k])
            for ici in range(4):
                p6b_products(ici)
            for icq in range(NIC // 4):
                q_scales(Ff[:, 0:8192], [("F", 0, 8192)], 256, QA)
                pend = []
                for sub in range(16):
                    ici, tokt = sub // 4, sub % 4
                    ic2 = icq * 4 + ici
                    dcol = sub * 512
                    s = sub % 2
                    ch = q_elem_steps(
                        Ff[:, dcol:dcol + 512], [("F", dcol, dcol + 512)],
                        qd[s][0], qd[s][1], 512, QA, scrP6b2[s],
                        nb_off=dcol // 32)
                    pend.append((ch, ic2, tokt, s))
                    if len(pend) == 2 or sub == 15:
                        zipper([p[0] for p in pend])
                        if icq < 3 and sub % 4 == 3:
                            p6b_products((icq + 1) * 4 + sub // 4)
                        for (_, ic2b, tokt2, s2) in pend:
                            for j in range(4):
                                op("t", lambda j=j, s2=s2: nc.tensor.transpose(
                                    ps_tr[:, j * 128:(j + 1) * 128],
                                    qd[s2][0][:, j * 128:(j + 1) * 128], identb),
                                    reads=[qd[s2][1], identb_k],
                                    writes=[("ps_tr", j * 128, (j + 1) * 128)])
                            dsts = Aa[:].rearrange("p (it c) -> p it c", c=512)[
                                :, ic2b * 4:ic2b * 4 + 4,
                                tokt2 * 128:(tokt2 + 1) * 128]
                            op("s", lambda dsts=dsts: nc.scalar.copy(
                                out=dsts,
                                in_=ps_tr[:, 0:512].rearrange("p (n c) -> p n c", c=128)),
                                reads=[("ps_tr", 0, 512)],
                                writes=[("A", ic2b * 4 * 512, (ic2b * 4 + 4) * 512)])
                        pend = []
            def wd_dma(idx):
                et2, hlf = idx // 2, idx % 2
                wds, wds_k = wdslot(idx % 2)
                st = "dS" if (idx // 2) % 2 == 0 else "dG"
                eng = nc.sync if st == "dS" else nc.gpsimd
                op(st, lambda et2=et2, hlf=hlf, wds=wds, eng=eng: eng.dma_start(
                    out=wds.rearrange("p (it c) -> p it c", c=128),
                    in_=s_wdT[et2, hlf * 32:(hlf + 1) * 32].rearrange(
                        "it p c -> p it c")),
                    reads=[("hwd", et2, et2 + 1)], writes=[wds_k])

            wd_dma(0)
            wd_dma(1)
            for et in range(ND):
                pcol = (et % 4) * 512
                for hlf in range(2):
                    idx = et * 2 + hlf
                    wds, wds_k = wdslot(idx % 2)
                    for it2 in range(32):
                        it = hlf * 32 + it2
                        op("t", lambda it=it, it2=it2, wds=wds, pcol=pcol:
                           nc.tensor.matmul(
                               ps_big[:, pcol:pcol + 512],
                               wds[:, it2 * 128:(it2 + 1) * 128],
                               Aa[:, it * 512:(it + 1) * 512],
                               start=(it == 0), stop=(it == NFI - 1)),
                           reads=[wds_k, ("A", it * 512, (it + 1) * 512)],
                           writes=[("ps_big", pcol, pcol + 512)], acc=True)
                    if idx + 2 < ND * 2:
                        wd_dma(idx + 2)
                ot2, ot2_k = (otile, otile_k) if et % 2 == 0 else (otile_b, otile_b_k)
                op("s", lambda pcol=pcol, ot2=ot2: nc.scalar.copy(
                    out=ot2, in_=ps_big[:, pcol:pcol + 512]),
                    reads=[("ps_big", pcol, pcol + 512)], writes=[ot2_k])
                for tokt in range(4):
                    op("dG", lambda hf=hf, tokt=tokt, et=et, ot2=ot2:
                       nc.gpsimd.dma_start(
                           out=s_x2[hf, tokt, et],
                           in_=ot2[:, tokt * 128:(tokt + 1) * 128]),
                       reads=[ot2_k],
                       writes=[("hx2", (hf * 4 + tokt) * ND + et,
                                (hf * 4 + tokt) * ND + et + 1)])

        em.barrier()
        x2t, x2t_k = Gx(0, 2048)
        x2row, x2row_k = Hx(256, 2048)
        x1row7, x1row7_k = Hx(256 + 2048, 2048)
        outrow, outrow_k = Dx(4736, D)
        for qt in range(NT):
            hf, tokt = qt // 4, qt % 4
            op("dG", lambda hf=hf, tokt=tokt: nc.gpsimd.dma_start(
                out=x2t.rearrange("p (et c) -> p et c", c=128),
                in_=s_x2[hf, tokt].rearrange("et p c -> p et c")),
                reads=[("hx2", (hf * 4 + tokt) * ND, (hf * 4 + tokt + 1) * ND)],
                writes=[x2t_k])
            for g in range(2):
                transpose8(
                    lambda j, g=g: x2t[:, (g * 8 + j) * 128:(g * 8 + j + 1) * 128],
                    [x2t_k], x2row[:, g * 1024:(g + 1) * 1024],
                    ("H", 256 + g * 1024, 256 + (g + 1) * 1024), 8)
            op("dG", lambda qt=qt: nc.gpsimd.dma_start(
                out=x1row7, in_=s_x1[qt * 128:(qt + 1) * 128, :]),
                reads=[("hx1", qt, qt + 1)], writes=[x1row7_k])
            op("v", lambda: nc.vector.tensor_tensor(
                out=rrow, in0=x1row7, in1=x2row, op=ALU.add),
                reads=[x1row7_k, ("H", 256, 256 + 2048)], writes=[rrow_k])
            ln_stats(rrow, rrow_k)
            op("v", lambda: nc.vector.tensor_scalar(
                out=outrow, in0=rrow, scalar1=E("msc")[0][:, 0:1],
                scalar2=E("rstd")[0][:, 0:1], op0=ALU.subtract, op1=ALU.mult),
                reads=[rrow_k, E("msc")[1], E("rstd")[1]], writes=[outrow_k])
            op("dG", lambda qt=qt: nc.gpsimd.dma_start(
                out=out_h[qt * 128:(qt + 1) * 128, :], in_=outrow),
                reads=[outrow_k], writes=[("hout", qt, qt + 1)])

        em.finish("dS")

    return nc


_CACHE = {}


def kernel(L0_attn_x, L0_attn_wq, L0_attn_wk, L0_attn_wv, L0_attn_wo,
           L0_attn_bq, L0_attn_bk, L0_attn_bv,
           L0_ffn_w_gate, L0_ffn_w_up, L0_ffn_w_down):
    x = np.ascontiguousarray(np.asarray(L0_attn_x, dtype=np.float32))
    wq = np.asarray(L0_attn_wq, np.float32)
    wk = np.asarray(L0_attn_wk, np.float32)
    wv = np.asarray(L0_attn_wv, np.float32)
    wo = np.asarray(L0_attn_wo, np.float32)
    wqkvT = np.ascontiguousarray(np.concatenate([wq, wk, wv], axis=0).T)
    woT = np.ascontiguousarray(wo.T)
    bqkv = np.ascontiguousarray(np.concatenate(
        [np.asarray(L0_attn_bq, np.float32), np.asarray(L0_attn_bk, np.float32),
         np.asarray(L0_attn_bv, np.float32)]))
    wg = np.ascontiguousarray(np.asarray(L0_ffn_w_gate, np.float32))
    wu = np.ascontiguousarray(np.asarray(L0_ffn_w_up, np.float32))
    wd = np.ascontiguousarray(np.asarray(L0_ffn_w_down, np.float32))
    ident, swap, freqmod, pos, sgn, mask01 = _host_consts()

    if "nc" not in _CACHE:
        _CACHE["nc"] = build_nc()
    nc = _CACHE["nc"]

    shared = {
        "wqkvT": wqkvT, "woT": woT, "bqkv": bqkv,
        "wg": wg, "wu": wu, "wd": wd,
        "c_ident": ident, "c_swap": swap, "c_freqmod": freqmod,
        "c_pos": pos, "c_sgn": sgn, "c_mask01": mask01,
    }
    in_maps = [dict(shared, x=np.ascontiguousarray(x[c])) for c in range(B)]
    res = run_bass_kernel_spmd(nc, in_maps, core_ids=list(range(B)))
    out = np.stack([np.asarray(res.results[c]["out"], dtype=np.float32)
                    for c in range(B)], axis=0)
    return out


# version 11
# speedup vs baseline: 1.0435x; 1.0087x over previous
import math
import numpy as np
import concourse.bass as bass
import concourse.mybir as mybir
from concourse.bass_utils import run_bass_kernel_spmd
from contextlib import ExitStack

F32 = mybir.dt.float32
BF16 = mybir.dt.bfloat16
ALU = mybir.AluOpType
ACT = mybir.ActivationFunctionType
AXL = mybir.AxisListType

B = 8
T = 2048
D = 2048
NT = T // 128
ND = D // 128
HQ, HKV, DH = 16, 2, 128
FI = 8192
NFI = FI // 128
HALF = 512
NHALF = T // HALF
ICS = 512
NIC = FI // ICS
ALPHA = 1.0 / math.sqrt(DH)
LN_EPS = 1e-5

QA = (3.0, 2.0, 7.5)
QW = (1.0, 2.0, 6.0)
LADDER = [32, 16, 8, 4, 2, 1]
CLIP_LO, CLIP_HI = 2.0 ** -26, 2.0 ** 37
PI2_HI = float(np.float32(2.0 * math.pi))
PI2_LO = 2.0 * math.pi - PI2_HI


def _host_consts():
    ident = np.eye(128, dtype=np.float32)
    swap = np.zeros((128, 128), dtype=np.float32)
    for k in range(128):
        swap[k, (k + 64) % 128] = 1.0
    freqmod = (np.arange(128, dtype=np.float32) % 64)[None, :]
    pos = np.arange(T, dtype=np.float32)[None, :]
    sgn = np.concatenate([-np.ones(64, np.float32), np.ones(64, np.float32)])
    mask01 = np.tril(np.ones((128, 128), np.float32))
    return ident, swap, freqmod, pos, sgn, mask01


class Em:

    def __init__(self, nc, ctx):
        self.nc = nc
        self.sems = {}
        self.tot = {}
        self.nsem = {}
        self.step = {}
        self.cnt = {}
        self.eng = {}
        self._ctx = ctx
        self.wr = {}
        self.rd = {}
        self.mark = {}

    def stream(self, name, eng, step=1, nsem=1):
        for i in range(nsem):
            self.sems[(name, i)] = self._ctx.enter_context(
                self.nc.semaphore(f"S_{name}_{i}"))
            self.tot[(name, i)] = 0
        self.nsem[name] = nsem
        self.eng[name] = eng
        self.step[name] = step
        self.cnt[name] = 0

    def op(self, st, fn, reads=(), writes=(), acc=False):
        waits = {}

        def need(s, i, c):
            if waits.get((s, i), 0) < c:
                waits[(s, i)] = c

        def norm(k):
            return (k, 0, 1) if isinstance(k, str) else k

        reads = [norm(k) for k in reads]
        writes = [norm(k) for k in writes]
        for tag, lo, hi in reads:
            for (l2, h2, s, i, c) in self.wr.get(tag, ()):
                if l2 < hi and lo < h2:
                    need(s, i, c)
        for tag, lo, hi in writes:
            for (l2, h2, s, i, c) in self.wr.get(tag, ()):
                if l2 < hi and lo < h2 and not (acc and s == st):
                    need(s, i, c)
            for (l2, h2, s, i, c) in self.rd.get(tag, ()):
                if l2 < hi and lo < h2 and not (acc and s == st):
                    need(s, i, c)
        eng = self.eng[st]
        for (s, i) in sorted(waits):
            c = waits[(s, i)]
            if self.step[s] > 1:
                c = self.tot[(s, i)]
            if self.mark.get((st, s, i), -1) >= c:
                continue
            eng.wait_ge(self.sems[(s, i)], c)
            self.mark[(st, s, i)] = c
        inst = fn()
        idx = (self.cnt[st] // self.step[st]) % self.nsem[st]             if self.step[st] > 1 else 0
        inst.then_inc(self.sems[(st, idx)], self.step[st])
        self.cnt[st] += self.step[st]
        self.tot[(st, idx)] += self.step[st]
        c = self.tot[(st, idx)]
        for tag, lo, hi in writes:
            lst = [e for e in self.wr.get(tag, []) if not (e[0] < hi and lo < e[1])]
            lst.append((lo, hi, st, idx, c))
            self.wr[tag] = lst
            self.rd[tag] = [e for e in self.rd.get(tag, [])
                            if not (e[0] < hi and lo < e[1])]
        for tag, lo, hi in reads:
            lst = self.rd.setdefault(tag, [])
            lst[:] = [e for e in lst
                      if not (e[0] == lo and e[1] == hi and e[2] == st and e[3] == idx)]
            lst.append((lo, hi, st, idx, c))
        return inst

    def barrier(self):
        for w in self.nsem:
            eng = self.eng[w]
            for (s, i), t in self.tot.items():
                if t > 0 and self.mark.get((w, s, i), -1) < t:
                    eng.wait_ge(self.sems[(s, i)], t)
                    self.mark[(w, s, i)] = t
        self.wr.clear()
        self.rd.clear()

    def finish(self, st):
        eng = self.eng[st]
        for (s, i), t in self.tot.items():
            if t > 0 and self.mark.get((st, s, i), -1) < t:
                eng.wait_ge(self.sems[(s, i)], t)


def build_nc():
    nc = bass.Bass("TRN2", target_bir_lowering=False)

    x_h = nc.dram_tensor("x", [T, D], F32, kind="ExternalInput").ap()
    wqkvT_h = nc.dram_tensor("wqkvT", [D, (HQ + 2 * HKV) * DH], F32, kind="ExternalInput").ap()
    woT_h = nc.dram_tensor("woT", [D, D], F32, kind="ExternalInput").ap()
    wg_h = nc.dram_tensor("wg", [FI, D], F32, kind="ExternalInput").ap()
    wu_h = nc.dram_tensor("wu", [FI, D], F32, kind="ExternalInput").ap()
    wd_h = nc.dram_tensor("wd", [D, FI], F32, kind="ExternalInput").ap()
    bqkv_h = nc.dram_tensor("bqkv", [(HQ + 2 * HKV) * DH], F32, kind="ExternalInput").ap()
    ident_h = nc.dram_tensor("c_ident", [128, 128], F32, kind="ExternalInput").ap()
    swap_h = nc.dram_tensor("c_swap", [128, 128], F32, kind="ExternalInput").ap()
    freqmod_h = nc.dram_tensor("c_freqmod", [1, 128], F32, kind="ExternalInput").ap()
    pos_h = nc.dram_tensor("c_pos", [1, T], F32, kind="ExternalInput").ap()
    sgn_h = nc.dram_tensor("c_sgn", [128], F32, kind="ExternalInput").ap()
    mask_h = nc.dram_tensor("c_mask01", [128, 128], F32, kind="ExternalInput").ap()
    out_h = nc.dram_tensor("out", [T, D], F32, kind="ExternalOutput").ap()
    s_attn = nc.dram_tensor("s_attn", [T, D], BF16, kind="Internal").ap()
    s_x1 = nc.dram_tensor("s_x1", [T, D], BF16, kind="Internal").ap()
    s_x1qT = nc.dram_tensor("s_x1qT", [D, T], BF16, kind="Internal").ap()
    s_wgT = nc.dram_tensor("s_wgT", [NIC, ND, 128, ICS], BF16, kind="Internal").ap()
    s_wuT = nc.dram_tensor("s_wuT", [NIC, ND, 128, ICS], BF16, kind="Internal").ap()
    s_wdT = nc.dram_tensor("s_wdT", [ND, NFI, 128, 128], BF16, kind="Internal").ap()
    s_x2 = nc.dram_tensor("s_x2", [NHALF, HALF // 128, ND, 128, 128], BF16, kind="Internal").ap()

    with ExitStack() as ctx:
        def sbt(name, shape, dt=F32):
            return ctx.enter_context(nc.sbuf_tensor(name, shape, dt))

        Aa = sbt("arenaA", [128, 32768], BF16)
        Bb = sbt("arenaB", [128, 32768], BF16)
        Dd = sbt("arenaD", [128, 6784], F32)
        Ee = sbt("arenaE", [128, 3160], F32)
        Ff = sbt("arenaF", [128, 8192], BF16)
        Gg = sbt("arenaG", [128, 8192], BF16)
        Hh = sbt("arenaH", [128, 4352], BF16)
        CS2 = Gg[:, 4096:8192]

        ps_big = ctx.enter_context(nc.psum_tensor("ps_big", [128, 2048], F32))
        ps_a = ctx.enter_context(nc.psum_tensor("ps_a", [128, 512], F32))
        ps_b = ctx.enter_context(nc.psum_tensor("ps_b", [128, 512], F32))
        ps_tr = ctx.enter_context(nc.psum_tensor("ps_tr", [128, 1024], BF16))
        ps_c = ctx.enter_context(nc.psum_tensor("ps_c", [128, 512], F32))

        block = ctx.enter_context(nc.Block())
        em = Em(nc, ctx)
        em.stream("t", nc.tensor)
        em.stream("v", nc.vector)
        em.stream("s", nc.scalar)
        em.stream("g", nc.gpsimd)
        em.stream("dS", nc.sync, 16, nsem=12)
        em.stream("dG", nc.gpsimd, 16, nsem=12)
        op = em.op

        def Dx(lo, w):
            return Dd[:, lo:lo + w], ("D", lo, lo + w)

        def Fx(lo, w):
            return Ff[:, lo:lo + w], ("F", lo, lo + w)

        def Gx(lo, w):
            return Gg[:, lo:lo + w], ("G", lo, lo + w)

        def Hx(lo, w):
            return Hh[:, lo:lo + w], ("H", lo, lo + w)

        EC = {}
        _eo = [0]

        def edef(name, w):
            EC[name] = (_eo[0], w)
            _eo[0] += w
            assert _eo[0] <= 3160, name

        def E(name):
            o, w = EC[name]
            return Ee[:, o:o + w], ("E", o, o + w)

        def eN(name, nb, off=0):
            o, w = EC[name]
            assert off + nb <= w
            return Ee[:, o + off:o + off + nb], ("E", o + off, o + off + nb)

        for nm in ["amax", "rr", "pA", "pB", "pw", "csb", "cge", "ff",
                   "psm", "pinv", "zmul", "qmul"]:
            edef(nm, 256)
        for nm in ["cmag", "det_a", "det_b", "det_c", "kcn", "kcs", "keps",
                   "k2p30", "kc26", "kcm26", "kzA", "kqA", "kzW", "kqW",
                   "kclo", "kchi", "k2pinv", "kpihalf", "kpiclp", "kpicln",
                   "kan", "keps_ln", "bias_sm", "rowmax", "sum_a", "sum_d",
                   "sum_t", "inv_s", "mean", "msc", "m2sq", "var_s", "var1",
                   "var2", "var3", "sqv", "rstd", "sgn"]:
            edef(nm, 1)
        for w in LADDER:
            edef(f"kp2_{w}", 1)
            edef(f"kp2m1_{w}", 1)
        edef("bq", 20)
        edef("rmp", 4)

        idf, idf_k = Dx(0, 128)
        swf, swf_k = Dx(128, 128)
        maskf, maskf_k = Dx(256, 128)
        fqrow = Dd[0:1, 384:512]
        fqrow_k = ("D", 384, 512)
        invf = Dd[0:1, 512:640]
        invf_k = ("D", 512, 640)
        posrow = Dd[0:1, 640:640 + T]
        posrow_k = ("D", 640, 640 + T)
        identb, identb_k = Hx(0, 128)
        maskb, maskb_k = Hx(128, 128)

        op("dS", lambda: nc.sync.dma_start(out=idf, in_=ident_h[:]), writes=[idf_k])
        op("dS", lambda: nc.sync.dma_start(out=swf, in_=swap_h[:]), writes=[swf_k])
        op("dS", lambda: nc.sync.dma_start(out=maskf, in_=mask_h[:]), writes=[maskf_k])
        op("dS", lambda: nc.sync.dma_start(
            out=E("bq")[0], in_=bqkv_h.rearrange("(t p) -> p t", p=128),
            allow_slow_non_contiguous=True), writes=[E("bq")[1]])
        op("dS", lambda: nc.sync.dma_start(
            out=E("sgn")[0], in_=sgn_h.rearrange("(p o) -> p o", o=1),
            allow_slow_non_contiguous=True), writes=[E("sgn")[1]])
        op("dS", lambda: nc.sync.dma_start(out=fqrow, in_=freqmod_h[:]), writes=[fqrow_k])
        op("dS", lambda: nc.sync.dma_start(out=posrow, in_=pos_h[:]), writes=[posrow_k])

        op("v", lambda: nc.vector.tensor_copy(out=identb, in_=idf),
           reads=[idf_k], writes=[identb_k])
        op("v", lambda: nc.vector.tensor_copy(out=maskb, in_=maskf),
           reads=[maskf_k], writes=[maskb_k])

        def kmem(name, val):
            ap, key = E(name)
            op("v", lambda ap=ap, val=val: nc.vector.memset(ap, float(val)),
               writes=[key])

        kmem("kcn", 1.5 * 2.0 ** 23)
        kmem("kcs", 1.5 * 2.0 ** 52)
        kmem("keps", 2.0 ** -30)
        kmem("k2p30", 2.0 ** 30)
        kmem("kc26", 2.0 ** 26)
        kmem("kcm26", 2.0 ** -26)
        kmem("kzA", 2.0 ** (QA[0] + QA[1]))
        kmem("kqA", 2.0 ** -(QA[0] + QA[1]))
        kmem("kzW", 2.0 ** (QW[0] + QW[1]))
        kmem("kqW", 2.0 ** -(QW[0] + QW[1]))
        kmem("kclo", CLIP_LO)
        kmem("kchi", CLIP_HI)
        kmem("k2pinv", 1.0 / (2.0 * math.pi))
        kmem("kpihalf", math.pi / 2.0)
        kmem("kpiclp", 3.1415925)
        kmem("kpicln", -3.1415925)
        kmem("kan", -ALPHA)
        kmem("keps_ln", LN_EPS)
        for w in LADDER:
            kmem(f"kp2_{w}", 2.0 ** w)
            kmem(f"kp2m1_{w}", 2.0 ** w - 1.0)

        det_a, det_a_k = E("det_a")
        det_b, det_b_k = E("det_b")
        det_c, det_c_k = E("det_c")
        cmag, cmag_k = E("cmag")
        op("v", lambda: nc.vector.tensor_scalar(
            out=det_a, in0=Dd[:, 0:1], scalar1=float(-3.0e38), scalar2=None,
            op0=ALU.is_ge), reads=[("D", 0, 1)], writes=[det_a_k])
        op("v", lambda: nc.vector.tensor_tensor(
            out=det_b, in0=det_a, in1=E("keps")[0], op=ALU.mult),
            reads=[det_a_k, E("keps")[1]], writes=[det_b_k])
        op("v", lambda: nc.vector.tensor_tensor(
            out=det_c, in0=det_a, in1=det_b, op=ALU.add),
            reads=[det_a_k, det_b_k], writes=[det_c_k])
        op("v", lambda: nc.vector.tensor_tensor(
            out=det_b, in0=det_c, in1=det_a, op=ALU.subtract),
            reads=[det_c_k, det_a_k], writes=[det_b_k])
        op("v", lambda: nc.vector.tensor_tensor(
            out=det_c, in0=det_b, in1=E("k2p30")[0], op=ALU.mult),
            reads=[det_b_k, E("k2p30")[1]], writes=[det_c_k])
        op("v", lambda: nc.vector.tensor_tensor(
            out=det_a, in0=det_c, in1=E("kcs")[0], op=ALU.mult),
            reads=[det_c_k, E("kcs")[1]], writes=[det_a_k])
        op("v", lambda: nc.vector.tensor_scalar(
            out=det_b, in0=det_c, scalar1=-1.0, scalar2=1.0,
            op0=ALU.mult, op1=ALU.add), reads=[det_c_k], writes=[det_b_k])
        op("v", lambda: nc.vector.tensor_tensor(
            out=det_c, in0=det_b, in1=E("kcn")[0], op=ALU.mult),
            reads=[det_b_k, E("kcn")[1]], writes=[det_c_k])
        op("v", lambda: nc.vector.tensor_tensor(
            out=cmag, in0=det_a, in1=det_c, op=ALU.add),
            reads=[det_a_k, det_c_k], writes=[cmag_k])

        def rne(out_ap, in_ap, rk, wk, st="v"):
            op(st, lambda: nc.vector.tensor_scalar(
                out=out_ap, in0=in_ap, scalar1=cmag[:, 0:1], scalar2=cmag[:, 0:1],
                op0=ALU.add, op1=ALU.subtract),
                reads=list(rk) + [cmag_k], writes=list(wk))

        op("s", lambda: nc.scalar.activation(
            out=invf, in_=fqrow, func=ACT.Exp, bias=0.0,
            scale=-math.log(10000.0) / 64.0),
            reads=[fqrow_k], writes=[invf_k])
        angA, angA_k = Dx(2688, 1024)
        angB, angB_k = Dx(3712, 1024)
        angC, angC_k = Dx(4736, 1024)
        scrX, scrX_k = Dx(5760, 1024)
        for hh in range(2):
            base = hh * 1024
            for c in range(2):
                op("t", lambda c=c, base=base: nc.tensor.matmul(
                    ps_c[:, 0:512], invf,
                    posrow[:, base + c * 512:base + (c + 1) * 512],
                    start=True, stop=True),
                    reads=[invf_k, posrow_k], writes=[("ps_c", 0, 512)])
                op("v", lambda c=c: nc.vector.tensor_copy(
                    out=angA[:, c * 512:(c + 1) * 512], in_=ps_c[:, 0:512]),
                    reads=[("ps_c", 0, 512)], writes=[angA_k])
            for (use_shift, dst_lo) in [(True, 0), (False, T)]:
                if use_shift:
                    op("v", lambda: nc.vector.tensor_scalar(
                        out=angB, in0=angA, scalar1=E("kpihalf")[0][:, 0:1],
                        scalar2=None, op0=ALU.add),
                        reads=[angA_k, E("kpihalf")[1]], writes=[angB_k])
                else:
                    op("v", lambda: nc.vector.tensor_copy(out=angB, in_=angA),
                       reads=[angA_k], writes=[angB_k])
                for _ in range(2):
                    op("v", lambda: nc.vector.tensor_scalar(
                        out=angC, in0=angB, scalar1=E("k2pinv")[0][:, 0:1],
                        scalar2=None, op0=ALU.mult),
                        reads=[angB_k, E("k2pinv")[1]], writes=[angC_k])
                    rne(angC, angC, [angC_k], [angC_k])
                    op("v", lambda: nc.vector.scalar_tensor_tensor(
                        out=scrX, in0=angC, scalar=-PI2_HI, in1=angB,
                        op0=ALU.mult, op1=ALU.add),
                        reads=[angC_k, angB_k], writes=[scrX_k])
                    op("v", lambda: nc.vector.scalar_tensor_tensor(
                        out=angB, in0=angC, scalar=-PI2_LO, in1=scrX,
                        op0=ALU.mult, op1=ALU.add),
                        reads=[angC_k, scrX_k], writes=[angB_k])
                op("v", lambda: nc.vector.tensor_scalar(
                    out=angC, in0=angB, scalar1=E("kpiclp")[0][:, 0:1],
                    scalar2=E("kpicln")[0][:, 0:1], op0=ALU.min, op1=ALU.max),
                    reads=[angB_k, E("kpiclp")[1], E("kpicln")[1]], writes=[angC_k])
                if not use_shift:
                    op("s", lambda base=base, dst_lo=dst_lo: nc.scalar.activation(
                        out=scrX, in_=angC, func=ACT.Sin, bias=0.0, scale=1.0),
                        reads=[angC_k], writes=[scrX_k])
                    op("v", lambda base=base, dst_lo=dst_lo: nc.vector.tensor_scalar(
                        out=CS2[:, dst_lo + base:dst_lo + base + 1024], in0=scrX,
                        scalar1=E("sgn")[0][:, 0:1], scalar2=None, op0=ALU.mult),
                        reads=[scrX_k, E("sgn")[1]],
                        writes=[("G", 4096 + dst_lo + base, 4096 + dst_lo + base + 1024)])
                else:
                    op("s", lambda base=base, dst_lo=dst_lo: nc.scalar.activation(
                        out=CS2[:, dst_lo + base:dst_lo + base + 1024], in_=angC,
                        func=ACT.Sin, bias=0.0, scale=1.0),
                        reads=[angC_k],
                        writes=[("G", 4096 + dst_lo + base, 4096 + dst_lo + base + 1024)])
        C2 = CS2[:, 0:T]
        C2_k = ("G", 4096, 4096 + T)
        S2 = CS2[:, T:2 * T]
        S2_k = ("G", 4096 + T, 4096 + 2 * T)

        def q_amax(src_ap, src_keys, nb, off):
            amax, amax_k = eN("amax", nb, off)
            op("v", lambda: nc.vector.tensor_reduce(
                out=amax, in_=src_ap.rearrange("p (b k) -> p b k", k=32),
                axis=AXL.X, op=ALU.max, apply_absolute_value=True),
                reads=list(src_keys), writes=[amax_k])

        def q_ladder(nb, params, off=0, lst="v", qmb=None):
            mfrac, emax, mn = params
            le = nc.gpsimd if lst == "g" else nc.vector
            kz = "kzA" if mfrac == QA[0] else "kzW"
            kq = "kqA" if mfrac == QA[0] else "kqW"
            amax, amax_k = eN("amax", nb, off)
            rr, rr_k = eN("rr", nb, off)
            pw, pw_k = eN("pw", nb, off)
            csb, csb_k = eN("csb", nb, off)
            cge, cge_k = eN("cge", nb, off)
            ffb, ffb_k = eN("ff", nb, off)
            psm, psm_k = eN("psm", nb, off)
            pinv, pinv_k = eN("pinv", nb, off)
            zmul, zmul_k = eN("zmul", nb, off)
            qmul, qmul_k = eN("qmul", nb, off)
            pa, pa_k = eN("pA", nb, off)
            pb, pb_k = eN("pB", nb, off)
            op(lst, lambda: le.tensor_scalar(
                out=csb, in0=amax, scalar1=E("kchi")[0][:, 0:1],
                scalar2=E("kclo")[0][:, 0:1], op0=ALU.min, op1=ALU.max),
                reads=[amax_k, E("kchi")[1], E("kclo")[1]], writes=[csb_k])
            op(lst, lambda: le.tensor_scalar(
                out=rr, in0=csb, scalar1=E("kc26")[0][:, 0:1], scalar2=None,
                op0=ALU.mult), reads=[csb_k, E("kc26")[1]], writes=[rr_k])
            cur, cur_k, nxt, nxt_k = pa, pa_k, pb, pb_k
            first = True
            for w in LADDER:
                if first:
                    op(lst, lambda w=w: le.tensor_scalar(
                        out=cge, in0=rr, scalar1=E(f"kp2_{w}")[0][:, 0:1],
                        scalar2=None, op0=ALU.is_ge),
                        reads=[rr_k, E(f"kp2_{w}")[1]], writes=[cge_k])
                else:
                    op(lst, lambda cur=cur, w=w: le.tensor_scalar(
                        out=pw, in0=cur, scalar1=E(f"kp2_{w}")[0][:, 0:1],
                        scalar2=None, op0=ALU.mult),
                        reads=[cur_k, E(f"kp2_{w}")[1]], writes=[pw_k])
                    op(lst, lambda: le.tensor_tensor(
                        out=cge, in0=rr, in1=pw, op=ALU.is_ge),
                        reads=[rr_k, pw_k], writes=[cge_k])
                if w >= 32:
                    op(lst, lambda w=w: le.tensor_scalar(
                        out=csb, in0=cge, scalar1=E(f"kp2_{w}")[0][:, 0:1],
                        scalar2=None, op0=ALU.mult),
                        reads=[cge_k, E(f"kp2_{w}")[1]], writes=[csb_k])
                    op(lst, lambda: le.tensor_scalar(
                        out=pw, in0=cge, scalar1=-1.0, scalar2=1.0,
                        op0=ALU.mult, op1=ALU.add), reads=[cge_k], writes=[pw_k])
                    op(lst, lambda: le.tensor_tensor(
                        out=ffb, in0=csb, in1=pw, op=ALU.add),
                        reads=[csb_k, pw_k], writes=[ffb_k])
                else:
                    op(lst, lambda w=w: le.tensor_scalar(
                        out=ffb, in0=cge, scalar1=E(f"kp2m1_{w}")[0][:, 0:1],
                        scalar2=1.0, op0=ALU.mult, op1=ALU.add),
                        reads=[cge_k, E(f"kp2m1_{w}")[1]], writes=[ffb_k])
                if first:
                    op(lst, lambda cur=cur: le.tensor_copy(
                        out=cur, in_=ffb), reads=[ffb_k], writes=[cur_k])
                    first = False
                else:
                    op(lst, lambda cur=cur, nxt=nxt: le.tensor_tensor(
                        out=nxt, in0=cur, in1=ffb, op=ALU.mult),
                        reads=[cur_k, ffb_k], writes=[nxt_k])
                    cur, cur_k, nxt, nxt_k = nxt, nxt_k, cur, cur_k
            op(lst, lambda cur=cur: le.tensor_scalar(
                out=psm, in0=cur, scalar1=E("kcm26")[0][:, 0:1], scalar2=None,
                op0=ALU.mult), reads=[cur_k, E("kcm26")[1]], writes=[psm_k])
            op("v", lambda: nc.vector.reciprocal(out=pinv, in_=psm),
               reads=[psm_k], writes=[pinv_k])
            op("v", lambda: nc.vector.tensor_scalar(
                out=zmul, in0=pinv, scalar1=E(kz)[0][:, 0:1], scalar2=None,
                op0=ALU.mult), reads=[pinv_k, E(kz)[1]], writes=[zmul_k])
            if qmb is not None:
                qba = qmb[0][:, off:off + nb]
                qbk = (qmb[1], off, off + nb)
                op(lst, lambda: le.tensor_scalar(
                    out=qba, in0=psm, scalar1=E(kq)[0][:, 0:1], scalar2=None,
                    op0=ALU.mult), reads=[psm_k, E(kq)[1]], writes=[qbk])
            else:
                op(lst, lambda: le.tensor_scalar(
                    out=qmul, in0=psm, scalar1=E(kq)[0][:, 0:1], scalar2=None,
                    op0=ALU.mult), reads=[psm_k, E(kq)[1]], writes=[qmul_k])

        def q_scales(src_ap, src_keys, nb, params, qmb=None):
            mfrac, emax, mn = params
            q_amax(src_ap, src_keys, nb, 0)
            q_ladder(nb, params, qmb=qmb)

        def q_elem_steps(src_ap, src_keys, dst_ap, dst_key, ncol, params, scr,
                         nb_off=0):
            steps = []
            add = steps.append
            mfrac, emax, mn = params
            nb = ncol // 32
            zz, zz_k = scr["zz"]
            yy, yy_k = scr["yy"]
            c1, c1_k = scr["c1"]
            c2, c2_k = scr["c2"]
            zmul, zmul_k = eN("zmul", nb, nb_off)
            qmb = scr.get("qmb")
            if qmb is not None:
                qmul = qmb[0][:, nb_off:nb_off + nb]
                qmul_k = (qmb[1], nb_off, nb_off + nb)
            else:
                qmul, qmul_k = eN("qmul", nb, nb_off)
            th1 = float(2.0 ** (mfrac + 1))
            th2 = float(2.0 ** (mfrac + 2))
            mnz = float(mn * 2.0 ** mfrac)
            add(lambda: op("v", lambda: nc.vector.tensor_tensor(
                out=zz.rearrange("p (b k) -> p b k", k=32),
                in0=src_ap.rearrange("p (b k) -> p b k", k=32),
                in1=zmul.unsqueeze(2).broadcast_to([128, nb, 32]), op=ALU.mult),
                reads=list(src_keys) + [zmul_k], writes=[zz_k]))
            add(lambda: op("v", lambda: nc.vector.tensor_tensor(
                out=yy, in0=zz, in1=zz, op=ALU.mult),
                reads=[zz_k], writes=[yy_k]))
            add(lambda: op("v", lambda: nc.vector.tensor_scalar(
                out=c1, in0=yy, scalar1=th1 * th1, scalar2=None,
                op0=ALU.is_ge), reads=[yy_k], writes=[c1_k]))
            add(lambda: op("v", lambda: nc.vector.tensor_scalar(
                out=c2, in0=yy, scalar1=th2 * th2, scalar2=None,
                op0=ALU.is_ge), reads=[yy_k], writes=[c2_k]))
            add(lambda: op("v", lambda: nc.vector.tensor_scalar(
                out=c2, in0=c2, scalar1=cmag[:, 0:1], scalar2=cmag[:, 0:1],
                op0=ALU.mult, op1=ALU.add),
                reads=[c2_k, cmag_k], writes=[c2_k]))
            add(lambda: op("v", lambda: nc.vector.scalar_tensor_tensor(
                out=c1, in0=c1, scalar=1.0, in1=c2, op0=ALU.add, op1=ALU.mult),
                reads=[c1_k, c2_k], writes=[c1_k]))
            add(lambda: op("v", lambda: nc.vector.tensor_tensor(
                out=yy, in0=zz, in1=c1, op=ALU.add),
                reads=[zz_k, c1_k], writes=[yy_k]))
            add(lambda: op("v", lambda: nc.vector.tensor_tensor(
                out=c1, in0=yy, in1=c1, op=ALU.subtract),
                reads=[yy_k, c1_k], writes=[c1_k]))
            add(lambda: op("v", lambda: nc.vector.tensor_scalar(
                out=c1, in0=c1, scalar1=mnz, scalar2=-mnz, op0=ALU.min,
                op1=ALU.max), reads=[c1_k], writes=[c1_k]))
            add(lambda: op("v", lambda: nc.vector.tensor_tensor(
                out=dst_ap.rearrange("p (b k) -> p b k", k=32),
                in0=c1.rearrange("p (b k) -> p b k", k=32),
                in1=qmul.unsqueeze(2).broadcast_to([128, nb, 32]), op=ALU.mult),
                reads=[c1_k, qmul_k], writes=[dst_key]))
            return steps

        def zipper(chains):
            i = 0
            while any(chains):
                for ch in chains:
                    if i < len(ch):
                        ch[i]()
                i += 1
                if all(i >= len(ch) for ch in chains):
                    break

        def q_elem(src_ap, src_keys, dst_ap, dst_key, ncol, params, scr, nb_off=0):
            for st_ in q_elem_steps(src_ap, src_keys, dst_ap, dst_key, ncol,
                                    params, scr, nb_off):
                st_()

        def transpose8(src_ap_fn, src_keys, dst_ap, dst_key, n, ev="v"):
            for j in range(n):
                op("t", lambda j=j: nc.tensor.transpose(
                    ps_tr[:, j * 128:(j + 1) * 128], src_ap_fn(j), identb),
                    reads=list(src_keys) + [identb_k],
                    writes=[("ps_tr", j * 128, (j + 1) * 128)])
            if ev == "v":
                op("v", lambda: nc.vector.tensor_copy(
                    out=dst_ap, in_=ps_tr[:, 0:n * 128]),
                    reads=[("ps_tr", 0, n * 128)], writes=[dst_key])
            else:
                op("s", lambda: nc.scalar.copy(
                    out=dst_ap, in_=ps_tr[:, 0:n * 128]),
                    reads=[("ps_tr", 0, n * 128)], writes=[dst_key])

        em.barrier()
        xrow, xrow_k = Dx(2688, D)
        xrow_bf, xrow_bf_k = Hx(256, D)
        for tt in range(NT):
            op("dG", lambda tt=tt: nc.gpsimd.dma_start(
                out=xrow, in_=x_h[tt * 128:(tt + 1) * 128, :]), writes=[xrow_k])
            op("v", lambda: nc.vector.tensor_copy(out=xrow_bf, in_=xrow),
               reads=[xrow_k], writes=[xrow_bf_k])
            for g in range(2):
                dst = Aa[:].rearrange("p (dt c) -> p dt c", c=T)[
                    :, g * 8:(g + 1) * 8, tt * 128:(tt + 1) * 128]
                transpose8(
                    lambda j, g=g: xrow_bf[:, (g * 8 + j) * 128:(g * 8 + j + 1) * 128],
                    [xrow_bf_k], dst, ("A", g * 8 * T, (g * 8 + 8) * T), 8)

        NOUT = HQ + 2 * HKV
        wrawA, wrawA_k = Dx(640, D)
        wrawB, wrawB_k = Dx(2688, D)
        qraw, qraw_k = Dx(4736, D)
        wbfA, wbfA_k = Fx(0, D)
        wbfB, wbfB_k = Fx(2048, D)
        t1b, t1b_k = Fx(4096, D)
        vtmp, vtmp_k = Hx(256, 512)
        for ot in range(NOUT):
            wr, wr_k, wb, wb_k = ((wrawA, wrawA_k, wbfA, wbfA_k) if ot % 2 == 0
                                  else (wrawB, wrawB_k, wbfB, wbfB_k))
            src = wqkvT_h[:, ot * 128:(ot + 1) * 128].rearrange(
                "(a p) o -> p a o", p=128)
            op("dS", lambda src=src, wr=wr: nc.sync.dma_start(
                out=wr.rearrange("p (a o) -> p a o", o=128), in_=src),
                writes=[wr_k])
            op("v", lambda wb=wb, wr=wr: nc.vector.tensor_copy(out=wb, in_=wr),
               reads=[wr_k], writes=[wb_k])
            for ch in range(4):
                for dt in range(ND):
                    op("t", lambda ch=ch, dt=dt, wb=wb: nc.tensor.matmul(
                        ps_big[:, ch * 512:(ch + 1) * 512],
                        wb[:, dt * 128:(dt + 1) * 128],
                        Aa[:, dt * T + ch * 512:dt * T + (ch + 1) * 512],
                        start=(dt == 0), stop=(dt == ND - 1)),
                        reads=[wb_k, ("A", dt * T + ch * 512, dt * T + (ch + 1) * 512)],
                        writes=[("ps_big", ch * 512, (ch + 1) * 512)], acc=True)
            if ot < HQ + HKV:
                for ch in range(4):
                    qck = ("D", 4736 + ch * 512, 4736 + (ch + 1) * 512)
                    op("s", lambda ch=ch, ot=ot: nc.scalar.activation(
                        out=qraw[:, ch * 512:(ch + 1) * 512],
                        in_=ps_big[:, ch * 512:(ch + 1) * 512],
                        func=ACT.Identity, bias=E("bq")[0][:, ot:ot + 1], scale=1.0),
                        reads=[("ps_big", ch * 512, (ch + 1) * 512), E("bq")[1]],
                        writes=[qck])
                for ch in range(4):
                    qck = ("D", 4736 + ch * 512, 4736 + (ch + 1) * 512)
                    op("t", lambda ch=ch: nc.tensor.matmul(
                        ps_c[:, 0:512], swf, qraw[:, ch * 512:(ch + 1) * 512],
                        start=True, stop=True),
                        reads=[swf_k, qck], writes=[("ps_c", 0, 512)])
                    op("v", lambda ch=ch: nc.vector.tensor_tensor(
                        out=t1b[:, ch * 512:(ch + 1) * 512],
                        in0=qraw[:, ch * 512:(ch + 1) * 512],
                        in1=C2[:, ch * 512:(ch + 1) * 512], op=ALU.mult),
                        reads=[qck, C2_k],
                        writes=[("F", 4096 + ch * 512, 4096 + (ch + 1) * 512)])
                    op("v", lambda ch=ch: nc.vector.tensor_tensor(
                        out=qraw[:, ch * 512:(ch + 1) * 512], in0=ps_c[:, 0:512],
                        in1=S2[:, ch * 512:(ch + 1) * 512], op=ALU.mult),
                        reads=[("ps_c", 0, 512), S2_k], writes=[qck])
                    if ot < HQ:
                        dst = Bb[:, ot * T + ch * 512:ot * T + (ch + 1) * 512]
                        dk = ("B", ot * T + ch * 512, ot * T + (ch + 1) * 512)
                    else:
                        kvo = (ot - HQ) * T
                        dst = Gg[:, kvo + ch * 512:kvo + (ch + 1) * 512]
                        dk = ("G", kvo + ch * 512, kvo + (ch + 1) * 512)
                    op("v", lambda dst=dst, ch=ch: nc.vector.tensor_tensor(
                        out=dst, in0=t1b[:, ch * 512:(ch + 1) * 512],
                        in1=qraw[:, ch * 512:(ch + 1) * 512], op=ALU.add),
                        reads=[("F", 4096 + ch * 512, 4096 + (ch + 1) * 512), qck],
                        writes=[dk])
            else:
                kv = ot - HQ - HKV
                for ch in range(4):
                    op("s", lambda ch=ch, ot=ot: nc.scalar.activation(
                        out=vtmp, in_=ps_big[:, ch * 512:(ch + 1) * 512],
                        func=ACT.Identity, bias=E("bq")[0][:, ot:ot + 1], scale=1.0),
                        reads=[("ps_big", ch * 512, (ch + 1) * 512), E("bq")[1]],
                        writes=[vtmp_k])
                    base = 4096 + (kv * NT + ch * 4) * 128
                    transpose8(
                        lambda j: vtmp[:, j * 128:(j + 1) * 128],
                        [vtmp_k], Gg[:, base:base + 512],
                        ("G", base, base + 512), 4)

        rawI = [Dx(2688, 2048), Dx(4736, 2048)]
        qmbA = Hx(1920, 256)
        scrI2 = [{
            "yy": Dx(640, 1024), "qmb": qmbA,
            "c1": Fx(4096, 1024), "c2": Fx(5120, 1024),
        }, {
            "yy": Dx(1664, 1024), "qmb": qmbA,
            "c1": Fx(6144, 1024), "c2": Fx(7168, 1024),
        }]
        wqnI, wqnI_k = Hx(2304, 2048)
        stageI, stageI_k = Hx(896, 1024)
        wq_parity = [0]

        def wq_pair_gen(kind, ic_or_et, pair):
            w_h = [wg_h, wu_h, wd_h][kind]
            eoff = 128 * wq_parity[0]
            wq_parity[0] ^= 1
            for k in range(2):
                wr, wr_k = rawI[k]
                if kind < 2:
                    ck = pair * 2 + k
                    r0 = ic_or_et * ICS + ck * 128
                    op("dS", lambda w_h=w_h, r0=r0, wr=wr: nc.sync.dma_start(
                        out=wr, in_=w_h[r0:r0 + 128, :]), writes=[wr_k])
                else:
                    cp = pair * 2 + k
                    op("dS", lambda w_h=w_h, et=ic_or_et, cp=cp, wr=wr:
                       nc.sync.dma_start(
                           out=wr, in_=w_h[et * 128:(et + 1) * 128,
                                           cp * 2048:(cp + 1) * 2048]),
                       writes=[wr_k])
                q_amax(rawI[k][0], [rawI[k][1]], 64, eoff + k * 64)
            yield
            q_ladder(128, QW, off=eoff, qmb=qmbA)
            yield
            for k in range(2):
                wr, wr_k = rawI[k]
                chains = []
                for s in range(2):
                    lo = s * 1024
                    raw_sl = (wr[:, lo:lo + 1024],
                              (wr_k[0], wr_k[1] + lo, wr_k[1] + lo + 1024))
                    scr = dict(scrI2[s])
                    scr["zz"] = raw_sl
                    chains.append(q_elem_steps(
                        raw_sl[0], [raw_sl[1]],
                        wqnI[:, lo:lo + 1024],
                        (wqnI_k[0], wqnI_k[1] + lo, wqnI_k[1] + lo + 1024),
                        1024, QW, scr,
                        nb_off=eoff + k * 64 + s * 32))
                zipper(chains)
                yield
                for g in range(2):
                    transpose8(
                        lambda j, g=g: wqnI[:, (g * 8 + j) * 128:(g * 8 + j + 1) * 128],
                        [(wqnI_k[0], wqnI_k[1] + g * 1024, wqnI_k[1] + (g + 1) * 1024)],
                        stageI, stageI_k, 8, ev="s")
                    if kind < 2:
                        ck = pair * 2 + k
                        s_wT = s_wgT if kind == 0 else s_wuT
                        dst = s_wT[ic_or_et, g * 8:(g + 1) * 8, :,
                                   ck * 128:(ck + 1) * 128].rearrange(
                                       "dt p c -> p dt c")
                        hk = (f"hw{kind}", ic_or_et, ic_or_et + 1)
                    else:
                        cp = pair * 2 + k
                        dst = s_wdT[ic_or_et, cp * 16 + g * 8:cp * 16 + (g + 1) * 8,
                                    :, :].rearrange("it p c -> p it c")
                        hk = ("hwd", ic_or_et, ic_or_et + 1)
                    op("dS", lambda dst=dst: nc.sync.dma_start(
                        out=dst, in_=stageI.rearrange("p (n c) -> p n c", c=128)),
                        reads=[stageI_k], writes=[hk])
                    yield

        wq_tasks = []
        for kind in range(3):
            for i in range(16):
                for pair in range(2):
                    wq_tasks.append((kind, i, pair))
        wq_tasks.sort(key=lambda t: (t[1], t[2], t[0]))

        def _wq_units():
            for kind, i, pair in wq_tasks:
                for _ in wq_pair_gen(kind, i, pair):
                    yield
        wq_gen = _wq_units()
        wq_done = [False]

        def emit_wq_task(n=1):
            for _ in range(n):
                try:
                    next(wq_gen)
                except StopIteration:
                    wq_done[0] = True
                    return

        em.barrier()
        Pbuf, Pbuf_k = Fx(0, T)
        Pd, Pd_k = Hx(768, 128)
        Pn, Pn_k = Fx(2048, T)
        for h in range(HQ):
            kv = h // (HQ // HKV)
            qkb = ("B", h * T + 0, h * T + T)
            for qt in range(NT):
                emit_wq_task(3)
                L = (qt + 1) * 128
                nch = (L + 511) // 512
                for c in range(nch):
                    c0, c1_ = c * 512, min((c + 1) * 512, L)
                    op("t", lambda h=h, qt=qt, kv=kv, c0=c0, c1_=c1_:
                       nc.tensor.matmul(
                           ps_big[:, c0:c1_],
                           Bb[:, h * T + qt * 128:h * T + (qt + 1) * 128],
                           Gg[:, kv * T + c0:kv * T + c1_], start=True, stop=True),
                       reads=[qkb, ("G", kv * T + c0, kv * T + c1_)],
                       writes=[("ps_big", c0, c1_)])
                    op("v", lambda c=c, c0=c0, c1_=c1_: nc.vector.tensor_reduce(
                        out=eN("rmp", 1, c)[0], in_=ps_big[:, c0:c1_],
                        axis=AXL.X, op=ALU.max),
                        reads=[("ps_big", c0, c1_)], writes=[eN("rmp", 1, c)[1]])
                op("v", lambda nch=nch: nc.vector.tensor_reduce(
                    out=E("rowmax")[0], in_=eN("rmp", nch)[0], axis=AXL.X,
                    op=ALU.max),
                    reads=[eN("rmp", nch)[1]], writes=[E("rowmax")[1]])
                op("v", lambda: nc.vector.tensor_scalar(
                    out=E("bias_sm")[0], in0=E("rowmax")[0],
                    scalar1=E("kan")[0][:, 0:1], scalar2=None, op0=ALU.mult),
                    reads=[E("rowmax")[1], E("kan")[1]], writes=[E("bias_sm")[1]])
                if qt > 0:
                    op("s", lambda qt=qt: nc.scalar.activation(
                        out=Pbuf[:, 0:qt * 128], in_=ps_big[:, 0:qt * 128],
                        func=ACT.Exp, bias=E("bias_sm")[0][:, 0:1], scale=ALPHA,
                        accum_out=E("sum_a")[0]),
                        reads=[("ps_big", 0, qt * 128), E("bias_sm")[1]],
                        writes=[("F", 0, qt * 128), E("sum_a")[1]])
                op("s", lambda qt=qt, L=L: nc.scalar.activation(
                    out=Pd, in_=ps_big[:, qt * 128:L], func=ACT.Exp,
                    bias=E("bias_sm")[0][:, 0:1], scale=ALPHA),
                    reads=[("ps_big", qt * 128, L), E("bias_sm")[1]], writes=[Pd_k])
                op("v", lambda qt=qt, L=L: nc.vector.tensor_tensor(
                    out=Pbuf[:, qt * 128:L], in0=Pd, in1=maskb, op=ALU.mult),
                    reads=[Pd_k, maskb_k], writes=[("F", qt * 128, L)])
                op("v", lambda qt=qt, L=L: nc.vector.tensor_reduce(
                    out=E("sum_d")[0], in_=Pbuf[:, qt * 128:L], axis=AXL.X,
                    op=ALU.add),
                    reads=[("F", qt * 128, L)], writes=[E("sum_d")[1]])
                if qt > 0:
                    op("v", lambda: nc.vector.tensor_tensor(
                        out=E("sum_t")[0], in0=E("sum_a")[0], in1=E("sum_d")[0],
                        op=ALU.add),
                        reads=[E("sum_a")[1], E("sum_d")[1]], writes=[E("sum_t")[1]])
                else:
                    op("v", lambda: nc.vector.tensor_copy(
                        out=E("sum_t")[0], in_=E("sum_d")[0]),
                        reads=[E("sum_d")[1]], writes=[E("sum_t")[1]])
                op("v", lambda: nc.vector.reciprocal(
                    out=E("inv_s")[0], in_=E("sum_t")[0]),
                    reads=[E("sum_t")[1]], writes=[E("inv_s")[1]])
                op("v", lambda L=L: nc.vector.tensor_scalar(
                    out=Pn[:, 0:L], in0=Pbuf[:, 0:L],
                    scalar1=E("inv_s")[0][:, 0:1], scalar2=None, op0=ALU.mult),
                    reads=[("F", 0, L), E("inv_s")[1]],
                    writes=[("F", 2048, 2048 + L)])
                nkt = qt + 1
                for g in range((nkt + 7) // 8):
                    jn = min(8, nkt - g * 8)
                    transpose8(
                        lambda j, g=g: Pn[:, (g * 8 + j) * 128:(g * 8 + j + 1) * 128],
                        [("F", 2048 + g * 1024, 2048 + g * 1024 + jn * 128)],
                        Pbuf[:, g * 1024:g * 1024 + jn * 128],
                        ("F", g * 1024, g * 1024 + jn * 128), jn, ev="s")
                for kt in range(nkt):
                    vb = 4096 + (kv * NT + kt) * 128
                    op("t", lambda vb=vb, kt=kt, qt=qt: nc.tensor.matmul(
                        ps_a[:, 0:128], Gg[:, vb:vb + 128],
                        Pbuf[:, kt * 128:(kt + 1) * 128],
                        start=(kt == 0), stop=(kt == qt)),
                        reads=[("G", vb, vb + 128),
                               ("F", kt * 128, (kt + 1) * 128)],
                        writes=[("ps_a", 0, 128)], acc=True)
                op("s", lambda h=h, qt=qt: nc.scalar.copy(
                    out=Aa[:, h * T + qt * 128:h * T + (qt + 1) * 128],
                    in_=ps_a[:, 0:128]),
                    reads=[("ps_a", 0, 128)],
                    writes=[("A", h * T + qt * 128, h * T + (qt + 1) * 128)])

        em.barrier()
        wtmp, wtmp_k = Dx(0, 512)
        x1row, x1row_k = Hx(256 + D, D)
        x1q, x1q_k = Gx(0, D)
        xqs, xqs_k = Gx(D, 2048)
        rrow, rrow_k = Dx(640, D)
        srow, srow_k = Dx(2688, D)
        scrP5 = {
            "zz": Dx(4736, 2048), "yy": Dx(2688, 2048), "qmb": qmbA,
            "c1": Fx(0, 2048), "c2": Fx(2048, 2048),
        }

        def ln_stats(rr_ap, rr_key):
            op("v", lambda: nc.vector.tensor_reduce(
                out=E("mean")[0], in_=rr_ap, axis=AXL.X, op=ALU.add),
                reads=[rr_key], writes=[E("mean")[1]])
            op("v", lambda: nc.vector.tensor_scalar(
                out=E("msc")[0], in0=E("mean")[0], scalar1=1.0 / D, scalar2=None,
                op0=ALU.mult), reads=[E("mean")[1]], writes=[E("msc")[1]])
            op("s", lambda: nc.scalar.activation(
                out=srow, in_=rr_ap, func=ACT.Square, bias=0.0, scale=1.0,
                accum_out=E("var_s")[0]),
                reads=[rr_key], writes=[srow_k, E("var_s")[1]])
            op("v", lambda: nc.vector.tensor_tensor(
                out=E("m2sq")[0], in0=E("msc")[0], in1=E("msc")[0], op=ALU.mult),
                reads=[E("msc")[1]], writes=[E("m2sq")[1]])
            op("v", lambda: nc.vector.tensor_scalar(
                out=E("var1")[0], in0=E("var_s")[0], scalar1=1.0 / D,
                scalar2=None, op0=ALU.mult),
                reads=[E("var_s")[1]], writes=[E("var1")[1]])
            op("v", lambda: nc.vector.tensor_tensor(
                out=E("var2")[0], in0=E("var1")[0], in1=E("m2sq")[0],
                op=ALU.subtract),
                reads=[E("var1")[1], E("m2sq")[1]], writes=[E("var2")[1]])
            op("v", lambda: nc.vector.tensor_scalar(
                out=E("var3")[0], in0=E("var2")[0],
                scalar1=E("keps_ln")[0][:, 0:1], scalar2=None, op0=ALU.add),
                reads=[E("var2")[1], E("keps_ln")[1]], writes=[E("var3")[1]])
            op("s", lambda: nc.scalar.activation(
                out=E("sqv")[0], in_=E("var3")[0], func=ACT.Sqrt, bias=0.0,
                scale=1.0), reads=[E("var3")[1]], writes=[E("sqv")[1]])
            op("v", lambda: nc.vector.reciprocal(
                out=E("rstd")[0], in_=E("sqv")[0]),
                reads=[E("sqv")[1]], writes=[E("rstd")[1]])

        for eq in range(4):
            for ot in range(16):
                op("dS", lambda ot=ot, eq=eq: nc.sync.dma_start(
                    out=wtmp,
                    in_=woT_h[ot * 128:(ot + 1) * 128, eq * 512:(eq + 1) * 512]),
                    writes=[wtmp_k])
                wqlo = eq * 8192
                op("v", lambda ot=ot, wqlo=wqlo: nc.vector.tensor_copy(
                    out=Bb[:, wqlo + ot * 512:wqlo + (ot + 1) * 512], in_=wtmp),
                    reads=[wtmp_k],
                    writes=[("B", wqlo + ot * 512, wqlo + (ot + 1) * 512)])
        for qt in range(NT):
            emit_wq_task(2)
            op("dG", lambda qt=qt: nc.gpsimd.dma_start(
                out=srow, in_=x_h[qt * 128:(qt + 1) * 128, :]), writes=[srow_k])
            for eq in range(4):
                wqlo = eq * 8192
                pcol = eq * 512
                for ot in range(16):
                    op("t", lambda qt=qt, ot=ot, wqlo=wqlo, pcol=pcol:
                       nc.tensor.matmul(
                           ps_big[:, pcol:pcol + 512],
                           Aa[:, ot * T + qt * 128:ot * T + (qt + 1) * 128],
                           Bb[:, wqlo + ot * 512:wqlo + (ot + 1) * 512],
                           start=(ot == 0), stop=(ot == 15)),
                       reads=[("A", ot * T + qt * 128, ot * T + (qt + 1) * 128),
                              ("B", wqlo + ot * 512, wqlo + (ot + 1) * 512)],
                       writes=[("ps_big", pcol, pcol + 512)], acc=True)
            op("v", lambda: nc.vector.tensor_tensor(
                out=rrow, in0=srow, in1=ps_big[:, 0:2048], op=ALU.add),
                reads=[srow_k, ("ps_big", 0, 2048)], writes=[rrow_k])
            ln_stats(rrow, rrow_k)
            op("v", lambda: nc.vector.tensor_scalar(
                out=x1row, in0=rrow, scalar1=E("msc")[0][:, 0:1],
                scalar2=E("rstd")[0][:, 0:1], op0=ALU.subtract, op1=ALU.mult),
                reads=[rrow_k, E("msc")[1], E("rstd")[1]], writes=[x1row_k])
            op("dG", lambda qt=qt: nc.gpsimd.dma_start(
                out=s_x1[qt * 128:(qt + 1) * 128, :], in_=x1row),
                reads=[x1row_k], writes=[("hx1", qt, qt + 1)])
            q_scales(x1row, [x1row_k], 64, QA, qmb=qmbA)
            q_elem(x1row, [x1row_k], x1q, x1q_k, D, QA, scrP5)
            for g in range(2):
                transpose8(
                    lambda j, g=g: x1q[:, (g * 8 + j) * 128:(g * 8 + j + 1) * 128],
                    [x1q_k], xqs[:, g * 1024:(g + 1) * 1024],
                    ("G", D + g * 1024, D + (g + 1) * 1024), 8)
            dst = s_x1qT.rearrange("(dt p) t -> p dt t", p=128)[
                :, :, qt * 128:(qt + 1) * 128]
            op("dG", lambda dst=dst: nc.gpsimd.dma_start(
                out=dst, in_=xqs.rearrange("p (dt c) -> p dt c", c=128)),
                reads=[xqs_k], writes=[("hx1qT", qt, qt + 1)])

        while not wq_done[0]:
            emit_wq_task()

        em.barrier()
        xqT, xqT_k = Gx(0, ND * HALF)
        otile, otile_k = Hx(1792, 512)
        otile_b, otile_b_k = Hx(3328, 512)
        sgb, sgb_k = Hx(256, 512)
        ubb, ubb_k = Hx(768, 512)
        intq, intq_k = Hx(1280, 512)
        qmbC = Hx(3840, 256)
        scrP6b2 = [{
            "zz": Dx(640, 512), "yy": Dx(1152, 512), "qmb": qmbC,
            "c1": Dx(1664, 512), "c2": Dx(2176, 512),
        }, {
            "zz": Dx(2688, 512), "yy": Dx(3200, 512), "qmb": qmbC,
            "c1": Dx(3712, 512), "c2": Dx(4224, 512),
        }]

        def wslot(k):
            return Bb[:, k * 8192:(k + 1) * 8192], ("B", k * 8192, (k + 1) * 8192)

        def wdslot(k):
            lo = 24576 + k * 4096
            return Bb[:, lo:lo + 4096], ("B", lo, lo + 4096)

        qd = [Hx(2304, 512), Hx(2816, 512)]

        def p6b_products(ic):
            ici = ic % 4
            sgs, sgs_k = wslot((2 * ic) % 3)
            sus, sus_k = wslot((2 * ic + 1) % 3)
            op("dS", lambda ic=ic, sgs=sgs: nc.sync.dma_start(
                out=sgs.rearrange("p (dt c) -> p dt c", c=ICS),
                in_=s_wgT[ic].rearrange("dt p c -> p dt c")),
                reads=[("hw0", ic, ic + 1)], writes=[sgs_k])
            op("dS", lambda ic=ic, sus=sus: nc.sync.dma_start(
                out=sus.rearrange("p (dt c) -> p dt c", c=ICS),
                in_=s_wuT[ic].rearrange("dt p c -> p dt c")),
                reads=[("hw1", ic, ic + 1)], writes=[sus_k])
            for tokt in range(4):
                gps, gps_k = (ps_a[:], ("ps_a", 0, 512)) if tokt % 2 == 0 \
                    else (ps_c[:], ("ps_c", 0, 512))
                ups, ups_k = (ps_b[:], ("ps_b", 0, 512)) if tokt % 2 == 0 \
                    else (ps_big[:, 0:512], ("ps_big", 0, 512))
                sgb2, sgb2_k = (sgb, sgb_k) if tokt % 2 == 0 else (intq, intq_k)
                for dt in range(ND):
                    op("t", lambda dt=dt, tokt=tokt, sgs=sgs, gps=gps:
                       nc.tensor.matmul(
                           gps,
                           xqT[:, dt * HALF + tokt * 128:
                               dt * HALF + (tokt + 1) * 128],
                           sgs[:, dt * ICS:(dt + 1) * ICS],
                           start=(dt == 0), stop=(dt == ND - 1)),
                       reads=[xqT_k, sgs_k],
                       writes=[gps_k], acc=True)
                for dt in range(ND):
                    op("t", lambda dt=dt, tokt=tokt, sus=sus, ups=ups:
                       nc.tensor.matmul(
                           ups,
                           xqT[:, dt * HALF + tokt * 128:
                               dt * HALF + (tokt + 1) * 128],
                           sus[:, dt * ICS:(dt + 1) * ICS],
                           start=(dt == 0), stop=(dt == ND - 1)),
                       reads=[xqT_k, sus_k],
                       writes=[ups_k], acc=True)
                op("s", lambda gps=gps, sgb2=sgb2: nc.scalar.activation(
                    out=sgb2, in_=gps, func=ACT.Silu, bias=0.0,
                    scale=1.0), reads=[gps_k], writes=[sgb2_k])
                op("s", lambda ups=ups: nc.scalar.copy(out=ubb, in_=ups),
                   reads=[ups_k], writes=[ubb_k])
                dcol = (ici * 4 + tokt) * 512
                op("v", lambda dcol=dcol, sgb2=sgb2: nc.vector.tensor_tensor(
                    out=Ff[:, dcol:dcol + 512], in0=sgb2, in1=ubb,
                    op=ALU.mult),
                    reads=[sgb2_k, ubb_k], writes=[("F", dcol, dcol + 512)])

        for hf in range(NHALF):
            op("dG", lambda hf=hf: nc.gpsimd.dma_start(
                out=xqT.rearrange("p (dt c) -> p dt c", c=HALF),
                in_=s_x1qT.rearrange("(dt p) t -> p dt t", p=128)[
                    :, :, hf * HALF:(hf + 1) * HALF]),
                reads=[("hx1qT", 0, NT)], writes=[xqT_k])
            for ici in range(4):
                p6b_products(ici)
            for icq in range(NIC // 4):
                q_scales(Ff[:, 0:8192], [("F", 0, 8192)], 256, QA, qmb=qmbC)
                pend = []
                for sub in range(16):
                    ici, tokt = sub // 4, sub % 4
                    ic2 = icq * 4 + ici
                    dcol = sub * 512
                    s = sub % 2
                    ch = q_elem_steps(
                        Ff[:, dcol:dcol + 512], [("F", dcol, dcol + 512)],
                        qd[s][0], qd[s][1], 512, QA, scrP6b2[s],
                        nb_off=dcol // 32)
                    pend.append((ch, ic2, tokt, s))
                    if len(pend) == 2 or sub == 15:
                        zipper([p[0] for p in pend])
                        if icq < 3 and sub % 4 == 3:
                            p6b_products((icq + 1) * 4 + sub // 4)
                        for (_, ic2b, tokt2, s2) in pend:
                            for j in range(4):
                                op("t", lambda j=j, s2=s2: nc.tensor.transpose(
                                    ps_tr[:, j * 128:(j + 1) * 128],
                                    qd[s2][0][:, j * 128:(j + 1) * 128], identb),
                                    reads=[qd[s2][1], identb_k],
                                    writes=[("ps_tr", j * 128, (j + 1) * 128)])
                            dsts = Aa[:].rearrange("p (it c) -> p it c", c=512)[
                                :, ic2b * 4:ic2b * 4 + 4,
                                tokt2 * 128:(tokt2 + 1) * 128]
                            op("s", lambda dsts=dsts: nc.scalar.copy(
                                out=dsts,
                                in_=ps_tr[:, 0:512].rearrange("p (n c) -> p n c", c=128)),
                                reads=[("ps_tr", 0, 512)],
                                writes=[("A", ic2b * 4 * 512, (ic2b * 4 + 4) * 512)])
                        pend = []
            def wd_dma(idx):
                et2, hlf = idx // 2, idx % 2
                wds, wds_k = wdslot(idx % 2)
                st = "dS" if (idx // 2) % 2 == 0 else "dG"
                eng = nc.sync if st == "dS" else nc.gpsimd
                op(st, lambda et2=et2, hlf=hlf, wds=wds, eng=eng: eng.dma_start(
                    out=wds.rearrange("p (it c) -> p it c", c=128),
                    in_=s_wdT[et2, hlf * 32:(hlf + 1) * 32].rearrange(
                        "it p c -> p it c")),
                    reads=[("hwd", et2, et2 + 1)], writes=[wds_k])

            wd_dma(0)
            wd_dma(1)
            for et in range(ND):
                pcol = (et % 4) * 512
                for hlf in range(2):
                    idx = et * 2 + hlf
                    wds, wds_k = wdslot(idx % 2)
                    for it2 in range(32):
                        it = hlf * 32 + it2
                        op("t", lambda it=it, it2=it2, wds=wds, pcol=pcol:
                           nc.tensor.matmul(
                               ps_big[:, pcol:pcol + 512],
                               wds[:, it2 * 128:(it2 + 1) * 128],
                               Aa[:, it * 512:(it + 1) * 512],
                               start=(it == 0), stop=(it == NFI - 1)),
                           reads=[wds_k, ("A", it * 512, (it + 1) * 512)],
                           writes=[("ps_big", pcol, pcol + 512)], acc=True)
                    if idx + 2 < ND * 2:
                        wd_dma(idx + 2)
                ot2, ot2_k = (otile, otile_k) if et % 2 == 0 else (otile_b, otile_b_k)
                op("s", lambda pcol=pcol, ot2=ot2: nc.scalar.copy(
                    out=ot2, in_=ps_big[:, pcol:pcol + 512]),
                    reads=[("ps_big", pcol, pcol + 512)], writes=[ot2_k])
                for tokt in range(4):
                    op("dG", lambda hf=hf, tokt=tokt, et=et, ot2=ot2:
                       nc.gpsimd.dma_start(
                           out=s_x2[hf, tokt, et],
                           in_=ot2[:, tokt * 128:(tokt + 1) * 128]),
                       reads=[ot2_k],
                       writes=[("hx2", (hf * 4 + tokt) * ND + et,
                                (hf * 4 + tokt) * ND + et + 1)])

        em.barrier()
        x2t, x2t_k = Gx(0, 2048)
        x2row, x2row_k = Hx(256, 2048)
        x1row7, x1row7_k = Hx(256 + 2048, 2048)
        outrow, outrow_k = Dx(4736, D)
        for qt in range(NT):
            hf, tokt = qt // 4, qt % 4
            op("dG", lambda hf=hf, tokt=tokt: nc.gpsimd.dma_start(
                out=x2t.rearrange("p (et c) -> p et c", c=128),
                in_=s_x2[hf, tokt].rearrange("et p c -> p et c")),
                reads=[("hx2", (hf * 4 + tokt) * ND, (hf * 4 + tokt + 1) * ND)],
                writes=[x2t_k])
            for g in range(2):
                transpose8(
                    lambda j, g=g: x2t[:, (g * 8 + j) * 128:(g * 8 + j + 1) * 128],
                    [x2t_k], x2row[:, g * 1024:(g + 1) * 1024],
                    ("H", 256 + g * 1024, 256 + (g + 1) * 1024), 8)
            op("dG", lambda qt=qt: nc.gpsimd.dma_start(
                out=x1row7, in_=s_x1[qt * 128:(qt + 1) * 128, :]),
                reads=[("hx1", qt, qt + 1)], writes=[x1row7_k])
            op("v", lambda: nc.vector.tensor_tensor(
                out=rrow, in0=x1row7, in1=x2row, op=ALU.add),
                reads=[x1row7_k, ("H", 256, 256 + 2048)], writes=[rrow_k])
            ln_stats(rrow, rrow_k)
            op("v", lambda: nc.vector.tensor_scalar(
                out=outrow, in0=rrow, scalar1=E("msc")[0][:, 0:1],
                scalar2=E("rstd")[0][:, 0:1], op0=ALU.subtract, op1=ALU.mult),
                reads=[rrow_k, E("msc")[1], E("rstd")[1]], writes=[outrow_k])
            op("dG", lambda qt=qt: nc.gpsimd.dma_start(
                out=out_h[qt * 128:(qt + 1) * 128, :], in_=outrow),
                reads=[outrow_k], writes=[("hout", qt, qt + 1)])

        em.finish("dS")

    return nc


_CACHE = {}


def kernel(L0_attn_x, L0_attn_wq, L0_attn_wk, L0_attn_wv, L0_attn_wo,
           L0_attn_bq, L0_attn_bk, L0_attn_bv,
           L0_ffn_w_gate, L0_ffn_w_up, L0_ffn_w_down):
    x = np.ascontiguousarray(np.asarray(L0_attn_x, dtype=np.float32))
    wq = np.asarray(L0_attn_wq, np.float32)
    wk = np.asarray(L0_attn_wk, np.float32)
    wv = np.asarray(L0_attn_wv, np.float32)
    wo = np.asarray(L0_attn_wo, np.float32)
    wqkvT = np.ascontiguousarray(np.concatenate([wq, wk, wv], axis=0).T)
    woT = np.ascontiguousarray(wo.T)
    bqkv = np.ascontiguousarray(np.concatenate(
        [np.asarray(L0_attn_bq, np.float32), np.asarray(L0_attn_bk, np.float32),
         np.asarray(L0_attn_bv, np.float32)]))
    wg = np.ascontiguousarray(np.asarray(L0_ffn_w_gate, np.float32))
    wu = np.ascontiguousarray(np.asarray(L0_ffn_w_up, np.float32))
    wd = np.ascontiguousarray(np.asarray(L0_ffn_w_down, np.float32))
    ident, swap, freqmod, pos, sgn, mask01 = _host_consts()

    if "nc" not in _CACHE:
        _CACHE["nc"] = build_nc()
    nc = _CACHE["nc"]

    shared = {
        "wqkvT": wqkvT, "woT": woT, "bqkv": bqkv,
        "wg": wg, "wu": wu, "wd": wd,
        "c_ident": ident, "c_swap": swap, "c_freqmod": freqmod,
        "c_pos": pos, "c_sgn": sgn, "c_mask01": mask01,
    }
    in_maps = [dict(shared, x=np.ascontiguousarray(x[c])) for c in range(B)]
    res = run_bass_kernel_spmd(nc, in_maps, core_ids=list(range(B)))
    out = np.stack([np.asarray(res.results[c]["out"], dtype=np.float32)
                    for c in range(B)], axis=0)
    return out


# version 12 (final)
# speedup vs baseline: 1.0580x; 1.0139x over previous
import math
import numpy as np
import concourse.bass as bass
import concourse.mybir as mybir
from concourse.bass_utils import run_bass_kernel_spmd
from contextlib import ExitStack

F32 = mybir.dt.float32
BF16 = mybir.dt.bfloat16
ALU = mybir.AluOpType
ACT = mybir.ActivationFunctionType
AXL = mybir.AxisListType

B = 8
T = 2048
D = 2048
NT = T // 128
ND = D // 128
HQ, HKV, DH = 16, 2, 128
FI = 8192
NFI = FI // 128
HALF = 512
NHALF = T // HALF
ICS = 512
NIC = FI // ICS
ALPHA = 1.0 / math.sqrt(DH)
LN_EPS = 1e-5

QA = (3.0, 2.0, 7.5)
QW = (1.0, 2.0, 6.0)
LADDER = [32, 16, 8, 4, 2, 1]
CLIP_LO, CLIP_HI = 2.0 ** -26, 2.0 ** 37
PI2_HI = float(np.float32(2.0 * math.pi))
PI2_LO = 2.0 * math.pi - PI2_HI


def _host_consts():
    ident = np.eye(128, dtype=np.float32)
    swap = np.zeros((128, 128), dtype=np.float32)
    for k in range(128):
        swap[k, (k + 64) % 128] = 1.0
    freqmod = (np.arange(128, dtype=np.float32) % 64)[None, :]
    pos = np.arange(T, dtype=np.float32)[None, :]
    sgn = np.concatenate([-np.ones(64, np.float32), np.ones(64, np.float32)])
    mask01 = np.tril(np.ones((128, 128), np.float32))
    return ident, swap, freqmod, pos, sgn, mask01


class Em:

    def __init__(self, nc, ctx):
        self.nc = nc
        self.sems = {}
        self.tot = {}
        self.nsem = {}
        self.step = {}
        self.cnt = {}
        self.eng = {}
        self._ctx = ctx
        self.wr = {}
        self.rd = {}
        self.mark = {}

    def stream(self, name, eng, step=1, nsem=1):
        for i in range(nsem):
            self.sems[(name, i)] = self._ctx.enter_context(
                self.nc.semaphore(f"S_{name}_{i}"))
            self.tot[(name, i)] = 0
        self.nsem[name] = nsem
        self.eng[name] = eng
        self.step[name] = step
        self.cnt[name] = 0

    def op(self, st, fn, reads=(), writes=(), acc=False):
        waits = {}

        def need(s, i, c):
            if waits.get((s, i), 0) < c:
                waits[(s, i)] = c

        def norm(k):
            return (k, 0, 1) if isinstance(k, str) else k

        reads = [norm(k) for k in reads]
        writes = [norm(k) for k in writes]
        for tag, lo, hi in reads:
            for (l2, h2, s, i, c) in self.wr.get(tag, ()):
                if l2 < hi and lo < h2:
                    need(s, i, c)
        for tag, lo, hi in writes:
            for (l2, h2, s, i, c) in self.wr.get(tag, ()):
                if l2 < hi and lo < h2 and not (acc and s == st):
                    need(s, i, c)
            for (l2, h2, s, i, c) in self.rd.get(tag, ()):
                if l2 < hi and lo < h2 and not (acc and s == st):
                    need(s, i, c)
        eng = self.eng[st]
        for (s, i) in sorted(waits):
            c = waits[(s, i)]
            if self.step[s] > 1:
                c = self.tot[(s, i)]
            if self.mark.get((st, s, i), -1) >= c:
                continue
            eng.wait_ge(self.sems[(s, i)], c)
            self.mark[(st, s, i)] = c
        inst = fn()
        idx = (self.cnt[st] // self.step[st]) % self.nsem[st]             if self.step[st] > 1 else 0
        inst.then_inc(self.sems[(st, idx)], self.step[st])
        self.cnt[st] += self.step[st]
        self.tot[(st, idx)] += self.step[st]
        c = self.tot[(st, idx)]
        for tag, lo, hi in writes:
            lst = [e for e in self.wr.get(tag, []) if not (e[0] < hi and lo < e[1])]
            lst.append((lo, hi, st, idx, c))
            self.wr[tag] = lst
            self.rd[tag] = [e for e in self.rd.get(tag, [])
                            if not (e[0] < hi and lo < e[1])]
        for tag, lo, hi in reads:
            lst = self.rd.setdefault(tag, [])
            lst[:] = [e for e in lst
                      if not (e[0] == lo and e[1] == hi and e[2] == st and e[3] == idx)]
            lst.append((lo, hi, st, idx, c))
        return inst

    def barrier(self):
        for w in self.nsem:
            eng = self.eng[w]
            for (s, i), t in self.tot.items():
                if t > 0 and self.mark.get((w, s, i), -1) < t:
                    eng.wait_ge(self.sems[(s, i)], t)
                    self.mark[(w, s, i)] = t
        self.wr.clear()
        self.rd.clear()

    def finish(self, st):
        eng = self.eng[st]
        for (s, i), t in self.tot.items():
            if t > 0 and self.mark.get((st, s, i), -1) < t:
                eng.wait_ge(self.sems[(s, i)], t)


def build_nc():
    nc = bass.Bass("TRN2", target_bir_lowering=False)

    x_h = nc.dram_tensor("x", [T, D], F32, kind="ExternalInput").ap()
    wqkvT_h = nc.dram_tensor("wqkvT", [D, (HQ + 2 * HKV) * DH], F32, kind="ExternalInput").ap()
    woT_h = nc.dram_tensor("woT", [D, D], F32, kind="ExternalInput").ap()
    wg_h = nc.dram_tensor("wg", [FI, D], F32, kind="ExternalInput").ap()
    wu_h = nc.dram_tensor("wu", [FI, D], F32, kind="ExternalInput").ap()
    wd_h = nc.dram_tensor("wd", [D, FI], F32, kind="ExternalInput").ap()
    bqkv_h = nc.dram_tensor("bqkv", [(HQ + 2 * HKV) * DH], F32, kind="ExternalInput").ap()
    ident_h = nc.dram_tensor("c_ident", [128, 128], F32, kind="ExternalInput").ap()
    swap_h = nc.dram_tensor("c_swap", [128, 128], F32, kind="ExternalInput").ap()
    freqmod_h = nc.dram_tensor("c_freqmod", [1, 128], F32, kind="ExternalInput").ap()
    pos_h = nc.dram_tensor("c_pos", [1, T], F32, kind="ExternalInput").ap()
    sgn_h = nc.dram_tensor("c_sgn", [128], F32, kind="ExternalInput").ap()
    mask_h = nc.dram_tensor("c_mask01", [128, 128], F32, kind="ExternalInput").ap()
    out_h = nc.dram_tensor("out", [T, D], F32, kind="ExternalOutput").ap()
    s_attn = nc.dram_tensor("s_attn", [T, D], BF16, kind="Internal").ap()
    s_x1 = nc.dram_tensor("s_x1", [T, D], BF16, kind="Internal").ap()
    s_x1qT = nc.dram_tensor("s_x1qT", [D, T], BF16, kind="Internal").ap()
    s_wgT = nc.dram_tensor("s_wgT", [NIC, ND, 128, ICS], BF16, kind="Internal").ap()
    s_wuT = nc.dram_tensor("s_wuT", [NIC, ND, 128, ICS], BF16, kind="Internal").ap()
    s_wdT = nc.dram_tensor("s_wdT", [ND, NFI, 128, 128], BF16, kind="Internal").ap()
    s_x2 = nc.dram_tensor("s_x2", [NHALF, HALF // 128, ND, 128, 128], BF16, kind="Internal").ap()

    with ExitStack() as ctx:
        def sbt(name, shape, dt=F32):
            return ctx.enter_context(nc.sbuf_tensor(name, shape, dt))

        Aa = sbt("arenaA", [128, 32768], BF16)
        Bb = sbt("arenaB", [128, 32768], BF16)
        Dd = sbt("arenaD", [128, 6784], F32)
        Ee = sbt("arenaE", [128, 3160], F32)
        Ff = sbt("arenaF", [128, 8192], BF16)
        Gg = sbt("arenaG", [128, 8192], BF16)
        Hh = sbt("arenaH", [128, 4352], BF16)
        CS2 = Gg[:, 4096:8192]

        ps_big = ctx.enter_context(nc.psum_tensor("ps_big", [128, 2048], F32))
        ps_a = ctx.enter_context(nc.psum_tensor("ps_a", [128, 512], F32))
        ps_b = ctx.enter_context(nc.psum_tensor("ps_b", [128, 512], F32))
        ps_tr = ctx.enter_context(nc.psum_tensor("ps_tr", [128, 1024], BF16))
        ps_c = ctx.enter_context(nc.psum_tensor("ps_c", [128, 512], F32))

        block = ctx.enter_context(nc.Block())
        em = Em(nc, ctx)
        em.stream("t", nc.tensor)
        em.stream("v", nc.vector)
        em.stream("s", nc.scalar)
        em.stream("g", nc.gpsimd)
        em.stream("dS", nc.sync, 16, nsem=12)
        em.stream("dG", nc.gpsimd, 16, nsem=12)
        op = em.op

        def Dx(lo, w):
            return Dd[:, lo:lo + w], ("D", lo, lo + w)

        def Fx(lo, w):
            return Ff[:, lo:lo + w], ("F", lo, lo + w)

        def Gx(lo, w):
            return Gg[:, lo:lo + w], ("G", lo, lo + w)

        def Hx(lo, w):
            return Hh[:, lo:lo + w], ("H", lo, lo + w)

        EC = {}
        _eo = [0]

        def edef(name, w):
            EC[name] = (_eo[0], w)
            _eo[0] += w
            assert _eo[0] <= 3160, name

        def E(name):
            o, w = EC[name]
            return Ee[:, o:o + w], ("E", o, o + w)

        def eN(name, nb, off=0):
            o, w = EC[name]
            assert off + nb <= w
            return Ee[:, o + off:o + off + nb], ("E", o + off, o + off + nb)

        for nm in ["amax", "rr", "pA", "pB", "pw", "csb", "cge", "ff",
                   "psm", "pinv", "zmul", "qmul"]:
            edef(nm, 256)
        for nm in ["cmag", "det_a", "det_b", "det_c", "kcn", "kcs", "keps",
                   "k2p30", "kc26", "kcm26", "kzA", "kqA", "kzW", "kqW",
                   "kclo", "kchi", "k2pinv", "kpihalf", "kpiclp", "kpicln",
                   "kan", "keps_ln", "bias_sm", "rowmax", "sum_a", "sum_d",
                   "sum_t", "inv_s", "mean", "msc", "m2sq", "var_s", "var1",
                   "var2", "var3", "sqv", "rstd", "sgn"]:
            edef(nm, 1)
        for w in LADDER:
            edef(f"kp2_{w}", 1)
            edef(f"kp2m1_{w}", 1)
        edef("bq", 20)
        edef("rmp", 4)

        idf, idf_k = Dx(0, 128)
        swf, swf_k = Dx(128, 128)
        maskf, maskf_k = Dx(256, 128)
        fqrow = Dd[0:1, 384:512]
        fqrow_k = ("D", 384, 512)
        invf = Dd[0:1, 512:640]
        invf_k = ("D", 512, 640)
        posrow = Dd[0:1, 640:640 + T]
        posrow_k = ("D", 640, 640 + T)
        identb, identb_k = Hx(0, 128)
        maskb, maskb_k = Hx(128, 128)

        op("dS", lambda: nc.sync.dma_start(out=idf, in_=ident_h[:]), writes=[idf_k])
        op("dS", lambda: nc.sync.dma_start(out=swf, in_=swap_h[:]), writes=[swf_k])
        op("dS", lambda: nc.sync.dma_start(out=maskf, in_=mask_h[:]), writes=[maskf_k])
        op("dS", lambda: nc.sync.dma_start(
            out=E("bq")[0], in_=bqkv_h.rearrange("(t p) -> p t", p=128),
            allow_slow_non_contiguous=True), writes=[E("bq")[1]])
        op("dS", lambda: nc.sync.dma_start(
            out=E("sgn")[0], in_=sgn_h.rearrange("(p o) -> p o", o=1),
            allow_slow_non_contiguous=True), writes=[E("sgn")[1]])
        op("dS", lambda: nc.sync.dma_start(out=fqrow, in_=freqmod_h[:]), writes=[fqrow_k])
        op("dS", lambda: nc.sync.dma_start(out=posrow, in_=pos_h[:]), writes=[posrow_k])

        op("v", lambda: nc.vector.tensor_copy(out=identb, in_=idf),
           reads=[idf_k], writes=[identb_k])
        op("v", lambda: nc.vector.tensor_copy(out=maskb, in_=maskf),
           reads=[maskf_k], writes=[maskb_k])

        def kmem(name, val):
            ap, key = E(name)
            op("v", lambda ap=ap, val=val: nc.vector.memset(ap, float(val)),
               writes=[key])

        kmem("kcn", 1.5 * 2.0 ** 23)
        kmem("kcs", 1.5 * 2.0 ** 52)
        kmem("keps", 2.0 ** -30)
        kmem("k2p30", 2.0 ** 30)
        kmem("kc26", 2.0 ** 26)
        kmem("kcm26", 2.0 ** -26)
        kmem("kzA", 2.0 ** (QA[0] + QA[1]))
        kmem("kqA", 2.0 ** -(QA[0] + QA[1]))
        kmem("kzW", 2.0 ** (QW[0] + QW[1]))
        kmem("kqW", 2.0 ** -(QW[0] + QW[1]))
        kmem("kclo", CLIP_LO)
        kmem("kchi", CLIP_HI)
        kmem("k2pinv", 1.0 / (2.0 * math.pi))
        kmem("kpihalf", math.pi / 2.0)
        kmem("kpiclp", 3.1415925)
        kmem("kpicln", -3.1415925)
        kmem("kan", -ALPHA)
        kmem("keps_ln", LN_EPS)
        for w in LADDER:
            kmem(f"kp2_{w}", 2.0 ** w)
            kmem(f"kp2m1_{w}", 2.0 ** w - 1.0)

        det_a, det_a_k = E("det_a")
        det_b, det_b_k = E("det_b")
        det_c, det_c_k = E("det_c")
        cmag, cmag_k = E("cmag")
        op("v", lambda: nc.vector.tensor_scalar(
            out=det_a, in0=Dd[:, 0:1], scalar1=float(-3.0e38), scalar2=None,
            op0=ALU.is_ge), reads=[("D", 0, 1)], writes=[det_a_k])
        op("v", lambda: nc.vector.tensor_tensor(
            out=det_b, in0=det_a, in1=E("keps")[0], op=ALU.mult),
            reads=[det_a_k, E("keps")[1]], writes=[det_b_k])
        op("v", lambda: nc.vector.tensor_tensor(
            out=det_c, in0=det_a, in1=det_b, op=ALU.add),
            reads=[det_a_k, det_b_k], writes=[det_c_k])
        op("v", lambda: nc.vector.tensor_tensor(
            out=det_b, in0=det_c, in1=det_a, op=ALU.subtract),
            reads=[det_c_k, det_a_k], writes=[det_b_k])
        op("v", lambda: nc.vector.tensor_tensor(
            out=det_c, in0=det_b, in1=E("k2p30")[0], op=ALU.mult),
            reads=[det_b_k, E("k2p30")[1]], writes=[det_c_k])
        op("v", lambda: nc.vector.tensor_tensor(
            out=det_a, in0=det_c, in1=E("kcs")[0], op=ALU.mult),
            reads=[det_c_k, E("kcs")[1]], writes=[det_a_k])
        op("v", lambda: nc.vector.tensor_scalar(
            out=det_b, in0=det_c, scalar1=-1.0, scalar2=1.0,
            op0=ALU.mult, op1=ALU.add), reads=[det_c_k], writes=[det_b_k])
        op("v", lambda: nc.vector.tensor_tensor(
            out=det_c, in0=det_b, in1=E("kcn")[0], op=ALU.mult),
            reads=[det_b_k, E("kcn")[1]], writes=[det_c_k])
        op("v", lambda: nc.vector.tensor_tensor(
            out=cmag, in0=det_a, in1=det_c, op=ALU.add),
            reads=[det_a_k, det_c_k], writes=[cmag_k])

        def rne(out_ap, in_ap, rk, wk, st="v"):
            op(st, lambda: nc.vector.tensor_scalar(
                out=out_ap, in0=in_ap, scalar1=cmag[:, 0:1], scalar2=cmag[:, 0:1],
                op0=ALU.add, op1=ALU.subtract),
                reads=list(rk) + [cmag_k], writes=list(wk))

        op("s", lambda: nc.scalar.activation(
            out=invf, in_=fqrow, func=ACT.Exp, bias=0.0,
            scale=-math.log(10000.0) / 64.0),
            reads=[fqrow_k], writes=[invf_k])
        angA, angA_k = Dx(2688, 1024)
        angB, angB_k = Dx(3712, 1024)
        angC, angC_k = Dx(4736, 1024)
        scrX, scrX_k = Dx(5760, 1024)
        for hh in range(2):
            base = hh * 1024
            for c in range(2):
                op("t", lambda c=c, base=base: nc.tensor.matmul(
                    ps_c[:, 0:512], invf,
                    posrow[:, base + c * 512:base + (c + 1) * 512],
                    start=True, stop=True),
                    reads=[invf_k, posrow_k], writes=[("ps_c", 0, 512)])
                op("v", lambda c=c: nc.vector.tensor_copy(
                    out=angA[:, c * 512:(c + 1) * 512], in_=ps_c[:, 0:512]),
                    reads=[("ps_c", 0, 512)], writes=[angA_k])
            for (use_shift, dst_lo) in [(True, 0), (False, T)]:
                if use_shift:
                    op("v", lambda: nc.vector.tensor_scalar(
                        out=angB, in0=angA, scalar1=E("kpihalf")[0][:, 0:1],
                        scalar2=None, op0=ALU.add),
                        reads=[angA_k, E("kpihalf")[1]], writes=[angB_k])
                else:
                    op("v", lambda: nc.vector.tensor_copy(out=angB, in_=angA),
                       reads=[angA_k], writes=[angB_k])
                for _ in range(2):
                    op("v", lambda: nc.vector.tensor_scalar(
                        out=angC, in0=angB, scalar1=E("k2pinv")[0][:, 0:1],
                        scalar2=None, op0=ALU.mult),
                        reads=[angB_k, E("k2pinv")[1]], writes=[angC_k])
                    rne(angC, angC, [angC_k], [angC_k])
                    op("v", lambda: nc.vector.scalar_tensor_tensor(
                        out=scrX, in0=angC, scalar=-PI2_HI, in1=angB,
                        op0=ALU.mult, op1=ALU.add),
                        reads=[angC_k, angB_k], writes=[scrX_k])
                    op("v", lambda: nc.vector.scalar_tensor_tensor(
                        out=angB, in0=angC, scalar=-PI2_LO, in1=scrX,
                        op0=ALU.mult, op1=ALU.add),
                        reads=[angC_k, scrX_k], writes=[angB_k])
                op("v", lambda: nc.vector.tensor_scalar(
                    out=angC, in0=angB, scalar1=E("kpiclp")[0][:, 0:1],
                    scalar2=E("kpicln")[0][:, 0:1], op0=ALU.min, op1=ALU.max),
                    reads=[angB_k, E("kpiclp")[1], E("kpicln")[1]], writes=[angC_k])
                if not use_shift:
                    op("s", lambda base=base, dst_lo=dst_lo: nc.scalar.activation(
                        out=scrX, in_=angC, func=ACT.Sin, bias=0.0, scale=1.0),
                        reads=[angC_k], writes=[scrX_k])
                    op("v", lambda base=base, dst_lo=dst_lo: nc.vector.tensor_scalar(
                        out=CS2[:, dst_lo + base:dst_lo + base + 1024], in0=scrX,
                        scalar1=E("sgn")[0][:, 0:1], scalar2=None, op0=ALU.mult),
                        reads=[scrX_k, E("sgn")[1]],
                        writes=[("G", 4096 + dst_lo + base, 4096 + dst_lo + base + 1024)])
                else:
                    op("s", lambda base=base, dst_lo=dst_lo: nc.scalar.activation(
                        out=CS2[:, dst_lo + base:dst_lo + base + 1024], in_=angC,
                        func=ACT.Sin, bias=0.0, scale=1.0),
                        reads=[angC_k],
                        writes=[("G", 4096 + dst_lo + base, 4096 + dst_lo + base + 1024)])
        C2 = CS2[:, 0:T]
        C2_k = ("G", 4096, 4096 + T)
        S2 = CS2[:, T:2 * T]
        S2_k = ("G", 4096 + T, 4096 + 2 * T)

        def q_amax(src_ap, src_keys, nb, off):
            amax, amax_k = eN("amax", nb, off)
            op("v", lambda: nc.vector.tensor_reduce(
                out=amax, in_=src_ap.rearrange("p (b k) -> p b k", k=32),
                axis=AXL.X, op=ALU.max, apply_absolute_value=True),
                reads=list(src_keys), writes=[amax_k])

        def q_ladder(nb, params, off=0, lst="v", qmb=None):
            mfrac, emax, mn = params
            le = nc.gpsimd if lst == "g" else nc.vector
            kz = "kzA" if mfrac == QA[0] else "kzW"
            kq = "kqA" if mfrac == QA[0] else "kqW"
            amax, amax_k = eN("amax", nb, off)
            rr, rr_k = eN("rr", nb, off)
            pw, pw_k = eN("pw", nb, off)
            csb, csb_k = eN("csb", nb, off)
            cge, cge_k = eN("cge", nb, off)
            ffb, ffb_k = eN("ff", nb, off)
            psm, psm_k = eN("psm", nb, off)
            pinv, pinv_k = eN("pinv", nb, off)
            zmul, zmul_k = eN("zmul", nb, off)
            qmul, qmul_k = eN("qmul", nb, off)
            pa, pa_k = eN("pA", nb, off)
            pb, pb_k = eN("pB", nb, off)
            op(lst, lambda: le.tensor_scalar(
                out=csb, in0=amax, scalar1=E("kchi")[0][:, 0:1],
                scalar2=E("kclo")[0][:, 0:1], op0=ALU.min, op1=ALU.max),
                reads=[amax_k, E("kchi")[1], E("kclo")[1]], writes=[csb_k])
            op(lst, lambda: le.tensor_scalar(
                out=rr, in0=csb, scalar1=E("kc26")[0][:, 0:1], scalar2=None,
                op0=ALU.mult), reads=[csb_k, E("kc26")[1]], writes=[rr_k])
            cur, cur_k, nxt, nxt_k = pa, pa_k, pb, pb_k
            first = True
            for w in LADDER:
                if first:
                    op(lst, lambda w=w: le.tensor_scalar(
                        out=cge, in0=rr, scalar1=E(f"kp2_{w}")[0][:, 0:1],
                        scalar2=None, op0=ALU.is_ge),
                        reads=[rr_k, E(f"kp2_{w}")[1]], writes=[cge_k])
                else:
                    op(lst, lambda cur=cur, w=w: le.tensor_scalar(
                        out=pw, in0=cur, scalar1=E(f"kp2_{w}")[0][:, 0:1],
                        scalar2=None, op0=ALU.mult),
                        reads=[cur_k, E(f"kp2_{w}")[1]], writes=[pw_k])
                    op(lst, lambda: le.tensor_tensor(
                        out=cge, in0=rr, in1=pw, op=ALU.is_ge),
                        reads=[rr_k, pw_k], writes=[cge_k])
                if w >= 32:
                    op(lst, lambda w=w: le.tensor_scalar(
                        out=csb, in0=cge, scalar1=E(f"kp2_{w}")[0][:, 0:1],
                        scalar2=None, op0=ALU.mult),
                        reads=[cge_k, E(f"kp2_{w}")[1]], writes=[csb_k])
                    op(lst, lambda: le.tensor_scalar(
                        out=pw, in0=cge, scalar1=-1.0, scalar2=1.0,
                        op0=ALU.mult, op1=ALU.add), reads=[cge_k], writes=[pw_k])
                    op(lst, lambda: le.tensor_tensor(
                        out=ffb, in0=csb, in1=pw, op=ALU.add),
                        reads=[csb_k, pw_k], writes=[ffb_k])
                else:
                    op(lst, lambda w=w: le.tensor_scalar(
                        out=ffb, in0=cge, scalar1=E(f"kp2m1_{w}")[0][:, 0:1],
                        scalar2=1.0, op0=ALU.mult, op1=ALU.add),
                        reads=[cge_k, E(f"kp2m1_{w}")[1]], writes=[ffb_k])
                if first:
                    op(lst, lambda cur=cur: le.tensor_copy(
                        out=cur, in_=ffb), reads=[ffb_k], writes=[cur_k])
                    first = False
                else:
                    op(lst, lambda cur=cur, nxt=nxt: le.tensor_tensor(
                        out=nxt, in0=cur, in1=ffb, op=ALU.mult),
                        reads=[cur_k, ffb_k], writes=[nxt_k])
                    cur, cur_k, nxt, nxt_k = nxt, nxt_k, cur, cur_k
            op(lst, lambda cur=cur: le.tensor_scalar(
                out=psm, in0=cur, scalar1=E("kcm26")[0][:, 0:1], scalar2=None,
                op0=ALU.mult), reads=[cur_k, E("kcm26")[1]], writes=[psm_k])
            op("v", lambda: nc.vector.reciprocal(out=pinv, in_=psm),
               reads=[psm_k], writes=[pinv_k])
            op("v", lambda: nc.vector.tensor_scalar(
                out=zmul, in0=pinv, scalar1=E(kz)[0][:, 0:1], scalar2=None,
                op0=ALU.mult), reads=[pinv_k, E(kz)[1]], writes=[zmul_k])
            if qmb is not None:
                qba = qmb[0][:, off:off + nb]
                qbk = (qmb[1], off, off + nb)
                op(lst, lambda: le.tensor_scalar(
                    out=qba, in0=psm, scalar1=E(kq)[0][:, 0:1], scalar2=None,
                    op0=ALU.mult), reads=[psm_k, E(kq)[1]], writes=[qbk])
            else:
                op(lst, lambda: le.tensor_scalar(
                    out=qmul, in0=psm, scalar1=E(kq)[0][:, 0:1], scalar2=None,
                    op0=ALU.mult), reads=[psm_k, E(kq)[1]], writes=[qmul_k])

        def q_scales(src_ap, src_keys, nb, params, qmb=None):
            mfrac, emax, mn = params
            q_amax(src_ap, src_keys, nb, 0)
            q_ladder(nb, params, qmb=qmb)

        def q_elem_steps(src_ap, src_keys, dst_ap, dst_key, ncol, params, scr,
                         nb_off=0):
            steps = []
            add = steps.append
            mfrac, emax, mn = params
            nb = ncol // 32
            zz, zz_k = scr["zz"]
            yy, yy_k = scr["yy"]
            c1, c1_k = scr["c1"]
            c2, c2_k = scr["c2"]
            zmul, zmul_k = eN("zmul", nb, nb_off)
            qmb = scr.get("qmb")
            if qmb is not None:
                qmul = qmb[0][:, nb_off:nb_off + nb]
                qmul_k = (qmb[1], nb_off, nb_off + nb)
            else:
                qmul, qmul_k = eN("qmul", nb, nb_off)
            th1 = float(2.0 ** (mfrac + 1))
            th2 = float(2.0 ** (mfrac + 2))
            mnz = float(mn * 2.0 ** mfrac)
            add(lambda: op("v", lambda: nc.vector.tensor_tensor(
                out=zz.rearrange("p (b k) -> p b k", k=32),
                in0=src_ap.rearrange("p (b k) -> p b k", k=32),
                in1=zmul.unsqueeze(2).broadcast_to([128, nb, 32]), op=ALU.mult),
                reads=list(src_keys) + [zmul_k], writes=[zz_k]))
            add(lambda: op("v", lambda: nc.vector.tensor_tensor(
                out=yy, in0=zz, in1=zz, op=ALU.mult),
                reads=[zz_k], writes=[yy_k]))
            add(lambda: op("v", lambda: nc.vector.tensor_scalar(
                out=c1, in0=yy, scalar1=th1 * th1, scalar2=None,
                op0=ALU.is_ge), reads=[yy_k], writes=[c1_k]))
            add(lambda: op("v", lambda: nc.vector.tensor_scalar(
                out=c2, in0=yy, scalar1=th2 * th2, scalar2=None,
                op0=ALU.is_ge), reads=[yy_k], writes=[c2_k]))
            add(lambda: op("v", lambda: nc.vector.tensor_scalar(
                out=c2, in0=c2, scalar1=cmag[:, 0:1], scalar2=cmag[:, 0:1],
                op0=ALU.mult, op1=ALU.add),
                reads=[c2_k, cmag_k], writes=[c2_k]))
            add(lambda: op("v", lambda: nc.vector.scalar_tensor_tensor(
                out=c1, in0=c1, scalar=1.0, in1=c2, op0=ALU.add, op1=ALU.mult),
                reads=[c1_k, c2_k], writes=[c1_k]))
            add(lambda: op("v", lambda: nc.vector.tensor_tensor(
                out=yy, in0=zz, in1=c1, op=ALU.add),
                reads=[zz_k, c1_k], writes=[yy_k]))
            add(lambda: op("v", lambda: nc.vector.tensor_tensor(
                out=c1, in0=yy, in1=c1, op=ALU.subtract),
                reads=[yy_k, c1_k], writes=[c1_k]))
            add(lambda: op("v", lambda: nc.vector.tensor_scalar(
                out=c1, in0=c1, scalar1=mnz, scalar2=-mnz, op0=ALU.min,
                op1=ALU.max), reads=[c1_k], writes=[c1_k]))
            add(lambda: op("v", lambda: nc.vector.tensor_tensor(
                out=dst_ap.rearrange("p (b k) -> p b k", k=32),
                in0=c1.rearrange("p (b k) -> p b k", k=32),
                in1=qmul.unsqueeze(2).broadcast_to([128, nb, 32]), op=ALU.mult),
                reads=[c1_k, qmul_k], writes=[dst_key]))
            return steps

        def zipper(chains):
            i = 0
            while any(chains):
                for ch in chains:
                    if i < len(ch):
                        ch[i]()
                i += 1
                if all(i >= len(ch) for ch in chains):
                    break

        def q_elem(src_ap, src_keys, dst_ap, dst_key, ncol, params, scr, nb_off=0):
            for st_ in q_elem_steps(src_ap, src_keys, dst_ap, dst_key, ncol,
                                    params, scr, nb_off):
                st_()

        def transpose8(src_ap_fn, src_keys, dst_ap, dst_key, n, ev="v"):
            for j in range(n):
                op("t", lambda j=j: nc.tensor.transpose(
                    ps_tr[:, j * 128:(j + 1) * 128], src_ap_fn(j), identb),
                    reads=list(src_keys) + [identb_k],
                    writes=[("ps_tr", j * 128, (j + 1) * 128)])
            if ev == "v":
                op("v", lambda: nc.vector.tensor_copy(
                    out=dst_ap, in_=ps_tr[:, 0:n * 128]),
                    reads=[("ps_tr", 0, n * 128)], writes=[dst_key])
            else:
                op("s", lambda: nc.scalar.copy(
                    out=dst_ap, in_=ps_tr[:, 0:n * 128]),
                    reads=[("ps_tr", 0, n * 128)], writes=[dst_key])

        em.barrier()
        xrow, xrow_k = Dx(2688, D)
        xrow_bf, xrow_bf_k = Hx(256, D)
        for tt in range(NT):
            op("dG", lambda tt=tt: nc.gpsimd.dma_start(
                out=xrow, in_=x_h[tt * 128:(tt + 1) * 128, :]), writes=[xrow_k])
            op("v", lambda: nc.vector.tensor_copy(out=xrow_bf, in_=xrow),
               reads=[xrow_k], writes=[xrow_bf_k])
            for g in range(2):
                dst = Aa[:].rearrange("p (dt c) -> p dt c", c=T)[
                    :, g * 8:(g + 1) * 8, tt * 128:(tt + 1) * 128]
                transpose8(
                    lambda j, g=g: xrow_bf[:, (g * 8 + j) * 128:(g * 8 + j + 1) * 128],
                    [xrow_bf_k], dst, ("A", g * 8 * T, (g * 8 + 8) * T), 8)

        NOUT = HQ + 2 * HKV
        wrawA, wrawA_k = Dx(640, D)
        wrawB, wrawB_k = Dx(2688, D)
        qraw, qraw_k = Dx(4736, D)
        wbfA, wbfA_k = Fx(0, D)
        wbfB, wbfB_k = Fx(2048, D)
        t1b, t1b_k = Fx(4096, D)
        vtmp, vtmp_k = Hx(256, 512)
        for ot in range(NOUT):
            wr, wr_k, wb, wb_k = ((wrawA, wrawA_k, wbfA, wbfA_k) if ot % 2 == 0
                                  else (wrawB, wrawB_k, wbfB, wbfB_k))
            src = wqkvT_h[:, ot * 128:(ot + 1) * 128].rearrange(
                "(a p) o -> p a o", p=128)
            op("dS", lambda src=src, wr=wr: nc.sync.dma_start(
                out=wr.rearrange("p (a o) -> p a o", o=128), in_=src),
                writes=[wr_k])
            op("v", lambda wb=wb, wr=wr: nc.vector.tensor_copy(out=wb, in_=wr),
               reads=[wr_k], writes=[wb_k])
            for ch in range(4):
                for dt in range(ND):
                    op("t", lambda ch=ch, dt=dt, wb=wb: nc.tensor.matmul(
                        ps_big[:, ch * 512:(ch + 1) * 512],
                        wb[:, dt * 128:(dt + 1) * 128],
                        Aa[:, dt * T + ch * 512:dt * T + (ch + 1) * 512],
                        start=(dt == 0), stop=(dt == ND - 1)),
                        reads=[wb_k, ("A", dt * T + ch * 512, dt * T + (ch + 1) * 512)],
                        writes=[("ps_big", ch * 512, (ch + 1) * 512)], acc=True)
            if ot < HQ + HKV:
                for ch in range(4):
                    qck = ("D", 4736 + ch * 512, 4736 + (ch + 1) * 512)
                    op("s", lambda ch=ch, ot=ot: nc.scalar.activation(
                        out=qraw[:, ch * 512:(ch + 1) * 512],
                        in_=ps_big[:, ch * 512:(ch + 1) * 512],
                        func=ACT.Identity, bias=E("bq")[0][:, ot:ot + 1], scale=1.0),
                        reads=[("ps_big", ch * 512, (ch + 1) * 512), E("bq")[1]],
                        writes=[qck])
                for ch in range(4):
                    qck = ("D", 4736 + ch * 512, 4736 + (ch + 1) * 512)
                    op("t", lambda ch=ch: nc.tensor.matmul(
                        ps_c[:, 0:512], swf, qraw[:, ch * 512:(ch + 1) * 512],
                        start=True, stop=True),
                        reads=[swf_k, qck], writes=[("ps_c", 0, 512)])
                    op("v", lambda ch=ch: nc.vector.tensor_tensor(
                        out=t1b[:, ch * 512:(ch + 1) * 512],
                        in0=qraw[:, ch * 512:(ch + 1) * 512],
                        in1=C2[:, ch * 512:(ch + 1) * 512], op=ALU.mult),
                        reads=[qck, C2_k],
                        writes=[("F", 4096 + ch * 512, 4096 + (ch + 1) * 512)])
                    op("v", lambda ch=ch: nc.vector.tensor_tensor(
                        out=qraw[:, ch * 512:(ch + 1) * 512], in0=ps_c[:, 0:512],
                        in1=S2[:, ch * 512:(ch + 1) * 512], op=ALU.mult),
                        reads=[("ps_c", 0, 512), S2_k], writes=[qck])
                    if ot < HQ:
                        dst = Bb[:, ot * T + ch * 512:ot * T + (ch + 1) * 512]
                        dk = ("B", ot * T + ch * 512, ot * T + (ch + 1) * 512)
                    else:
                        kvo = (ot - HQ) * T
                        dst = Gg[:, kvo + ch * 512:kvo + (ch + 1) * 512]
                        dk = ("G", kvo + ch * 512, kvo + (ch + 1) * 512)
                    op("v", lambda dst=dst, ch=ch: nc.vector.tensor_tensor(
                        out=dst, in0=t1b[:, ch * 512:(ch + 1) * 512],
                        in1=qraw[:, ch * 512:(ch + 1) * 512], op=ALU.add),
                        reads=[("F", 4096 + ch * 512, 4096 + (ch + 1) * 512), qck],
                        writes=[dk])
            else:
                kv = ot - HQ - HKV
                for ch in range(4):
                    op("s", lambda ch=ch, ot=ot: nc.scalar.activation(
                        out=vtmp, in_=ps_big[:, ch * 512:(ch + 1) * 512],
                        func=ACT.Identity, bias=E("bq")[0][:, ot:ot + 1], scale=1.0),
                        reads=[("ps_big", ch * 512, (ch + 1) * 512), E("bq")[1]],
                        writes=[vtmp_k])
                    base = 4096 + (kv * NT + ch * 4) * 128
                    transpose8(
                        lambda j: vtmp[:, j * 128:(j + 1) * 128],
                        [vtmp_k], Gg[:, base:base + 512],
                        ("G", base, base + 512), 4)

        rawI = [Dx(2688, 2048), Dx(4736, 2048)]
        qmbA = Hx(1920, 256)
        scrI2 = [{
            "yy": Dx(640, 1024), "qmb": qmbA,
            "c1": Fx(4096, 1024), "c2": Fx(5120, 1024),
        }, {
            "yy": Dx(1664, 1024), "qmb": qmbA,
            "c1": Fx(6144, 1024), "c2": Fx(7168, 1024),
        }]
        wqnI, wqnI_k = Hx(2304, 2048)
        stageI, stageI_k = Hx(896, 1024)
        wq_parity = [0]

        def wq_pair_gen(kind, ic_or_et, pair):
            w_h = [wg_h, wu_h, wd_h][kind]
            eoff = 128 * wq_parity[0]
            wq_parity[0] ^= 1
            for k in range(2):
                wr, wr_k = rawI[k]
                if kind < 2:
                    ck = pair * 2 + k
                    r0 = ic_or_et * ICS + ck * 128
                    op("dS", lambda w_h=w_h, r0=r0, wr=wr: nc.sync.dma_start(
                        out=wr, in_=w_h[r0:r0 + 128, :]), writes=[wr_k])
                else:
                    cp = pair * 2 + k
                    op("dS", lambda w_h=w_h, et=ic_or_et, cp=cp, wr=wr:
                       nc.sync.dma_start(
                           out=wr, in_=w_h[et * 128:(et + 1) * 128,
                                           cp * 2048:(cp + 1) * 2048]),
                       writes=[wr_k])
                q_amax(rawI[k][0], [rawI[k][1]], 64, eoff + k * 64)
            yield
            q_ladder(128, QW, off=eoff, qmb=qmbA)
            yield
            for k in range(2):
                wr, wr_k = rawI[k]
                chains = []
                for s in range(2):
                    lo = s * 1024
                    raw_sl = (wr[:, lo:lo + 1024],
                              (wr_k[0], wr_k[1] + lo, wr_k[1] + lo + 1024))
                    scr = dict(scrI2[s])
                    scr["zz"] = raw_sl
                    chains.append(q_elem_steps(
                        raw_sl[0], [raw_sl[1]],
                        wqnI[:, lo:lo + 1024],
                        (wqnI_k[0], wqnI_k[1] + lo, wqnI_k[1] + lo + 1024),
                        1024, QW, scr,
                        nb_off=eoff + k * 64 + s * 32))
                zipper(chains)
                yield
                for g in range(2):
                    transpose8(
                        lambda j, g=g: wqnI[:, (g * 8 + j) * 128:(g * 8 + j + 1) * 128],
                        [(wqnI_k[0], wqnI_k[1] + g * 1024, wqnI_k[1] + (g + 1) * 1024)],
                        stageI, stageI_k, 8, ev="s")
                    if kind < 2:
                        ck = pair * 2 + k
                        s_wT = s_wgT if kind == 0 else s_wuT
                        dst = s_wT[ic_or_et, g * 8:(g + 1) * 8, :,
                                   ck * 128:(ck + 1) * 128].rearrange(
                                       "dt p c -> p dt c")
                        hk = (f"hw{kind}", ic_or_et, ic_or_et + 1)
                    else:
                        cp = pair * 2 + k
                        dst = s_wdT[ic_or_et, cp * 16 + g * 8:cp * 16 + (g + 1) * 8,
                                    :, :].rearrange("it p c -> p it c")
                        hk = ("hwd", ic_or_et, ic_or_et + 1)
                    op("dS", lambda dst=dst: nc.sync.dma_start(
                        out=dst, in_=stageI.rearrange("p (n c) -> p n c", c=128)),
                        reads=[stageI_k], writes=[hk])
                    yield

        wq_tasks = []
        for kind in range(3):
            for i in range(16):
                for pair in range(2):
                    wq_tasks.append((kind, i, pair))
        wq_tasks.sort(key=lambda t: (t[1], t[2], t[0]))

        def _wq_units():
            for kind, i, pair in wq_tasks:
                for _ in wq_pair_gen(kind, i, pair):
                    yield
        wq_gen = _wq_units()
        wq_done = [False]

        def emit_wq_task(n=1):
            for _ in range(n):
                try:
                    next(wq_gen)
                except StopIteration:
                    wq_done[0] = True
                    return

        em.barrier()
        Pbuf, Pbuf_k = Fx(0, T)
        Pd, Pd_k = Hx(768, 128)
        Pn, Pn_k = Fx(2048, T)
        for h in range(HQ):
            kv = h // (HQ // HKV)
            qkb = ("B", h * T + 0, h * T + T)
            for qt in range(NT):
                emit_wq_task(1)
                L = (qt + 1) * 128
                po = 1024 * ((h * NT + qt) % 2) if L <= 1024 else 0
                nch = (L + 511) // 512
                for c in range(nch):
                    c0, c1_ = c * 512, min((c + 1) * 512, L)
                    op("t", lambda h=h, qt=qt, kv=kv, c0=c0, c1_=c1_, po=po:
                       nc.tensor.matmul(
                           ps_big[:, po + c0:po + c1_],
                           Bb[:, h * T + qt * 128:h * T + (qt + 1) * 128],
                           Gg[:, kv * T + c0:kv * T + c1_], start=True, stop=True),
                       reads=[qkb, ("G", kv * T + c0, kv * T + c1_)],
                       writes=[("ps_big", po + c0, po + c1_)])
                    op("v", lambda c=c, c0=c0, c1_=c1_, po=po:
                       nc.vector.tensor_reduce(
                           out=eN("rmp", 1, c)[0], in_=ps_big[:, po + c0:po + c1_],
                           axis=AXL.X, op=ALU.max),
                       reads=[("ps_big", po + c0, po + c1_)],
                       writes=[eN("rmp", 1, c)[1]])
                op("v", lambda nch=nch: nc.vector.tensor_reduce(
                    out=E("rowmax")[0], in_=eN("rmp", nch)[0], axis=AXL.X,
                    op=ALU.max),
                    reads=[eN("rmp", nch)[1]], writes=[E("rowmax")[1]])
                op("v", lambda: nc.vector.tensor_scalar(
                    out=E("bias_sm")[0], in0=E("rowmax")[0],
                    scalar1=E("kan")[0][:, 0:1], scalar2=None, op0=ALU.mult),
                    reads=[E("rowmax")[1], E("kan")[1]], writes=[E("bias_sm")[1]])
                if qt > 0:
                    op("s", lambda qt=qt, po=po: nc.scalar.activation(
                        out=Pbuf[:, 0:qt * 128], in_=ps_big[:, po:po + qt * 128],
                        func=ACT.Exp, bias=E("bias_sm")[0][:, 0:1], scale=ALPHA,
                        accum_out=E("sum_a")[0]),
                        reads=[("ps_big", po, po + qt * 128), E("bias_sm")[1]],
                        writes=[("F", 0, qt * 128), E("sum_a")[1]])
                op("s", lambda qt=qt, L=L, po=po: nc.scalar.activation(
                    out=Pd, in_=ps_big[:, po + qt * 128:po + L], func=ACT.Exp,
                    bias=E("bias_sm")[0][:, 0:1], scale=ALPHA),
                    reads=[("ps_big", po + qt * 128, po + L), E("bias_sm")[1]],
                    writes=[Pd_k])
                emit_wq_task(2)
                op("v", lambda qt=qt, L=L: nc.vector.tensor_tensor(
                    out=Pbuf[:, qt * 128:L], in0=Pd, in1=maskb, op=ALU.mult),
                    reads=[Pd_k, maskb_k], writes=[("F", qt * 128, L)])
                op("v", lambda qt=qt, L=L: nc.vector.tensor_reduce(
                    out=E("sum_d")[0], in_=Pbuf[:, qt * 128:L], axis=AXL.X,
                    op=ALU.add),
                    reads=[("F", qt * 128, L)], writes=[E("sum_d")[1]])
                if qt > 0:
                    op("v", lambda: nc.vector.tensor_tensor(
                        out=E("sum_t")[0], in0=E("sum_a")[0], in1=E("sum_d")[0],
                        op=ALU.add),
                        reads=[E("sum_a")[1], E("sum_d")[1]], writes=[E("sum_t")[1]])
                else:
                    op("v", lambda: nc.vector.tensor_copy(
                        out=E("sum_t")[0], in_=E("sum_d")[0]),
                        reads=[E("sum_d")[1]], writes=[E("sum_t")[1]])
                op("v", lambda: nc.vector.reciprocal(
                    out=E("inv_s")[0], in_=E("sum_t")[0]),
                    reads=[E("sum_t")[1]], writes=[E("inv_s")[1]])
                op("v", lambda L=L: nc.vector.tensor_scalar(
                    out=Pn[:, 0:L], in0=Pbuf[:, 0:L],
                    scalar1=E("inv_s")[0][:, 0:1], scalar2=None, op0=ALU.mult),
                    reads=[("F", 0, L), E("inv_s")[1]],
                    writes=[("F", 2048, 2048 + L)])
                nkt = qt + 1
                for g in range((nkt + 7) // 8):
                    jn = min(8, nkt - g * 8)
                    transpose8(
                        lambda j, g=g: Pn[:, (g * 8 + j) * 128:(g * 8 + j + 1) * 128],
                        [("F", 2048 + g * 1024, 2048 + g * 1024 + jn * 128)],
                        Pbuf[:, g * 1024:g * 1024 + jn * 128],
                        ("F", g * 1024, g * 1024 + jn * 128), jn, ev="s")
                for kt in range(nkt):
                    vb = 4096 + (kv * NT + kt) * 128
                    op("t", lambda vb=vb, kt=kt, qt=qt: nc.tensor.matmul(
                        ps_a[:, 0:128], Gg[:, vb:vb + 128],
                        Pbuf[:, kt * 128:(kt + 1) * 128],
                        start=(kt == 0), stop=(kt == qt)),
                        reads=[("G", vb, vb + 128),
                               ("F", kt * 128, (kt + 1) * 128)],
                        writes=[("ps_a", 0, 128)], acc=True)
                op("s", lambda h=h, qt=qt: nc.scalar.copy(
                    out=Aa[:, h * T + qt * 128:h * T + (qt + 1) * 128],
                    in_=ps_a[:, 0:128]),
                    reads=[("ps_a", 0, 128)],
                    writes=[("A", h * T + qt * 128, h * T + (qt + 1) * 128)])

        em.barrier()
        wtmp, wtmp_k = Dx(0, 512)
        x1row, x1row_k = Hx(256 + D, D)
        x1q, x1q_k = Gx(0, D)
        xqs, xqs_k = Gx(D, 2048)
        rrow, rrow_k = Dx(640, D)
        srow, srow_k = Dx(2688, D)
        scrP5 = {
            "zz": Dx(4736, 2048), "yy": Dx(2688, 2048), "qmb": qmbA,
            "c1": Fx(0, 2048), "c2": Fx(2048, 2048),
        }

        def ln_stats(rr_ap, rr_key):
            op("v", lambda: nc.vector.tensor_reduce(
                out=E("mean")[0], in_=rr_ap, axis=AXL.X, op=ALU.add),
                reads=[rr_key], writes=[E("mean")[1]])
            op("v", lambda: nc.vector.tensor_scalar(
                out=E("msc")[0], in0=E("mean")[0], scalar1=1.0 / D, scalar2=None,
                op0=ALU.mult), reads=[E("mean")[1]], writes=[E("msc")[1]])
            op("s", lambda: nc.scalar.activation(
                out=srow, in_=rr_ap, func=ACT.Square, bias=0.0, scale=1.0,
                accum_out=E("var_s")[0]),
                reads=[rr_key], writes=[srow_k, E("var_s")[1]])
            op("v", lambda: nc.vector.tensor_tensor(
                out=E("m2sq")[0], in0=E("msc")[0], in1=E("msc")[0], op=ALU.mult),
                reads=[E("msc")[1]], writes=[E("m2sq")[1]])
            op("v", lambda: nc.vector.tensor_scalar(
                out=E("var1")[0], in0=E("var_s")[0], scalar1=1.0 / D,
                scalar2=None, op0=ALU.mult),
                reads=[E("var_s")[1]], writes=[E("var1")[1]])
            op("v", lambda: nc.vector.tensor_tensor(
                out=E("var2")[0], in0=E("var1")[0], in1=E("m2sq")[0],
                op=ALU.subtract),
                reads=[E("var1")[1], E("m2sq")[1]], writes=[E("var2")[1]])
            op("v", lambda: nc.vector.tensor_scalar(
                out=E("var3")[0], in0=E("var2")[0],
                scalar1=E("keps_ln")[0][:, 0:1], scalar2=None, op0=ALU.add),
                reads=[E("var2")[1], E("keps_ln")[1]], writes=[E("var3")[1]])
            op("s", lambda: nc.scalar.activation(
                out=E("sqv")[0], in_=E("var3")[0], func=ACT.Sqrt, bias=0.0,
                scale=1.0), reads=[E("var3")[1]], writes=[E("sqv")[1]])
            op("v", lambda: nc.vector.reciprocal(
                out=E("rstd")[0], in_=E("sqv")[0]),
                reads=[E("sqv")[1]], writes=[E("rstd")[1]])

        for eq in range(4):
            for ot in range(16):
                op("dS", lambda ot=ot, eq=eq: nc.sync.dma_start(
                    out=wtmp,
                    in_=woT_h[ot * 128:(ot + 1) * 128, eq * 512:(eq + 1) * 512]),
                    writes=[wtmp_k])
                wqlo = eq * 8192
                op("v", lambda ot=ot, wqlo=wqlo: nc.vector.tensor_copy(
                    out=Bb[:, wqlo + ot * 512:wqlo + (ot + 1) * 512], in_=wtmp),
                    reads=[wtmp_k],
                    writes=[("B", wqlo + ot * 512, wqlo + (ot + 1) * 512)])
        for qt in range(NT):
            emit_wq_task(2)
            op("dG", lambda qt=qt: nc.gpsimd.dma_start(
                out=srow, in_=x_h[qt * 128:(qt + 1) * 128, :]), writes=[srow_k])
            for eq in range(4):
                wqlo = eq * 8192
                pcol = eq * 512
                for ot in range(16):
                    op("t", lambda qt=qt, ot=ot, wqlo=wqlo, pcol=pcol:
                       nc.tensor.matmul(
                           ps_big[:, pcol:pcol + 512],
                           Aa[:, ot * T + qt * 128:ot * T + (qt + 1) * 128],
                           Bb[:, wqlo + ot * 512:wqlo + (ot + 1) * 512],
                           start=(ot == 0), stop=(ot == 15)),
                       reads=[("A", ot * T + qt * 128, ot * T + (qt + 1) * 128),
                              ("B", wqlo + ot * 512, wqlo + (ot + 1) * 512)],
                       writes=[("ps_big", pcol, pcol + 512)], acc=True)
            op("v", lambda: nc.vector.tensor_tensor(
                out=rrow, in0=srow, in1=ps_big[:, 0:2048], op=ALU.add),
                reads=[srow_k, ("ps_big", 0, 2048)], writes=[rrow_k])
            ln_stats(rrow, rrow_k)
            op("v", lambda: nc.vector.tensor_scalar(
                out=x1row, in0=rrow, scalar1=E("msc")[0][:, 0:1],
                scalar2=E("rstd")[0][:, 0:1], op0=ALU.subtract, op1=ALU.mult),
                reads=[rrow_k, E("msc")[1], E("rstd")[1]], writes=[x1row_k])
            op("dG", lambda qt=qt: nc.gpsimd.dma_start(
                out=s_x1[qt * 128:(qt + 1) * 128, :], in_=x1row),
                reads=[x1row_k], writes=[("hx1", qt, qt + 1)])
            q_scales(x1row, [x1row_k], 64, QA, qmb=qmbA)
            q_elem(x1row, [x1row_k], x1q, x1q_k, D, QA, scrP5)
            for g in range(2):
                transpose8(
                    lambda j, g=g: x1q[:, (g * 8 + j) * 128:(g * 8 + j + 1) * 128],
                    [x1q_k], xqs[:, g * 1024:(g + 1) * 1024],
                    ("G", D + g * 1024, D + (g + 1) * 1024), 8)
            dst = s_x1qT.rearrange("(dt p) t -> p dt t", p=128)[
                :, :, qt * 128:(qt + 1) * 128]
            op("dG", lambda dst=dst: nc.gpsimd.dma_start(
                out=dst, in_=xqs.rearrange("p (dt c) -> p dt c", c=128)),
                reads=[xqs_k], writes=[("hx1qT", qt, qt + 1)])

        while not wq_done[0]:
            emit_wq_task()

        em.barrier()
        xqT, xqT_k = Gx(0, ND * HALF)
        otile, otile_k = Hx(1792, 512)
        otile_b, otile_b_k = Hx(3328, 512)
        sgb, sgb_k = Hx(256, 512)
        ubb, ubb_k = Hx(768, 512)
        intq, intq_k = Hx(1280, 512)
        qmbC = Hx(3840, 256)
        scrP6b2 = [{
            "zz": Dx(640, 512), "yy": Dx(1152, 512), "qmb": qmbC,
            "c1": Dx(1664, 512), "c2": Dx(2176, 512),
        }, {
            "zz": Dx(2688, 512), "yy": Dx(3200, 512), "qmb": qmbC,
            "c1": Dx(3712, 512), "c2": Dx(4224, 512),
        }]

        def wslot(k):
            return Bb[:, k * 8192:(k + 1) * 8192], ("B", k * 8192, (k + 1) * 8192)

        def wdslot(k):
            lo = 24576 + k * 4096
            return Bb[:, lo:lo + 4096], ("B", lo, lo + 4096)

        qd = [Hx(2304, 512), Hx(2816, 512)]

        def p6b_products(ic):
            ici = ic % 4
            sgs, sgs_k = wslot((2 * ic) % 3)
            sus, sus_k = wslot((2 * ic + 1) % 3)
            op("dS", lambda ic=ic, sgs=sgs: nc.sync.dma_start(
                out=sgs.rearrange("p (dt c) -> p dt c", c=ICS),
                in_=s_wgT[ic].rearrange("dt p c -> p dt c")),
                reads=[("hw0", ic, ic + 1)], writes=[sgs_k])
            op("dS", lambda ic=ic, sus=sus: nc.sync.dma_start(
                out=sus.rearrange("p (dt c) -> p dt c", c=ICS),
                in_=s_wuT[ic].rearrange("dt p c -> p dt c")),
                reads=[("hw1", ic, ic + 1)], writes=[sus_k])
            for tokt in range(4):
                gps, gps_k = (ps_a[:], ("ps_a", 0, 512)) if tokt % 2 == 0 \
                    else (ps_c[:], ("ps_c", 0, 512))
                ups, ups_k = (ps_b[:], ("ps_b", 0, 512)) if tokt % 2 == 0 \
                    else (ps_big[:, 0:512], ("ps_big", 0, 512))
                sgb2, sgb2_k = (sgb, sgb_k) if tokt % 2 == 0 else (intq, intq_k)
                for dt in range(ND):
                    op("t", lambda dt=dt, tokt=tokt, sgs=sgs, gps=gps:
                       nc.tensor.matmul(
                           gps,
                           xqT[:, dt * HALF + tokt * 128:
                               dt * HALF + (tokt + 1) * 128],
                           sgs[:, dt * ICS:(dt + 1) * ICS],
                           start=(dt == 0), stop=(dt == ND - 1)),
                       reads=[xqT_k, sgs_k],
                       writes=[gps_k], acc=True)
                for dt in range(ND):
                    op("t", lambda dt=dt, tokt=tokt, sus=sus, ups=ups:
                       nc.tensor.matmul(
                           ups,
                           xqT[:, dt * HALF + tokt * 128:
                               dt * HALF + (tokt + 1) * 128],
                           sus[:, dt * ICS:(dt + 1) * ICS],
                           start=(dt == 0), stop=(dt == ND - 1)),
                       reads=[xqT_k, sus_k],
                       writes=[ups_k], acc=True)
                op("s", lambda gps=gps, sgb2=sgb2: nc.scalar.activation(
                    out=sgb2, in_=gps, func=ACT.Silu, bias=0.0,
                    scale=1.0), reads=[gps_k], writes=[sgb2_k])
                op("s", lambda ups=ups: nc.scalar.copy(out=ubb, in_=ups),
                   reads=[ups_k], writes=[ubb_k])
                dcol = (ici * 4 + tokt) * 512
                op("v", lambda dcol=dcol, sgb2=sgb2: nc.vector.tensor_tensor(
                    out=Ff[:, dcol:dcol + 512], in0=sgb2, in1=ubb,
                    op=ALU.mult),
                    reads=[sgb2_k, ubb_k], writes=[("F", dcol, dcol + 512)])

        for hf in range(NHALF):
            op("dG", lambda hf=hf: nc.gpsimd.dma_start(
                out=xqT.rearrange("p (dt c) -> p dt c", c=HALF),
                in_=s_x1qT.rearrange("(dt p) t -> p dt t", p=128)[
                    :, :, hf * HALF:(hf + 1) * HALF]),
                reads=[("hx1qT", 0, NT)], writes=[xqT_k])
            for ici in range(4):
                p6b_products(ici)
            for icq in range(NIC // 4):
                q_scales(Ff[:, 0:8192], [("F", 0, 8192)], 256, QA, qmb=qmbC)
                pend = []
                for sub in range(16):
                    ici, tokt = sub // 4, sub % 4
                    ic2 = icq * 4 + ici
                    dcol = sub * 512
                    s = sub % 2
                    ch = q_elem_steps(
                        Ff[:, dcol:dcol + 512], [("F", dcol, dcol + 512)],
                        qd[s][0], qd[s][1], 512, QA, scrP6b2[s],
                        nb_off=dcol // 32)
                    pend.append((ch, ic2, tokt, s))
                    if len(pend) == 2 or sub == 15:
                        zipper([p[0] for p in pend])
                        if icq < 3 and sub % 4 == 3:
                            p6b_products((icq + 1) * 4 + sub // 4)
                        for (_, ic2b, tokt2, s2) in pend:
                            for j in range(4):
                                op("t", lambda j=j, s2=s2: nc.tensor.transpose(
                                    ps_tr[:, j * 128:(j + 1) * 128],
                                    qd[s2][0][:, j * 128:(j + 1) * 128], identb),
                                    reads=[qd[s2][1], identb_k],
                                    writes=[("ps_tr", j * 128, (j + 1) * 128)])
                            dsts = Aa[:].rearrange("p (it c) -> p it c", c=512)[
                                :, ic2b * 4:ic2b * 4 + 4,
                                tokt2 * 128:(tokt2 + 1) * 128]
                            op("s", lambda dsts=dsts: nc.scalar.copy(
                                out=dsts,
                                in_=ps_tr[:, 0:512].rearrange("p (n c) -> p n c", c=128)),
                                reads=[("ps_tr", 0, 512)],
                                writes=[("A", ic2b * 4 * 512, (ic2b * 4 + 4) * 512)])
                        pend = []
            def wd_dma(idx):
                et2, hlf = idx // 2, idx % 2
                wds, wds_k = wdslot(idx % 2)
                st = "dS" if (idx // 2) % 2 == 0 else "dG"
                eng = nc.sync if st == "dS" else nc.gpsimd
                op(st, lambda et2=et2, hlf=hlf, wds=wds, eng=eng: eng.dma_start(
                    out=wds.rearrange("p (it c) -> p it c", c=128),
                    in_=s_wdT[et2, hlf * 32:(hlf + 1) * 32].rearrange(
                        "it p c -> p it c")),
                    reads=[("hwd", et2, et2 + 1)], writes=[wds_k])

            wd_dma(0)
            wd_dma(1)
            for et in range(ND):
                pcol = (et % 4) * 512
                for hlf in range(2):
                    idx = et * 2 + hlf
                    wds, wds_k = wdslot(idx % 2)
                    for it2 in range(32):
                        it = hlf * 32 + it2
                        op("t", lambda it=it, it2=it2, wds=wds, pcol=pcol:
                           nc.tensor.matmul(
                               ps_big[:, pcol:pcol + 512],
                               wds[:, it2 * 128:(it2 + 1) * 128],
                               Aa[:, it * 512:(it + 1) * 512],
                               start=(it == 0), stop=(it == NFI - 1)),
                           reads=[wds_k, ("A", it * 512, (it + 1) * 512)],
                           writes=[("ps_big", pcol, pcol + 512)], acc=True)
                    if idx + 2 < ND * 2:
                        wd_dma(idx + 2)
                ot2, ot2_k = (otile, otile_k) if et % 2 == 0 else (otile_b, otile_b_k)
                op("s", lambda pcol=pcol, ot2=ot2: nc.scalar.copy(
                    out=ot2, in_=ps_big[:, pcol:pcol + 512]),
                    reads=[("ps_big", pcol, pcol + 512)], writes=[ot2_k])
                for tokt in range(4):
                    op("dG", lambda hf=hf, tokt=tokt, et=et, ot2=ot2:
                       nc.gpsimd.dma_start(
                           out=s_x2[hf, tokt, et],
                           in_=ot2[:, tokt * 128:(tokt + 1) * 128]),
                       reads=[ot2_k],
                       writes=[("hx2", (hf * 4 + tokt) * ND + et,
                                (hf * 4 + tokt) * ND + et + 1)])

        em.barrier()
        x2t, x2t_k = Gx(0, 2048)
        x2row, x2row_k = Hx(256, 2048)
        x1row7, x1row7_k = Hx(256 + 2048, 2048)
        outrow, outrow_k = Dx(4736, D)
        for qt in range(NT):
            hf, tokt = qt // 4, qt % 4
            op("dG", lambda hf=hf, tokt=tokt: nc.gpsimd.dma_start(
                out=x2t.rearrange("p (et c) -> p et c", c=128),
                in_=s_x2[hf, tokt].rearrange("et p c -> p et c")),
                reads=[("hx2", (hf * 4 + tokt) * ND, (hf * 4 + tokt + 1) * ND)],
                writes=[x2t_k])
            for g in range(2):
                transpose8(
                    lambda j, g=g: x2t[:, (g * 8 + j) * 128:(g * 8 + j + 1) * 128],
                    [x2t_k], x2row[:, g * 1024:(g + 1) * 1024],
                    ("H", 256 + g * 1024, 256 + (g + 1) * 1024), 8)
            op("dG", lambda qt=qt: nc.gpsimd.dma_start(
                out=x1row7, in_=s_x1[qt * 128:(qt + 1) * 128, :]),
                reads=[("hx1", qt, qt + 1)], writes=[x1row7_k])
            op("v", lambda: nc.vector.tensor_tensor(
                out=rrow, in0=x1row7, in1=x2row, op=ALU.add),
                reads=[x1row7_k, ("H", 256, 256 + 2048)], writes=[rrow_k])
            ln_stats(rrow, rrow_k)
            op("v", lambda: nc.vector.tensor_scalar(
                out=outrow, in0=rrow, scalar1=E("msc")[0][:, 0:1],
                scalar2=E("rstd")[0][:, 0:1], op0=ALU.subtract, op1=ALU.mult),
                reads=[rrow_k, E("msc")[1], E("rstd")[1]], writes=[outrow_k])
            op("dG", lambda qt=qt: nc.gpsimd.dma_start(
                out=out_h[qt * 128:(qt + 1) * 128, :], in_=outrow),
                reads=[outrow_k], writes=[("hout", qt, qt + 1)])

        em.finish("dS")

    return nc


_CACHE = {}


def kernel(L0_attn_x, L0_attn_wq, L0_attn_wk, L0_attn_wv, L0_attn_wo,
           L0_attn_bq, L0_attn_bk, L0_attn_bv,
           L0_ffn_w_gate, L0_ffn_w_up, L0_ffn_w_down):
    x = np.ascontiguousarray(np.asarray(L0_attn_x, dtype=np.float32))
    wq = np.asarray(L0_attn_wq, np.float32)
    wk = np.asarray(L0_attn_wk, np.float32)
    wv = np.asarray(L0_attn_wv, np.float32)
    wo = np.asarray(L0_attn_wo, np.float32)
    wqkvT = np.ascontiguousarray(np.concatenate([wq, wk, wv], axis=0).T)
    woT = np.ascontiguousarray(wo.T)
    bqkv = np.ascontiguousarray(np.concatenate(
        [np.asarray(L0_attn_bq, np.float32), np.asarray(L0_attn_bk, np.float32),
         np.asarray(L0_attn_bv, np.float32)]))
    wg = np.ascontiguousarray(np.asarray(L0_ffn_w_gate, np.float32))
    wu = np.ascontiguousarray(np.asarray(L0_ffn_w_up, np.float32))
    wd = np.ascontiguousarray(np.asarray(L0_ffn_w_down, np.float32))
    ident, swap, freqmod, pos, sgn, mask01 = _host_consts()

    if "nc" not in _CACHE:
        _CACHE["nc"] = build_nc()
    nc = _CACHE["nc"]

    shared = {
        "wqkvT": wqkvT, "woT": woT, "bqkv": bqkv,
        "wg": wg, "wu": wu, "wd": wd,
        "c_ident": ident, "c_swap": swap, "c_freqmod": freqmod,
        "c_pos": pos, "c_sgn": sgn, "c_mask01": mask01,
    }
    in_maps = [dict(shared, x=np.ascontiguousarray(x[c])) for c in range(B)]
    res = run_bass_kernel_spmd(nc, in_maps, core_ids=list(range(B)))
    out = np.stack([np.asarray(res.results[c]["out"], dtype=np.float32)
                    for c in range(B)], axis=0)
    return out
